# version 29
# baseline (speedup 1.0000x reference)
import numpy as np
import ml_dtypes
import concourse.bass as bass
import concourse.mybir as mybir
from concourse.bass_utils import run_bass_kernel_spmd
from concourse.bass_core_rust import IndirectOffsetOnAxis
from contextlib import ExitStack

F32 = mybir.dt.float32
BF16 = mybir.dt.bfloat16
U32 = mybir.dt.uint32
AF = mybir.ActivationFunctionType
ALU = mybir.AluOpType
AX = mybir.AxisListType.X

D = 2048
TOK = 1024
NB = 8
KC = 16
NEXP = 8
CL = 384
NSLOT = NEXP * CL
NSB = CL // 128
IDXSPACE = 128 * 32
JSH = 44
JEX = 11
NGRP = 4
GSZ = JSH // NGRP
EPS = 1e-5
BIGF = 1.0e9

ENGINES = ("tensor", "vector", "scalar", "sync", "gpsimd")


class Buf:
    __slots__ = ("w", "r", "name")

    def __init__(self, name=""):
        self.name = name
        self.w = []
        self.r = []


class Prog:
    def __init__(self, nc, stack):
        self.nc = nc
        self.q = {e: [] for e in ENGINES}
        self.cnt = {}
        self.sems = {}
        self.waited = {e: {} for e in ENGINES}
        for s in ("LIN", "LIN2", "WMISC", "XN", "IDX", "W0", "W1", "WD", "WDV",
                  "OB0", "OB1", "CS0", "CS1", "CS2", "CS3", "WB0", "WB1",
                  "U0", "U1",
                  "IND", "G0", "G1", "OUT", "OUT2", "DLD", "SIO", "W", "PE",
                  "DVE", "ACT"):
            self.sems[s] = stack.enter_context(nc.semaphore(s))
            self.cnt[s] = 0

    def _collect(self, eng, toks):
        need = {}
        for t in toks:
            if t is None:
                continue
            s, v, peng = t
            if peng == "tensor" and eng == "tensor":
                continue
            if self.waited[eng].get(s, -1) >= v:
                continue
            if need.get(s, -1) < v:
                need[s] = v
        for s, v in need.items():
            self.waited[eng][s] = v
        return sorted(need.items())

    def _waits_for(self, eng, reads, writes):
        toks = []
        for b in reads:
            toks += b.w
        for b in writes:
            toks += b.w + b.r
        return self._collect(eng, toks)

    def op(self, eng, fn, reads=(), writes=(), add_writes=(), sem=None, inc=None):
        need = self._waits_for(eng, reads, writes)
        if sem is None:
            sem = {"tensor": "PE", "vector": "DVE", "scalar": "ACT",
                   "sync": "SIO", "gpsimd": "W"}[eng]
        if inc is None:
            inc = 16 if eng in ("sync", "gpsimd") else 1
        self.cnt[sem] += inc
        tok = (sem, self.cnt[sem], eng)
        sems = self.sems

        def closure(E, fn=fn, need=tuple(need), semh=sems[sem], n=inc):
            for s, v in need:
                E.wait_ge(sems[s], v)
            fn(E).then_inc(semh, n)

        self.q[eng].append(closure)
        for b in reads:
            b.r.append(tok)
        for b in writes:
            b.w = [tok]
            b.r = []
        for b in add_writes:
            b.w.append(tok)
        return tok

    def mm_group(self, mms, reads=(), psum=None, accum=False):
        toks = []
        for b in reads:
            toks += b.w
        if psum is not None:
            toks += psum.r
            if not accum:
                toks += psum.w
        need = self._collect("tensor", toks)
        sems = self.sems
        nmm = len(mms)
        self.cnt["PE"] += 1
        tok = ("PE", self.cnt["PE"], "tensor")
        for i, (out_fn, l_fn, r_fn, st, sp) in enumerate(mms):
            def closure(E, out_fn=out_fn, l_fn=l_fn, r_fn=r_fn, st=st, sp=sp,
                        need=tuple(need) if i == 0 else (),
                        last=(i == nmm - 1)):
                for s, v in need:
                    E.wait_ge(sems[s], v)
                inst = E.matmul(out_fn(), l_fn(), r_fn(), start=st, stop=sp)
                if last:
                    inst.then_inc(sems["PE"], 1)
            self.q["tensor"].append(closure)
        for b in reads:
            b.r.append(tok)
        if psum is not None:
            if accum:
                psum.w.append(tok)
            else:
                psum.w = [tok]
                psum.r = []
        return tok

    def tr_group(self, in_fn_list, out_fn_list, reads, psum, identity_ap):
        toks = []
        for b in reads:
            toks += b.w
        toks += psum.w + psum.r
        need = self._collect("tensor", toks)
        sems = self.sems
        self.cnt["PE"] += 1
        tok = ("PE", self.cnt["PE"], "tensor")
        n = len(in_fn_list)
        for i in range(n):
            def closure(E, in_fn=in_fn_list[i], out_fn=out_fn_list[i],
                        need=tuple(need) if i == 0 else (), last=(i == n - 1)):
                for s, v in need:
                    E.wait_ge(sems[s], v)
                inst = E.transpose(out=out_fn(), in_=in_fn(), identity=identity_ap)
                if last:
                    inst.then_inc(sems["PE"], 1)
            self.q["tensor"].append(closure)
        for b in reads:
            b.r.append(tok)
        psum.w = [tok]
        psum.r = []
        return tok


def fresh(buf):
    nb = Buf(buf.name)
    nb.w = list(buf.w) + list(buf.r)
    nb.r = []
    return nb


def build_nc(debug=False):
    nc = bass.Bass("TRN2", target_bir_lowering=False, debug=False)

    d_x = nc.dram_tensor("x_in", [TOK, D], F32, kind="ExternalInput").ap()
    d_wsh_up = nc.dram_tensor("wsh_up", [JSH, 128, 4096], F32, kind="ExternalInput").ap()
    d_wsh_dn = nc.dram_tensor("wsh_dn", [NGRP, 128, GSZ * D], F32, kind="ExternalInput").ap()
    d_wex_up = nc.dram_tensor("wex_up", [NEXP, JEX, 128, 4096], F32, kind="ExternalInput").ap()
    d_wex_dn = nc.dram_tensor("wex_dn", [NEXP, 128, JEX * D], F32, kind="ExternalInput").ap()
    d_bt = nc.dram_tensor("bt_w", [8, 128, D], F32, kind="ExternalInput").ap()
    d_ct = nc.dram_tensor("ct_w", [128, 8 * D], F32, kind="ExternalInput").ap()
    d_rt = nc.dram_tensor("rt_w", [128, 128], F32, kind="ExternalInput").ap()
    d_dvec = nc.dram_tensor("d_vec", [1, D], F32, kind="ExternalInput").ap()
    d_tri = nc.dram_tensor("tri_c", [128, 128], BF16, kind="ExternalInput").ap()
    d_idn = nc.dram_tensor("idn_c", [128, 128], BF16, kind="ExternalInput").ap()
    d_iota8 = nc.dram_tensor("iota8_c", [128, 8], F32, kind="ExternalInput").ap()
    d_tokid = nc.dram_tensor("tokid_c", [128, 8], U32, kind="ExternalInput").ap()
    d_ones128 = nc.dram_tensor("ones128_c", [128, 1], BF16, kind="ExternalInput").ap()
    d_ones1 = nc.dram_tensor("ones1_c", [1, 128], F32, kind="ExternalInput").ap()
    d_sent = nc.dram_tensor("sent_c", [128, 32], U32, kind="ExternalInput").ap()
    d_eps = nc.dram_tensor("eps_c", [128, 1], F32, kind="ExternalInput").ap()
    d_neg1 = nc.dram_tensor("neg1_c", [128, 1], F32, kind="ExternalInput").ap()
    d_y = nc.dram_tensor("y_out", [TOK, D], F32, kind="ExternalOutput").ap()
    if debug:
        dbg = {
            "xn1": nc.dram_tensor("dbg_xn1", [TOK, D], BF16, kind="ExternalOutput").ap(),
            "rt1": nc.dram_tensor("dbg_rt1", [TOK, 16], F32, kind="ExternalOutput").ap(),
            "wv1": nc.dram_tensor("dbg_wv1", [128, 16], F32, kind="ExternalOutput").ap(),
            "sl1": nc.dram_tensor("dbg_sl1", [128, 16], U32, kind="ExternalOutput").ap(),
            "ix1": nc.dram_tensor("dbg_ix1", [128, 32], U32, kind="ExternalOutput").ap(),
            "x1": nc.dram_tensor("dbg_x1", [TOK, D], F32, kind="ExternalOutput").ap(),
            "x2": nc.dram_tensor("dbg_x2", [TOK, D], F32, kind="ExternalOutput").ap(),
            "x3": nc.dram_tensor("dbg_x3", [TOK, D], F32, kind="ExternalOutput").ap(),
            "ob1": nc.dram_tensor("dbg_ob1", [NSLOT, D], F32, kind="ExternalOutput").ap(),
        }
    d_xnd = nc.dram_tensor("xn_scr", [TOK, D], BF16, kind="Internal").ap()
    d_idx = nc.dram_tensor("idx_scr", [IDXSPACE, 1], U32, kind="Internal").ap()
    d_ob = nc.dram_tensor("ob_scr", [NSLOT, D], F32, kind="Internal").ap()
    d_cshup = nc.dram_tensor("cshup", [JSH, 128, 4096], BF16, kind="Internal").ap()
    d_cshdn = nc.dram_tensor("cshdn", [NGRP, 128, GSZ * D], BF16, kind="Internal").ap()
    d_cexdn = nc.dram_tensor("cexdn", [NEXP, 128, JEX * D], BF16, kind="Internal").ap()

    stack = ExitStack()
    with stack:
        E_ = stack.enter_context
        t_x = E_(nc.sbuf_tensor("t_x", [128, NB * D], F32))
        t_xnf = E_(nc.sbuf_tensor("t_xnf", [128, KC * TOK], BF16))
        t_h = E_(nc.sbuf_tensor("t_h", [128, GSZ * TOK], BF16))
        t_wd = E_(nc.sbuf_tensor("t_wd", [128, JEX * D], BF16))
        t_up = E_(nc.sbuf_tensor("t_up", [128, 2 * 4096], BF16))
        t_osb = E_(nc.sbuf_tensor("t_osb", [128, 2 * D], F32))
        t_xnt = E_(nc.sbuf_tensor("t_xnt", [128, D], BF16))
        t_gbs = E_(nc.sbuf_tensor("t_gbs", [128, D], BF16))
        t_rt = E_(nc.sbuf_tensor("t_rt", [128, 128], BF16))
        t_tri = E_(nc.sbuf_tensor("t_tri", [128, 128], BF16))
        t_idn = E_(nc.sbuf_tensor("t_idn", [128, 128], BF16))
        t_iota8 = E_(nc.sbuf_tensor("t_iota8", [128, 8], F32))
        t_tokid = E_(nc.sbuf_tensor("t_tokid", [128, 8], U32))
        t_ones128 = E_(nc.sbuf_tensor("t_ones128", [128, 1], BF16))
        t_ones1 = E_(nc.sbuf_tensor("t_ones1", [1, 128], F32))
        t_sent = E_(nc.sbuf_tensor("t_sent", [128, 32], U32))
        t_eps = E_(nc.sbuf_tensor("t_eps", [128, 1], F32))
        t_neg1 = E_(nc.sbuf_tensor("t_neg1", [128, 1], F32))
        t_t8 = E_(nc.sbuf_tensor("t_t8", [128, 8], F32))
        t_ix8 = E_(nc.sbuf_tensor("t_ix8", [128, 8], U32))
        t_t2A = E_(nc.sbuf_tensor("t_t2A", [128, 16], F32))
        t_e12A = E_(nc.sbuf_tensor("t_e12A", [128, 16], F32))
        t_eqA = E_(nc.sbuf_tensor("t_eqA", [128, 128], F32))
        t_sbfA = E_(nc.sbuf_tensor("t_sbfA", [128, 64], BF16))
        t_psbA = E_(nc.sbuf_tensor("t_psbA", [128, 64], F32))
        t_d12A = E_(nc.sbuf_tensor("t_d12A", [128, 8], F32))
        t_cwA = E_(nc.sbuf_tensor("t_cwA", [128, 16], F32))
        t_posA = E_(nc.sbuf_tensor("t_posA", [128, 16], F32))
        t_sA = E_(nc.sbuf_tensor("t_sA", [128, 32], F32))
        t_suA = E_(nc.sbuf_tensor("t_suA", [128, 16], U32))
        t_su2 = E_(nc.sbuf_tensor("t_su2", [128, 32], U32))
        t_baseA = E_(nc.sbuf_tensor("t_baseA", [1, 64], F32))
        t_slots = E_(nc.sbuf_tensor("t_slots", [128, 16], U32))
        t_wv = E_(nc.sbuf_tensor("t_wv", [128, 16], F32))
        t_idxsb = E_(nc.sbuf_tensor("t_idxsb", [128, 32], U32))
        t_bn = E_(nc.sbuf_tensor("t_bn", [128, 24], F32))
        t_bn2 = E_(nc.sbuf_tensor("t_bn2", [128, 2], F32))
        t_ln = E_(nc.sbuf_tensor("t_ln", [128, 4], F32))
        t_lnm = E_(nc.sbuf_tensor("t_lnm", [128, 16], F32))
        t_sil = E_(nc.sbuf_tensor("t_sil", [128, 512], F32))

        p_y = [E_(nc.psum_tensor("p_y%d" % i, [128, 512], F32)) for i in range(4)]
        p_ga = E_(nc.psum_tensor("p_ga", [128, 512], F32))
        p_gb = E_(nc.psum_tensor("p_gb", [128, 512], F32))
        p_tr = E_(nc.psum_tensor("p_tr", [128, 1024], BF16))
        p_sm = E_(nc.psum_tensor("p_sm", [128, 512], F32))

        P = Prog(nc, stack)
        regs = {}
        def _mkregs(E):
            regs["b_tok"] = E.to_reg(TOK - 1)
            regs["b_slot"] = E.to_reg(NSLOT - 1)
            regs["b_idxsp"] = E.to_reg(IDXSPACE - 1)
        P.q["gpsimd"].append(lambda E: _mkregs(E))

        b_x = [Buf("x%d" % b) for b in range(NB)]
        b_xnf = Buf("xnf")
        b_h = [Buf("h%d" % j) for j in range(GSZ)]
        b_wd = Buf("wd")
        b_upA = [Buf("upA0"), Buf("upA1")]
        b_upB = [Buf("upB0"), Buf("upB1")]
        b_osb = [Buf("osb0"), Buf("osb1")]
        b_wd2 = Buf("wd2")
        b_xnt = Buf("xnt")
        b_gbs = [Buf("gbs0"), None]
        b_consts = Buf("consts")
        b_rt = Buf("rt")
        b_t8 = Buf("t8"); b_ix8 = Buf("ix8"); b_t2A = Buf("t2A")
        b_e12A = Buf("e12A")
        b_eqA = Buf("eqA"); b_sbfA = Buf("sbfA")
        b_psbA = Buf("psbA"); b_d12A = Buf("d12A"); b_cwA = Buf("cwA")
        b_posA = Buf("posA"); b_sA = Buf("sA"); b_suA = Buf("suA")
        b_su2 = Buf("su2"); b_baseA = Buf("baseA")
        b_slots = Buf("slots"); b_wv = Buf("wv")
        b_idxsb = Buf("idxsb"); b_bn = Buf("bn"); b_bn2 = Buf("bn2")
        b_ln = Buf("ln"); b_lnm = Buf("lnm"); b_sil = Buf("sil")
        b_gbs[1] = b_xnt
        b_py = [Buf("py%d" % i) for i in range(4)]
        b_ga = Buf("ga"); b_gb = Buf("gb"); b_tr = Buf("tr")
        b_sm_lg = [Buf("sm_lg%d" % b) for b in range(NB)]
        b_sm_cnt = Buf("sm_cnt")
        b_sm_tri = Buf("sm_tri"); b_sm_bas = Buf("sm_bas")
        st_bufs = {"xnd": Buf("xnd"), "idxd": Buf("idxd"), "obd": Buf("obd")}
        cb_shup = [Buf("cshup%d" % j) for j in range(JSH)]
        cb_shdn = [Buf("cshdn%d" % g) for g in range(NGRP)]
        cb_exdn = [Buf("cexdn%d" % e) for e in range(NEXP)]
        cs_slot = [Buf("cs_s%d" % i) for i in range(4)]
        wb_slot = [Buf("wb_s%d" % i) for i in range(2)]
        castn = {"i": 0}

        def ld(dst, src, buf=None, add=False, eng="sync", sem="LIN"):
            return P.op(eng, lambda E, d=dst, s=src: E.dma_start(out=d, in_=s),
                        writes=(buf,) if (buf and not add) else (),
                        add_writes=(buf,) if (buf and add) else (),
                        sem=sem, inc=16)

        ld(t_tri[:, :], d_tri[:, :])
        ld(t_idn[:, :], d_idn[:, :])
        ld(t_iota8[:, :], d_iota8[:, :])
        ld(t_tokid[:, :], d_tokid[:, :])
        ld(t_ones128[:, :], d_ones128[:, :])
        ld(t_ones1[:, :], d_ones1[:, :])
        ld(t_sent[:, :], d_sent[:, :])
        ld(t_eps[:, :], d_eps[:, :])
        ld(t_neg1[:, :], d_neg1[:, :])
        d_xr = d_x.rearrange("(b p) d -> p b d", p=128)
        t_xr = t_x[:, :].rearrange("p (b d) -> p b d", d=D)
        tok_lo = ld(t_xr[:, 0:4, :], d_xr[:, 0:4, :])
        tok_hi = ld(t_xr[:, 4:8, :], d_xr[:, 4:8, :], eng="scalar", sem="LIN2")
        b_consts.w = [tok_lo]
        for b in range(NB):
            b_x[b].w = [tok_lo if b < 4 else tok_hi]
        P.op("gpsimd", lambda E: E.dma_start(out=t_rt[:, :], in_=d_rt[:, :]),
             writes=(b_rt,), sem="WMISC")

        def xblk(b):
            return t_x[:, b * D:(b + 1) * D]

        def ln_block(b, keep_mix=False):
            for c in range(4):
                P.op("vector", lambda E, c=c, b=b: E.bn_stats(
                    out=t_bn[:, 6 * c:6 * c + 6],
                    in_=xblk(b)[:, 512 * c:512 * c + 512]),
                    reads=(b_x[b],), writes=(b_bn,) if c == 0 else (),
                    add_writes=(b_bn,) if c > 0 else ())
            P.op("vector", lambda E: E.bn_aggr(out=t_bn2[:, :], in_=t_bn[:, :]),
                 reads=(b_bn,), writes=(b_bn2,))
            P.op("scalar", lambda E: E.activation(out=t_ln[:, 2:3], in_=t_bn2[:, 1:2],
                                                  func=AF.Sqrt, bias=t_eps[:, :]),
                 reads=(b_bn2, b_consts), writes=(b_ln,))
            P.op("vector", lambda E: E.reciprocal(out=t_ln[:, 0:1], in_=t_ln[:, 2:3]),
                 reads=(b_ln,), add_writes=(b_ln,))
            P.op("vector", lambda E: E.tensor_scalar(
                out=t_ln[:, 1:2], in0=t_bn2[:, 0:1], scalar1=t_ln[:, 0:1],
                scalar2=-1.0, op0=ALU.mult, op1=ALU.mult),
                reads=(b_bn2, b_ln), add_writes=(b_ln,))
            if keep_mix:
                P.op("vector", lambda E, b=b: E.tensor_copy(
                    out=t_lnm[:, 2 * b:2 * b + 2], in_=t_ln[:, 0:2]),
                    reads=(b_ln,), writes=(b_lnm,) if b == 0 else (),
                    add_writes=(b_lnm,) if b > 0 else ())
            P.op("scalar", lambda E, b=b: E.activation(
                out=t_xnt[:, :], in_=xblk(b), func=AF.Identity,
                bias=t_ln[:, 1:2], scale=t_ln[:, 0:1]),
                reads=(b_x[b], b_ln), writes=(b_xnt,))

        def xn_transpose(b, first):
            for half in range(2):
                ins = [(lambda k=k: t_xnt[:, 128 * k:128 * k + 128])
                       for k in range(8 * half, 8 * half + 8)]
                outs = [(lambda q=q: p_tr[:, 128 * q:128 * q + 128])
                        for q in range(8)]
                P.tr_group(ins, outs, (b_xnt, b_consts), b_tr, t_idn[:, :])

                def ev(E, b=b, half=half):
                    dst = t_xnf[:, :].rearrange("p (k t) -> p k t", k=KC)[
                        :, 8 * half:8 * half + 8, 128 * b:128 * b + 128]
                    src = p_tr[:, :].rearrange("p (k t) -> p k t", k=8)
                    return E.tensor_copy(out=dst, in_=src)
                fw = first and half == 0
                P.op("vector", ev, reads=(b_tr,),
                     writes=(b_xnf,) if fw else (),
                     add_writes=(b_xnf,) if not fw else ())

        def routing_logits(b):
            mms = [(lambda b=b: p_sm[:, 8 * b:8 * b + 8],
                    lambda k=k, b=b: t_xnf[:, TOK * k + 128 * b: TOK * k + 128 * b + 128],
                    lambda k=k: t_rt[:, 8 * k:8 * k + 8],
                    k == 0, k == KC - 1) for k in range(KC)]
            P.mm_group(mms, reads=(b_xnf, b_rt), psum=b_sm_lg[b])
            P.op("vector", lambda E, b=b: E.max(
                out=t_t8[:, :], in_=p_sm[:, 8 * b:8 * b + 8]),
                reads=(b_sm_lg[b],), writes=(b_t8,))
            P.op("vector", lambda E, b=b: E.max_index(
                out=t_ix8[:, :], in_max=t_t8[:, :],
                in_values=p_sm[:, 8 * b:8 * b + 8]),
                reads=(b_sm_lg[b], b_t8), writes=(b_ix8,))
            P.op("vector", lambda E, b=b: E.tensor_copy(
                out=t_t2A[:, 2 * b:2 * b + 2], in_=t_t8[:, 0:2]),
                reads=(b_t8,),
                writes=(b_t2A,) if b == 0 else (), add_writes=(b_t2A,) if b else ())
            P.op("vector", lambda E, b=b: E.tensor_copy(
                out=t_e12A[:, 2 * b:2 * b + 2], in_=t_ix8[:, 0:2]),
                reads=(b_ix8,),
                writes=(b_e12A,) if b == 0 else (),
                add_writes=(b_e12A,) if b else ())

        def routing_batched(b_idxd):
            def r3(t):
                return t[:, :].rearrange("p (b e) -> p b e", e=8)

            def r2(t):
                return t[:, :].rearrange("p (b k) -> p b k", k=2)

            P.op("vector", lambda E: E.tensor_tensor(
                out=t_d12A[:, :], in0=r2(t_t2A)[:, :, 0:1],
                in1=r2(t_t2A)[:, :, 1:2], op=ALU.subtract),
                reads=(b_t2A,), writes=(b_d12A,))
            P.op("scalar", lambda E: E.activation(
                out=r2(t_cwA)[:, :, 0:1], in_=t_d12A[:, :], func=AF.Sigmoid),
                reads=(b_d12A,), writes=(b_cwA,))
            P.op("scalar", lambda E: E.activation(
                out=r2(t_cwA)[:, :, 1:2], in_=t_d12A[:, :], func=AF.Sigmoid,
                scale=t_neg1[:, :]),
                reads=(b_d12A, b_consts), add_writes=(b_cwA,))
            for b in range(NB):
                for kk in range(2):
                    P.op("vector", lambda E, b=b, kk=kk: E.tensor_scalar(
                        out=t_eqA[:, 64 * kk + 8 * b:64 * kk + 8 * b + 8],
                        in0=t_iota8[:, :], scalar1=t_e12A[:, 2 * b + kk:2 * b + kk + 1],
                        scalar2=None, op0=ALU.is_equal),
                        reads=(b_consts, b_e12A),
                        writes=(b_eqA,) if (b == 0 and kk == 0) else (),
                        add_writes=(b_eqA,) if (b or kk) else ())
            P.op("vector", lambda E: E.tensor_tensor(
                out=t_psbA[:, :], in0=t_eqA[:, 0:64], in1=t_eqA[:, 64:128],
                op=ALU.add),
                reads=(b_eqA,), writes=(b_psbA,))
            P.op("vector", lambda E: E.tensor_copy(out=t_sbfA[:, :], in_=t_psbA[:, :]),
                 reads=(b_psbA,), writes=(b_sbfA,))
            def routing_tail1():
                P.mm_group([(lambda: p_sm[0:1, 256:320], lambda: t_ones128[:, :],
                             lambda: t_sbfA[:, :], True, True)],
                           reads=(b_consts, b_sbfA), psum=b_sm_cnt)
                P.mm_group([(lambda: p_sm[:, 128:192], lambda: t_tri[:, :],
                             lambda: t_sbfA[:, :], True, True)],
                           reads=(b_consts, b_sbfA), psum=b_sm_tri)
                P.op("vector", lambda E: E.memset(t_baseA[:, 0:8], 0.0),
                     writes=(b_baseA,))
                for b in range(1, NB):
                    P.op("vector", lambda E, b=b: E.tensor_tensor(
                        out=t_baseA[:, 8 * b:8 * b + 8],
                        in0=t_baseA[:, 8 * b - 8:8 * b],
                        in1=p_sm[0:1, 256 + 8 * b - 8:256 + 8 * b], op=ALU.add),
                        reads=(b_sm_cnt, b_baseA), add_writes=(b_baseA,))

            def routing_tail2():
              P.mm_group([(lambda: p_sm[:, 192:256], lambda: t_ones1[:, :],
                           lambda: t_baseA[:, :], True, True)],
                         reads=(b_consts, b_baseA), psum=b_sm_bas)
              P.op("vector", lambda E: E.tensor_copy(out=t_psbA[:, :],
                                                     in_=p_sm[:, 128:192]),
                   reads=(b_sm_tri,), writes=(b_psbA,))
              P.op("vector", lambda E: E.scalar_tensor_tensor(
                  out=t_psbA[:, :], in0=p_sm[:, 192:256], scalar=1.0,
                  in1=t_psbA[:, :], op0=ALU.mult, op1=ALU.add),
                  reads=(b_sm_bas, b_psbA), add_writes=(b_psbA,))
              for kk in range(2):
                  P.op("vector", lambda E, kk=kk: E.tensor_tensor(
                      out=t_eqA[:, 64 * kk:64 * kk + 64], in0=t_psbA[:, :],
                      in1=t_eqA[:, 64 * kk:64 * kk + 64], op=ALU.mult),
                      reads=(b_psbA,), writes=(b_eqA,))
                  P.op("vector", lambda E, kk=kk: E.reduce_sum(
                      out=r2(t_posA)[:, :, kk:kk + 1],
                      in_=t_eqA[:, 64 * kk:64 * kk + 64].rearrange(
                          "p (b e) -> p b e", e=8), axis=AX),
                      reads=(b_eqA,), writes=(b_posA,) if kk == 0 else (),
                      add_writes=(b_posA,) if kk else ())
              P.op("vector", lambda E: E.tensor_single_scalar(
                  out=t_sA[:, 0:16], in_=t_posA[:, :], scalar=float(CL),
                  op=ALU.is_ge),
                  reads=(b_posA,), writes=(b_sA,))
              P.op("vector", lambda E: E.tensor_scalar(
                  out=t_sA[:, 16:32], in0=t_sA[:, 0:16], scalar1=-0.5, scalar2=0.5,
                  op0=ALU.mult, op1=ALU.add),
                  reads=(b_sA,), add_writes=(b_sA,))
              P.op("vector", lambda E: E.tensor_tensor(
                  out=t_wv[:, :], in0=t_sA[:, 16:32], in1=t_cwA[:, :], op=ALU.mult),
                  reads=(b_sA, b_cwA), writes=(b_wv,))
              P.op("vector", lambda E: E.scalar_tensor_tensor(
                  out=t_sA[:, 16:32], in0=t_e12A[:, :], scalar=float(CL),
                  in1=t_posA[:, :], op0=ALU.mult, op1=ALU.add),
                  reads=(b_e12A, b_posA, b_wv), add_writes=(b_sA,))
              P.op("vector", lambda E: E.scalar_tensor_tensor(
                  out=t_cwA[:, :], in0=t_sA[:, 0:16], scalar=BIGF,
                  in1=t_sA[:, 16:32], op0=ALU.mult, op1=ALU.add),
                  reads=(b_sA,), writes=(b_cwA,))
              P.op("vector", lambda E: E.tensor_copy(
                  out=t_slots[:, :], in_=t_cwA[:, :]),
                  reads=(b_cwA,), writes=(b_slots,))
              P.op("vector", lambda E: E.tensor_copy(out=t_su2[:, 0:16],
                                                     in_=t_sA[:, 16:32]),
                   reads=(b_sA,), writes=(b_su2,))
              P.op("vector", lambda E: E.tensor_scalar(
                  out=t_suA[:, :], in0=t_su2[:, 0:16], scalar1=7, scalar2=None,
                  op0=ALU.logical_shift_right),
                  reads=(b_su2,), writes=(b_suA,))
              P.op("vector", lambda E: E.tensor_scalar(
                  out=t_su2[:, 16:32], in0=t_su2[:, 0:16], scalar1=127, scalar2=None,
                  op0=ALU.bitwise_and),
                  reads=(b_su2,), add_writes=(b_su2,))
              P.op("vector", lambda E: E.tensor_scalar(
                  out=t_su2[:, 16:32], in0=t_su2[:, 16:32], scalar1=5, scalar2=None,
                  op0=ALU.logical_shift_left),
                  reads=(b_su2,), add_writes=(b_su2,))
              P.op("vector", lambda E: E.tensor_tensor(
                  out=t_suA[:, :], in0=t_suA[:, :], in1=t_su2[:, 16:32],
                  op=ALU.bitwise_or),
                  reads=(b_su2,), add_writes=(b_suA,))
              P.op("vector", lambda E: E.tensor_copy(out=t_su2[:, 0:16],
                                                     in_=t_sA[:, 0:16]),
                   reads=(b_sA,), add_writes=(b_su2,))
              P.op("vector", lambda E: E.tensor_scalar(
                  out=t_su2[:, 0:16], in0=t_su2[:, 0:16], scalar1=30, scalar2=None,
                  op0=ALU.logical_shift_left),
                  reads=(b_su2,), add_writes=(b_su2,))
              P.op("vector", lambda E: E.tensor_tensor(
                  out=t_suA[:, :], in0=t_suA[:, :], in1=t_su2[:, 0:16],
                  op=ALU.bitwise_or),
                  reads=(b_su2,), add_writes=(b_suA,))
            def emit_scatters():
                for b in range(NB):
                    for kk in range(2):
                        P.op("gpsimd", lambda E, b=b, kk=kk: E.indirect_dma_start(
                            out=d_idx[:, :],
                            out_offset=IndirectOffsetOnAxis(
                                t_suA[:, 2 * b + kk:2 * b + kk + 1], axis=0),
                            in_=t_tokid[:, b:b + 1], in_offset=None,
                            bounds_check=regs["b_idxsp"], oob_is_err=False),
                            reads=(b_suA, b_consts, b_idxd), add_writes=(b_idxd,),
                            sem="IND")
            return routing_tail1, routing_tail2, emit_scatters

        def silu_mul(dst_ap, dst_buf, first_dst, n=512):
            P.op("scalar", lambda E, n=n: E.activation(out=t_sil[:, 0:n], in_=p_ga[:, 0:n],
                                                       func=AF.Silu),
                 reads=(b_ga,), writes=(b_sil,))
            P.op("vector", lambda E, dst_ap=dst_ap, n=n: E.tensor_tensor(
                out=dst_ap, in0=t_sil[:, 0:n], in1=p_gb[:, 0:n], op=ALU.mult),
                reads=(b_sil, b_gb),
                writes=(dst_buf,) if first_dst else (),
                add_writes=(dst_buf,) if not first_dst else ())

        def dbg_store(dst, src_ap, rbuf, slow=False):
            P.op("sync", lambda E, d=dst, sp=src_ap: E.dma_start(
                out=d, in_=sp, allow_slow_non_contiguous=True) if slow else
                E.dma_start(out=d, in_=sp),
                reads=(rbuf,), sem="OUT")

        def emit_ffn(call, pre_block=None):
            b_xnd = st_bufs["xnd"] = fresh(st_bufs["xnd"])
            b_idxd = st_bufs["idxd"] = fresh(st_bufs["idxd"])
            b_obd = st_bufs["obd"] = fresh(st_bufs["obd"])
            if call == 0:
                QW = GSZ * D // 4
                pend = ([("shdn", g, q) for g in range(NGRP) for q in range(4)]
                        + [("exdn", e, q) for e in range(NEXP) for q in range(4)])

                def pop_pieces(k):
                    for _ in range(k):
                        if not pend:
                            return
                        p = pend.pop(0)
                        sl = castn["i"] % 4
                        castn["i"] += 1
                        sem = "CS%d" % sl
                        if p[0] == "shdn":
                            g, q = p[1], p[2]
                            P.op("gpsimd", lambda E, g=g, q=q: E.dma_start(
                                out=d_cshdn[g][:, QW * q:QW * q + QW],
                                in_=d_wsh_dn[g][:, QW * q:QW * q + QW]),
                                writes=(cs_slot[sl],),
                                add_writes=(cb_shdn[g],), sem=sem)
                        else:
                            e, q = p[1], p[2]
                            P.op("gpsimd", lambda E, e=e, q=q: E.dma_start(
                                out=d_cexdn[e][:, QW * q:QW * q + QW],
                                in_=d_wex_dn[e][:, QW * q:QW * q + QW]),
                                writes=(cs_slot[sl],),
                                add_writes=(cb_exdn[e],), sem=sem)

                for j in range(2):
                    P.op("gpsimd", lambda E, j=j: E.dma_start(
                        out=t_up[:, 4096 * (j % 2):4096 * (j % 2) + 4096],
                        in_=d_wsh_up[j]),
                        writes=(b_upA[j % 2], b_upB[j % 2]),
                        sem="W%d" % (j % 2))
                pop_pieces(16)
            else:
                pend = []

                def pop_pieces(k):
                    pass

                for j in range(2):
                    P.op("sync" if j % 2 == 0 else "scalar",
                         lambda E, j=j: E.dma_start(
                             out=t_up[:, 4096 * (j % 2):4096 * (j % 2) + 4096],
                             in_=d_cshup[j]),
                         reads=(cb_shup[j],),
                         writes=(b_upA[j % 2], b_upB[j % 2]),
                         sem="W%d" % (j % 2), inc=16)
            P.op("sync", lambda E: E.dma_start(
                out=d_idx.rearrange("(p s) o -> p (s o)", p=128), in_=t_sent[:, :]),
                reads=(b_consts,), writes=(b_idxd,), sem="IDX")
            for b in range(NB):
                if pre_block is not None:
                    pre_block(b)
                ln_block(b)
                P.op("sync", lambda E, b=b: E.dma_start(
                    out=d_xnd[128 * b:128 * b + 128, :], in_=t_xnt[:, :]),
                    reads=(b_xnt,), writes=(b_xnd,) if b == 0 else (),
                    add_writes=(b_xnd,) if b else (), sem="XN")
                if debug and call == 0:
                    dbg_store(dbg["xn1"][128 * b:128 * b + 128, :], t_xnt[:, :], b_xnt)
                xn_transpose(b, first=(b == 0))
                routing_logits(b)
            routing_tail1, routing_tail2, emit_scatters = routing_batched(b_idxd)
            def emit_scat_idx():
                emit_scatters()
                P.op("sync", lambda E: E.dma_start(
                    out=t_idxsb[:, :],
                    in_=d_idx.rearrange("(p s) o -> p (s o)", p=128)),
                    reads=(b_idxd,), writes=(b_idxsb,), sem="IDX")
                if debug and call == 0:
                    dbg_store(dbg["ix1"][:, :], t_idxsb[:, :], b_idxsb)

            disp = {"steps": [], "b_buf": None}

            def run_step():
                if not disp["steps"]:
                    return
                st = disp["steps"].pop(0)
                b_buf = disp["b_buf"]
                if st[0] == "g":
                    _, e_t, sb = st
                    col = (e_t * CL + 128 * sb) // 128
                    gp = sb % 2
                    P.op("gpsimd", lambda E, col=col, gp=gp: E.indirect_dma_start(
                        out=(t_gbs[:, :] if gp == 0 else t_xnt[:, :]), out_offset=None,
                        in_=d_xnd[:, :],
                        in_offset=IndirectOffsetOnAxis(t_idxsb[:, col:col + 1], axis=0),
                        bounds_check=regs["b_tok"], oob_is_err=False),
                        reads=(b_idxsb, b_xnd), writes=(b_gbs[gp],), sem="IND" if gp == 0 else "G0")
                else:
                    _, e_t, sb, half = st
                    gp = sb % 2
                    src_t = t_gbs if gp == 0 else t_xnt
                    ins = [(lambda k=k, src_t=src_t: src_t[:, 128 * k:128 * k + 128])
                           for k in range(8 * half, 8 * half + 8)]
                    outs = [(lambda q=q: p_tr[:, 128 * q:128 * q + 128])
                            for q in range(8)]
                    P.tr_group(ins, outs, (b_gbs[gp], b_consts), b_tr, t_idn[:, :])

                    def ev(E, e_t=e_t, sb=sb, half=half):
                        dst = t_xnf[:, 0:2 * KC * CL].rearrange(
                            "p (pp k t) -> p pp k t", pp=2, k=KC)[
                            :, e_t % 2, 8 * half:8 * half + 8, 128 * sb:128 * sb + 128]
                        src = p_tr[:, :].rearrange("p (k t) -> p k t", k=8)
                        return E.tensor_copy(out=dst, in_=src)
                    fw = (e_t == 0 and sb == 0 and half == 0)
                    P.op("vector", ev, reads=(b_tr,),
                         writes=(b_buf,) if fw else (),
                         add_writes=(b_buf,) if not fw else ())

            def queue_dispatch(e_t):
                for sb in range(NSB):
                    disp["steps"].append(("g", e_t, sb))
                    disp["steps"].append(("t", e_t, sb, 0))
                    disp["steps"].append(("t", e_t, sb, 1))

            for g in range(NGRP):
                for jj in range(GSZ):
                    j = g * GSZ + jj
                    par = j % 2
                    if not (g == 0 and jj < 2):
                        if call == 0:
                            P.op("gpsimd", lambda E, j=j, par=par: E.dma_start(
                                out=t_up[:, 4096 * par:4096 * par + 4096],
                                in_=d_wsh_up[j]),
                                writes=(b_upA[par], b_upB[par]),
                                sem="W%d" % par)
                        else:
                            P.op("sync" if par == 0 else "scalar",
                                 lambda E, j=j, par=par: E.dma_start(
                                     out=t_up[:, 4096 * par:4096 * par + 4096],
                                     in_=d_cshup[j]),
                                 reads=(cb_shup[j],),
                                 writes=(b_upA[par], b_upB[par]),
                                 sem="W%d" % par, inc=16)
                    if g == 0 and jj == 1:
                        routing_tail1()
                        routing_tail2()
                    if g == 0 and jj == 10:
                        emit_scat_idx()
                    if jj == 1:
                        HW = GSZ * D // 2
                        P.op("scalar", lambda E, g=g, HW=HW: E.dma_start(
                            out=t_wd[:, :HW], in_=d_cshdn[g][:, :HW]),
                            reads=(cb_shdn[g],), writes=(b_wd,), sem="WDV", inc=16)
                        P.op("sync", lambda E, g=g, HW=HW: E.dma_start(
                            out=t_wd[:, HW:GSZ * D], in_=d_cshdn[g][:, HW:]),
                            reads=(cb_shdn[g],), writes=(b_wd2,), sem="WD", inc=16)
                    for tt in range(2):
                        mms = [(lambda: p_ga[:, :],
                                lambda k=k, par=par: t_up[:, 4096 * par + 128 * k:
                                                          4096 * par + 128 * k + 128],
                                lambda k=k, tt=tt: t_xnf[:, TOK * k + 512 * tt:
                                                         TOK * k + 512 * tt + 512],
                                k == 0, k == KC - 1) for k in range(KC)]
                        P.mm_group(mms, reads=(b_upA[par], b_xnf), psum=b_ga)
                        mms = [(lambda: p_gb[:, :],
                                lambda k=k, par=par: t_up[:, 4096 * par + 2048 + 128 * k:
                                                          4096 * par + 2048 + 128 * k + 128],
                                lambda k=k, tt=tt: t_xnf[:, TOK * k + 512 * tt:
                                                         TOK * k + 512 * tt + 512],
                                k == 0, k == KC - 1) for k in range(KC)]
                        P.mm_group(mms, reads=(b_upB[par], b_xnf), psum=b_gb)
                        silu_mul(t_h[:, TOK * jj + 512 * tt: TOK * jj + 512 * tt + 512],
                                 b_h[jj], first_dst=(tt == 0))
                        if call == 0 and tt == 0:
                            P.op("scalar", lambda E, j=j, par=par: E.dma_start(
                                out=d_cshup[j],
                                in_=t_up[:, 4096 * par:4096 * par + 4096]),
                                reads=(b_upA[par], b_upB[par]),
                                writes=(wb_slot[par],),
                                add_writes=(cb_shup[j],),
                                sem="WB%d" % par, inc=16)
                if g == NGRP - 1:
                    disp["b_buf"] = fresh(b_xnf)
                    queue_dispatch(0)
                for b in range(NB):
                    pop_pieces(1)
                    for q in range(4):
                        mms = [(lambda q=q: p_y[q][:, :],
                                lambda jj=jj, b=b: t_h[:, TOK * jj + 128 * b:
                                                       TOK * jj + 128 * b + 128],
                                lambda jj=jj, q=q: t_wd[:, D * jj + 512 * q:
                                                        D * jj + 512 * q + 512],
                                jj == 0, jj == GSZ - 1) for jj in range(GSZ)]
                        P.mm_group(mms,
                                   reads=[b_h[jj] for jj in range(GSZ)] + [b_wd, b_wd2],
                                   psum=b_py[q])
                    for q in range(4):
                        P.op("vector", lambda E, b=b, q=q: E.scalar_tensor_tensor(
                            out=xblk(b)[:, 512 * q:512 * q + 512],
                            in0=p_y[q][:, :], scalar=0.5,
                            in1=xblk(b)[:, 512 * q:512 * q + 512],
                            op0=ALU.mult, op1=ALU.add),
                            reads=(b_py[q], b_x[b]), add_writes=(b_x[b],))
                    if g == NGRP - 1:
                        run_step()
                        run_step()

            if debug and call == 0:
                dbg_store(dbg["wv1"][:, :], t_wv[:, :], b_wv)
                dbg_store(dbg["sl1"][:, :], t_slots[:, :], b_slots)
                for b in range(NB):
                    dbg_store(dbg["x1"][128 * b:128 * b + 128, :], xblk(b), b_x[b])

            b_buf = disp["b_buf"]
            for e in range(NEXP):
                par = e % 2
                if e + 1 < NEXP:
                    queue_dispatch(e + 1)
                for j in range(JEX):
                    wpar = j % 2
                    P.op("gpsimd", lambda E, e=e, j=j, wpar=wpar: E.dma_start(
                        out=t_up[:, 4096 * wpar:4096 * wpar + 2048],
                        in_=d_wex_up[e, j][:, 0:2048]),
                        writes=(b_upA[wpar],), sem="W%d" % wpar)
                    P.op("gpsimd", lambda E, e=e, j=j, wpar=wpar: E.dma_start(
                        out=t_up[:, 4096 * wpar + 2048:4096 * wpar + 4096],
                        in_=d_wex_up[e, j][:, 2048:4096]),
                        writes=(b_upB[wpar],), sem="U%d" % wpar)
                    if j == 1:
                        HW = JEX * D // 2
                        P.op("scalar", lambda E, e=e, HW=HW: E.dma_start(
                            out=t_wd[:, :HW], in_=d_cexdn[e][:, :HW]),
                            reads=(cb_exdn[e],), writes=(b_wd,), sem="WDV", inc=16)
                        P.op("sync", lambda E, e=e, HW=HW: E.dma_start(
                            out=t_wd[:, HW:], in_=d_cexdn[e][:, HW:]),
                            reads=(cb_exdn[e],), writes=(b_wd2,), sem="WD", inc=16)
                    mms = [(lambda: p_ga[:, 0:CL],
                            lambda k=k, wpar=wpar: t_up[:, 4096 * wpar + 128 * k:
                                                        4096 * wpar + 128 * k + 128],
                            lambda k=k, par=par: t_xnf[:, KC * CL * par + CL * k:
                                                       KC * CL * par + CL * k + CL],
                            k == 0, k == KC - 1) for k in range(KC)]
                    P.mm_group(mms, reads=(b_upA[wpar], b_buf), psum=b_ga)
                    mms = [(lambda: p_gb[:, 0:CL],
                            lambda k=k, wpar=wpar: t_up[:, 4096 * wpar + 2048 + 128 * k:
                                                        4096 * wpar + 2048 + 128 * k + 128],
                            lambda k=k, par=par: t_xnf[:, KC * CL * par + CL * k:
                                                       KC * CL * par + CL * k + CL],
                            k == 0, k == KC - 1) for k in range(KC)]
                    P.mm_group(mms, reads=(b_upB[wpar], b_buf), psum=b_gb)
                    silu_mul(t_h[:, TOK * j: TOK * j + CL], b_h[j], first_dst=True, n=CL)
                    run_step()
                for sb in range(NSB):
                    for q in range(4):
                        mms = [(lambda q=q: p_y[q][:, :],
                                lambda j=j, sb=sb: t_h[:, TOK * j + 128 * sb:
                                                       TOK * j + 128 * sb + 128],
                                lambda j=j, q=q: t_wd[:, D * j + 512 * q:
                                                      D * j + 512 * q + 512],
                                j == 0, j == JEX - 1) for j in range(JEX)]
                        P.mm_group(mms,
                                   reads=[b_h[j] for j in range(JEX)] + [b_wd, b_wd2],
                                   psum=b_py[q])
                    opar = sb % 2
                    for q in range(4):
                        P.op("vector", lambda E, opar=opar, q=q: E.tensor_copy(
                            out=t_osb[:, D * opar + 512 * q: D * opar + 512 * q + 512],
                            in_=p_y[q][:, :]),
                            reads=(b_py[q],),
                            writes=(b_osb[opar],) if q == 0 else (),
                            add_writes=(b_osb[opar],) if q else ())
                    first_store = (e == 0 and sb == 0)
                    P.op("sync",
                         lambda E, e=e, sb=sb, opar=opar: E.dma_start(
                        out=d_ob[e * CL + 128 * sb: e * CL + 128 * sb + 128, :],
                        in_=t_osb[:, D * opar: D * opar + D]),
                        reads=(b_osb[opar],),
                        writes=(b_obd,) if first_store else (),
                        add_writes=(b_obd,) if not first_store else (),
                        sem="OB%d" % opar, inc=16)
            b_xnf.w = list(b_buf.w)
            b_xnf.r = list(b_buf.r)
            if debug and call == 0:
                P.op("sync", lambda E: E.dma_start(out=dbg["ob1"][:, :], in_=d_ob[:, :]),
                     reads=(b_obd,), sem="OUT")

            def combine_block(b):
                for kk in range(2):
                    P.op("gpsimd", lambda E, b=b, kk=kk: E.indirect_dma_start(
                        out=t_osb[:, D * kk: D * kk + D], out_offset=None,
                        in_=d_ob[:, :],
                        in_offset=IndirectOffsetOnAxis(
                            t_slots[:, 2 * b + kk: 2 * b + kk + 1], axis=0),
                        bounds_check=regs["b_slot"], oob_is_err=False),
                        reads=(b_slots, b_obd), writes=(b_osb[kk],), sem="G%d" % kk)
                for kk in range(2):
                    P.op("vector", lambda E, b=b, kk=kk: E.scalar_tensor_tensor(
                        out=xblk(b), in0=t_osb[:, D * kk: D * kk + D],
                        scalar=t_wv[:, 2 * b + kk: 2 * b + kk + 1],
                        in1=xblk(b), op0=ALU.mult, op1=ALU.add),
                        reads=(b_osb[kk], b_wv, b_x[b]), add_writes=(b_x[b],))
            return combine_block

        def emit_mixer(pre_block):
            for b in range(NB):
                pre_block(b)
                ln_block(b, keep_mix=True)
                xn_transpose(b, first=(b == 0))
            P.op("sync", lambda E: E.dma_start(out=t_osb[0:1, 0:D], in_=d_dvec[:, :]),
                 writes=(b_osb[0],), sem="DLD")
            for q in range(4):
                P.mm_group([(lambda q=q: p_y[q][:, :], lambda: t_ones1[:, :],
                             lambda q=q: t_osb[0:1, 512 * q:512 * q + 512], True, True)],
                           reads=(b_consts, b_osb[0]), psum=b_py[q])
            for q in range(4):
                P.op("vector", lambda E, q=q: E.tensor_copy(
                    out=t_osb[:, D + 512 * q: D + 512 * q + 512], in_=p_y[q][:, :]),
                    reads=(b_py[q],),
                    writes=(b_osb[1],) if q == 0 else (),
                    add_writes=(b_osb[1],) if q else ())
            def bt_load(p):
                P.op("gpsimd", lambda E, p=p: E.dma_start(
                    out=t_up[:, 4096 * (p % 2):4096 * (p % 2) + 4096].rearrange(
                        "p (n d) -> p n d", n=2),
                    in_=d_bt[2 * p:2 * p + 2].rearrange("n p d -> p n d")),
                    writes=(b_upA[p % 2], b_upB[p % 2]), sem="W%d" % (p % 2))
            bt_load(0)
            bt_load(1)
            for n in range(8):
                wpar = (n // 2) % 2
                for tt in range(2):
                    rix = (2 * n + tt) % 4
                    pp, bpp = p_y[rix], b_py[rix]
                    mms = [(lambda pp=pp: pp[:, :],
                            lambda k=k, n=n, wpar=wpar: t_up[
                                :, 4096 * wpar + 2048 * (n % 2) + 128 * k:
                                4096 * wpar + 2048 * (n % 2) + 128 * k + 128],
                            lambda k=k, tt=tt: t_xnf[:, TOK * k + 512 * tt:
                                                     TOK * k + 512 * tt + 512],
                            k == 0, k == KC - 1) for k in range(KC)]
                    P.mm_group(mms, reads=((b_upA[wpar] if n % 2 == 0 else
                                            b_upB[wpar]), b_xnf), psum=bpp)
                    P.op("scalar", lambda E, n=n, tt=tt, pp=pp: E.activation(
                        out=t_h[:, TOK * n + 512 * tt: TOK * n + 512 * tt + 512],
                        in_=pp[:, :], func=AF.Tanh),
                        reads=(bpp,),
                        writes=(b_h[n],) if tt == 0 else (),
                        add_writes=(b_h[n],) if tt else ())
                if n % 2 == 1 and n // 2 + 2 < 4:
                    bt_load(n // 2 + 2)
            P.op("gpsimd", lambda E: E.dma_start(out=t_wd[:, :8 * D], in_=d_ct[:, :]),
                 writes=(b_wd, b_wd2), sem="WD")

            def mixer_tail(b):
                for q in range(4):
                    mms = [(lambda q=q: p_y[q][:, :],
                            lambda n=n, b=b: t_h[:, TOK * n + 128 * b:
                                                 TOK * n + 128 * b + 128],
                            lambda n=n, q=q: t_wd[:, D * n + 512 * q:
                                                  D * n + 512 * q + 512],
                            n == 0, n == 7) for n in range(8)]
                    P.mm_group(mms, reads=[b_h[n] for n in range(8)] + [b_wd, b_wd2],
                               psum=b_py[q])
                P.op("vector", lambda E, b=b: E.tensor_tensor(
                    out=t_osb[:, 0:D], in0=xblk(b), in1=t_osb[:, D:2 * D],
                    op=ALU.mult),
                    reads=(b_x[b], b_osb[1]), writes=(b_osb[0],))
                P.op("vector", lambda E, b=b: E.tensor_scalar(
                    out=t_osb[:, 0:D], in0=t_osb[:, 0:D],
                    scalar1=t_lnm[:, 2 * b:2 * b + 1], scalar2=None, op0=ALU.mult),
                    reads=(b_lnm, b_osb[0]), add_writes=(b_osb[0],))
                P.op("vector", lambda E, b=b: E.scalar_tensor_tensor(
                    out=t_osb[:, 0:D], in0=t_osb[:, D:2 * D],
                    scalar=t_lnm[:, 2 * b + 1:2 * b + 2], in1=t_osb[:, 0:D],
                    op0=ALU.mult, op1=ALU.add),
                    reads=(b_lnm, b_osb[1], b_osb[0]), add_writes=(b_osb[0],))
                P.op("vector", lambda E, b=b: E.tensor_tensor(
                    out=xblk(b), in0=xblk(b), in1=t_osb[:, 0:D], op=ALU.add),
                    reads=(b_osb[0], b_x[b]), add_writes=(b_x[b],))
                for q in range(4):
                    P.op("vector", lambda E, b=b, q=q: E.scalar_tensor_tensor(
                        out=xblk(b)[:, 512 * q:512 * q + 512],
                        in0=p_y[q][:, :], scalar=1.0,
                        in1=xblk(b)[:, 512 * q:512 * q + 512],
                        op0=ALU.mult, op1=ALU.add),
                        reads=(b_py[q], b_x[b]), add_writes=(b_x[b],))
            return mixer_tail

        comb1 = emit_ffn(0)
        if debug:
            for b in range(NB):
                comb1(b)
            comb1 = lambda b: None
            for b in range(NB):
                dbg_store(dbg["x2"][128 * b:128 * b + 128, :], xblk(b), b_x[b])
        mixer_tail = emit_mixer(comb1)
        if debug:
            for b in range(NB):
                mixer_tail(b)
            mixer_tail = None
        if debug:
            for b in range(NB):
                dbg_store(dbg["x3"][128 * b:128 * b + 128, :], xblk(b), b_x[b])
        comb2 = emit_ffn(1, pre_block=mixer_tail)

        for b in range(NB):
            comb2(b)
            P.op("sync" if b % 2 == 0 else "scalar",
                 lambda E, b=b: E.dma_start(
                     out=d_y[128 * b:128 * b + 128, :], in_=xblk(b)),
                 reads=(b_x[b],), sem="OUT" if b % 2 == 0 else "OUT2", inc=16)
        fin = P.cnt["OUT"]
        fin2 = P.cnt["OUT2"]
        sems = P.sems
        P.q["sync"].append(lambda E, v=fin: E.wait_ge(sems["OUT"], v))
        P.q["sync"].append(lambda E, v=fin2: E.wait_ge(sems["OUT2"], v))

        with nc.Block() as block:
            @block.sync
            def _(E):
                for f in P.q["sync"]:
                    f(E)

            @block.gpsimd
            def _(E):
                for f in P.q["gpsimd"]:
                    f(E)

            @block.tensor
            def _(E):
                for f in P.q["tensor"]:
                    f(E)

            @block.vector
            def _(E):
                for f in P.q["vector"]:
                    f(E)

            @block.scalar
            def _(E):
                for f in P.q["scalar"]:
                    f(E)

    return nc


def _prep_weights(L0_ffn_shared_w_up_gate, L0_ffn_shared_w_down,
                  L0_ffn_w_up_gate, L0_ffn_w_down,
                  L0_attn_B, L0_attn_C, L0_ffn_router, L0_attn_D):
    f = np.float32

    def colblk(W, j):
        return np.ascontiguousarray(
            W[:, 128 * j:128 * j + 128].reshape(KC, 128, 128)
            .transpose(1, 0, 2).reshape(128, 2048))

    Wsh = np.asarray(L0_ffn_shared_w_up_gate, f)
    J = 5632
    gate, up = Wsh[:, :J], Wsh[:, J:]
    wsh_up = np.stack([np.concatenate([colblk(gate, j), colblk(up, j)], axis=1)
                       for j in range(JSH)])

    Wd = np.asarray(L0_ffn_shared_w_down, f)
    wsh_dn = np.ascontiguousarray(
        Wd.reshape(NGRP, GSZ, 128, D).transpose(0, 2, 1, 3).reshape(NGRP, 128, GSZ * D))

    We = np.asarray(L0_ffn_w_up_gate, f)
    I = 1408
    wex_up = np.empty((NEXP, JEX, 128, 4096), f)
    for e in range(NEXP):
        ge, ue = We[e][:, :I], We[e][:, I:]
        for j in range(JEX):
            wex_up[e, j, :, :2048] = colblk(ge, j)
            wex_up[e, j, :, 2048:] = colblk(ue, j)

    Wed = np.asarray(L0_ffn_w_down, f)
    wex_dn = np.ascontiguousarray(
        Wed.reshape(NEXP, JEX, 128, D).transpose(0, 2, 1, 3).reshape(NEXP, 128, JEX * D))

    Bt = np.ascontiguousarray(np.asarray(L0_attn_B, f).T)
    bt_w = np.stack([colblk(Bt, n) for n in range(8)])

    Ct = np.ascontiguousarray(np.asarray(L0_attn_C, f).T)
    ct_w = np.ascontiguousarray(
        Ct.reshape(8, 128, D).transpose(1, 0, 2).reshape(128, 8 * D))

    rt_w = np.ascontiguousarray(
        np.asarray(L0_ffn_router, f).reshape(KC, 128, 8).transpose(1, 0, 2).reshape(128, 128))

    d_vec = np.ascontiguousarray(np.asarray(L0_attn_D, f).reshape(1, D))
    return dict(wsh_up=wsh_up, wsh_dn=wsh_dn, wex_up=wex_up, wex_dn=wex_dn,
                bt_w=bt_w, ct_w=ct_w, rt_w=rt_w, d_vec=d_vec)


def _consts():
    bf = ml_dtypes.bfloat16
    tri = np.triu(np.ones((128, 128), np.float32), 1).astype(bf)
    idn = np.eye(128, dtype=np.float32).astype(bf)
    iota8 = np.tile(np.arange(8, dtype=np.float32), (128, 1))
    tokid = (np.arange(128, dtype=np.uint32)[:, None]
             + 128 * np.arange(8, dtype=np.uint32)[None, :]).astype(np.uint32)
    ones128 = np.ones((128, 1), np.float32).astype(bf)
    ones1 = np.ones((1, 128), np.float32)
    sent = np.full((128, 32), 0x3FFFFFFF, np.uint32)
    eps = np.full((128, 1), 1e-5, np.float32)
    neg1 = np.full((128, 1), -1.0, np.float32)
    return dict(tri_c=np.ascontiguousarray(tri), idn_c=np.ascontiguousarray(idn),
                iota8_c=iota8, tokid_c=tokid, ones128_c=ones128, ones1_c=ones1,
                sent_c=sent, eps_c=eps, neg1_c=neg1)


def kernel(L0_attn_x, L0_attn_B, L0_attn_C, L0_attn_D, L0_ffn_router,
           L0_ffn_w_up_gate, L0_ffn_w_down, L0_ffn_shared_w_up_gate,
           L0_ffn_shared_w_down):
    x = np.ascontiguousarray(np.asarray(L0_attn_x, np.float32).reshape(8192, D))
    w = _prep_weights(L0_ffn_shared_w_up_gate, L0_ffn_shared_w_down,
                      L0_ffn_w_up_gate, L0_ffn_w_down,
                      L0_attn_B, L0_attn_C, L0_ffn_router, L0_attn_D)
    c = _consts()
    nc = build_nc()
    in_maps = []
    for core in range(8):
        m = {"x_in": np.ascontiguousarray(x[TOK * core:TOK * (core + 1)])}
        m.update(w)
        m.update(c)
        in_maps.append(m)
    res = run_bass_kernel_spmd(nc, in_maps, core_ids=list(range(8)))
    out = np.concatenate([np.asarray(r["y_out"]) for r in res.results], axis=0)
    return out.reshape(2, 4096, D).astype(np.float32)


# version 30
# speedup vs baseline: 1.0566x; 1.0566x over previous
import numpy as np
import ml_dtypes
import concourse.bass as bass
import concourse.mybir as mybir
from concourse.bass_utils import run_bass_kernel_spmd
from concourse.bass_core_rust import IndirectOffsetOnAxis
from contextlib import ExitStack

F32 = mybir.dt.float32
BF16 = mybir.dt.bfloat16
U32 = mybir.dt.uint32
AF = mybir.ActivationFunctionType
ALU = mybir.AluOpType
AX = mybir.AxisListType.X

D = 2048
TOK = 1024
NB = 8
KC = 16
NEXP = 8
CL = 384
NSLOT = NEXP * CL
NSB = CL // 128
IDXSPACE = 128 * 32
JSH = 44
JEX = 11
NGRP = 4
GSZ = JSH // NGRP
EPS = 1e-5
BIGF = 1.0e9

ENGINES = ("tensor", "vector", "scalar", "sync", "gpsimd")


class Buf:
    __slots__ = ("w", "r", "name")

    def __init__(self, name=""):
        self.name = name
        self.w = []
        self.r = []


class Prog:
    def __init__(self, nc, stack):
        self.nc = nc
        self.q = {e: [] for e in ENGINES}
        self.cnt = {}
        self.sems = {}
        self.waited = {e: {} for e in ENGINES}
        for s in ("LIN", "LIN2", "WMISC", "XN", "IDX", "W0", "W1", "WD", "WDV",
                  "OB0", "OB1", "CS0", "CS1", "CS2", "CS3", "WB0", "WB1",
                  "U0", "U1",
                  "IND", "G0", "G1", "OUT", "OUT2", "DLD", "SIO", "W", "PE",
                  "DVE", "ACT"):
            self.sems[s] = stack.enter_context(nc.semaphore(s))
            self.cnt[s] = 0

    def _collect(self, eng, toks):
        need = {}
        for t in toks:
            if t is None:
                continue
            s, v, peng = t
            if peng == "tensor" and eng == "tensor":
                continue
            if self.waited[eng].get(s, -1) >= v:
                continue
            if need.get(s, -1) < v:
                need[s] = v
        for s, v in need.items():
            self.waited[eng][s] = v
        return sorted(need.items())

    def _waits_for(self, eng, reads, writes):
        toks = []
        for b in reads:
            toks += b.w
        for b in writes:
            toks += b.w + b.r
        return self._collect(eng, toks)

    def op(self, eng, fn, reads=(), writes=(), add_writes=(), sem=None, inc=None):
        need = self._waits_for(eng, reads, writes)
        if sem is None:
            sem = {"tensor": "PE", "vector": "DVE", "scalar": "ACT",
                   "sync": "SIO", "gpsimd": "W"}[eng]
        if inc is None:
            inc = 16 if eng in ("sync", "gpsimd") else 1
        self.cnt[sem] += inc
        tok = (sem, self.cnt[sem], eng)
        sems = self.sems

        def closure(E, fn=fn, need=tuple(need), semh=sems[sem], n=inc):
            for s, v in need:
                E.wait_ge(sems[s], v)
            fn(E).then_inc(semh, n)

        self.q[eng].append(closure)
        for b in reads:
            b.r.append(tok)
        for b in writes:
            b.w = [tok]
            b.r = []
        for b in add_writes:
            b.w.append(tok)
        return tok

    def mm_group(self, mms, reads=(), psum=None, accum=False):
        toks = []
        for b in reads:
            toks += b.w
        if psum is not None:
            toks += psum.r
            if not accum:
                toks += psum.w
        need = self._collect("tensor", toks)
        sems = self.sems
        nmm = len(mms)
        self.cnt["PE"] += 1
        tok = ("PE", self.cnt["PE"], "tensor")
        for i, (out_fn, l_fn, r_fn, st, sp) in enumerate(mms):
            def closure(E, out_fn=out_fn, l_fn=l_fn, r_fn=r_fn, st=st, sp=sp,
                        need=tuple(need) if i == 0 else (),
                        last=(i == nmm - 1)):
                for s, v in need:
                    E.wait_ge(sems[s], v)
                inst = E.matmul(out_fn(), l_fn(), r_fn(), start=st, stop=sp)
                if last:
                    inst.then_inc(sems["PE"], 1)
            self.q["tensor"].append(closure)
        for b in reads:
            b.r.append(tok)
        if psum is not None:
            if accum:
                psum.w.append(tok)
            else:
                psum.w = [tok]
                psum.r = []
        return tok

    def tr_group(self, in_fn_list, out_fn_list, reads, psum, identity_ap):
        toks = []
        for b in reads:
            toks += b.w
        toks += psum.w + psum.r
        need = self._collect("tensor", toks)
        sems = self.sems
        self.cnt["PE"] += 1
        tok = ("PE", self.cnt["PE"], "tensor")
        n = len(in_fn_list)
        for i in range(n):
            def closure(E, in_fn=in_fn_list[i], out_fn=out_fn_list[i],
                        need=tuple(need) if i == 0 else (), last=(i == n - 1)):
                for s, v in need:
                    E.wait_ge(sems[s], v)
                inst = E.transpose(out=out_fn(), in_=in_fn(), identity=identity_ap)
                if last:
                    inst.then_inc(sems["PE"], 1)
            self.q["tensor"].append(closure)
        for b in reads:
            b.r.append(tok)
        psum.w = [tok]
        psum.r = []
        return tok


def fresh(buf):
    nb = Buf(buf.name)
    nb.w = list(buf.w) + list(buf.r)
    nb.r = []
    return nb


def build_nc(debug=False):
    nc = bass.Bass("TRN2", target_bir_lowering=False, debug=False)

    d_x = nc.dram_tensor("x_in", [TOK, D], F32, kind="ExternalInput").ap()
    d_wsh_up = nc.dram_tensor("wsh_up", [JSH, 128, 4096], F32, kind="ExternalInput").ap()
    d_wsh_dn = nc.dram_tensor("wsh_dn", [NGRP, 128, GSZ * D], F32, kind="ExternalInput").ap()
    d_wex_up = nc.dram_tensor("wex_up", [NEXP, JEX, 128, 4096], F32, kind="ExternalInput").ap()
    d_wex_dn = nc.dram_tensor("wex_dn", [NEXP, 128, JEX * D], F32, kind="ExternalInput").ap()
    d_bt = nc.dram_tensor("bt_w", [8, 128, D], F32, kind="ExternalInput").ap()
    d_ct = nc.dram_tensor("ct_w", [128, 8 * D], F32, kind="ExternalInput").ap()
    d_rt = nc.dram_tensor("rt_w", [128, 128], F32, kind="ExternalInput").ap()
    d_dvec = nc.dram_tensor("d_vec", [1, D], F32, kind="ExternalInput").ap()
    d_tri = nc.dram_tensor("tri_c", [128, 128], BF16, kind="ExternalInput").ap()
    d_idn = nc.dram_tensor("idn_c", [128, 128], BF16, kind="ExternalInput").ap()
    d_iota8 = nc.dram_tensor("iota8_c", [128, 8], F32, kind="ExternalInput").ap()
    d_tokid = nc.dram_tensor("tokid_c", [128, 8], U32, kind="ExternalInput").ap()
    d_ones128 = nc.dram_tensor("ones128_c", [128, 1], BF16, kind="ExternalInput").ap()
    d_ones1 = nc.dram_tensor("ones1_c", [1, 128], F32, kind="ExternalInput").ap()
    d_sent = nc.dram_tensor("sent_c", [128, 32], U32, kind="ExternalInput").ap()
    d_eps = nc.dram_tensor("eps_c", [128, 1], F32, kind="ExternalInput").ap()
    d_neg1 = nc.dram_tensor("neg1_c", [128, 1], F32, kind="ExternalInput").ap()
    d_y = nc.dram_tensor("y_out", [TOK, D], F32, kind="ExternalOutput").ap()
    if debug:
        dbg = {
            "xn1": nc.dram_tensor("dbg_xn1", [TOK, D], BF16, kind="ExternalOutput").ap(),
            "rt1": nc.dram_tensor("dbg_rt1", [TOK, 16], F32, kind="ExternalOutput").ap(),
            "wv1": nc.dram_tensor("dbg_wv1", [128, 16], F32, kind="ExternalOutput").ap(),
            "sl1": nc.dram_tensor("dbg_sl1", [128, 16], U32, kind="ExternalOutput").ap(),
            "ix1": nc.dram_tensor("dbg_ix1", [128, 32], U32, kind="ExternalOutput").ap(),
            "x1": nc.dram_tensor("dbg_x1", [TOK, D], F32, kind="ExternalOutput").ap(),
            "x2": nc.dram_tensor("dbg_x2", [TOK, D], F32, kind="ExternalOutput").ap(),
            "x3": nc.dram_tensor("dbg_x3", [TOK, D], F32, kind="ExternalOutput").ap(),
            "ob1": nc.dram_tensor("dbg_ob1", [NSLOT, D], F32, kind="ExternalOutput").ap(),
        }
    d_xnd = nc.dram_tensor("xn_scr", [TOK, D], BF16, kind="Internal").ap()
    d_idx = nc.dram_tensor("idx_scr", [IDXSPACE, 1], U32, kind="Internal").ap()
    d_ob = nc.dram_tensor("ob_scr", [NSLOT, D], F32, kind="Internal").ap()
    d_cshup = nc.dram_tensor("cshup", [JSH, 128, 4096], BF16, kind="Internal").ap()
    d_cshdn = nc.dram_tensor("cshdn", [NGRP, 128, GSZ * D], BF16, kind="Internal").ap()
    d_cexdn = nc.dram_tensor("cexdn", [NEXP, 128, JEX * D], BF16, kind="Internal").ap()

    stack = ExitStack()
    with stack:
        E_ = stack.enter_context
        t_x = E_(nc.sbuf_tensor("t_x", [128, NB * D], F32))
        t_xnf = E_(nc.sbuf_tensor("t_xnf", [128, KC * TOK], BF16))
        t_h = E_(nc.sbuf_tensor("t_h", [128, GSZ * TOK], BF16))
        t_wd = E_(nc.sbuf_tensor("t_wd", [128, JEX * D], BF16))
        t_up = E_(nc.sbuf_tensor("t_up", [128, 2 * 4096], BF16))
        t_osb = E_(nc.sbuf_tensor("t_osb", [128, 2 * D], F32))
        t_xnt = E_(nc.sbuf_tensor("t_xnt", [128, D], BF16))
        t_gbs = E_(nc.sbuf_tensor("t_gbs", [128, D], BF16))
        t_rt = E_(nc.sbuf_tensor("t_rt", [128, 128], BF16))
        t_tri = E_(nc.sbuf_tensor("t_tri", [128, 128], BF16))
        t_idn = E_(nc.sbuf_tensor("t_idn", [128, 128], BF16))
        t_iota8 = E_(nc.sbuf_tensor("t_iota8", [128, 8], F32))
        t_tokid = E_(nc.sbuf_tensor("t_tokid", [128, 8], U32))
        t_ones128 = E_(nc.sbuf_tensor("t_ones128", [128, 1], BF16))
        t_ones1 = E_(nc.sbuf_tensor("t_ones1", [1, 128], F32))
        t_sent = E_(nc.sbuf_tensor("t_sent", [128, 32], U32))
        t_eps = E_(nc.sbuf_tensor("t_eps", [128, 1], F32))
        t_neg1 = E_(nc.sbuf_tensor("t_neg1", [128, 1], F32))
        t_t8 = E_(nc.sbuf_tensor("t_t8", [128, 8], F32))
        t_ix8 = E_(nc.sbuf_tensor("t_ix8", [128, 8], U32))
        t_t2A = E_(nc.sbuf_tensor("t_t2A", [128, 16], F32))
        t_e12A = E_(nc.sbuf_tensor("t_e12A", [128, 16], F32))
        t_eqA = E_(nc.sbuf_tensor("t_eqA", [128, 128], F32))
        t_sbfA = E_(nc.sbuf_tensor("t_sbfA", [128, 64], BF16))
        t_psbA = E_(nc.sbuf_tensor("t_psbA", [128, 64], F32))

        t_cwA = E_(nc.sbuf_tensor("t_cwA", [128, 16], F32))
        t_posA = E_(nc.sbuf_tensor("t_posA", [128, 16], F32))
        t_sA = E_(nc.sbuf_tensor("t_sA", [128, 32], F32))
        t_suA = E_(nc.sbuf_tensor("t_suA", [128, 16], U32))
        t_su2 = E_(nc.sbuf_tensor("t_su2", [128, 16], U32))
        t_baseA = E_(nc.sbuf_tensor("t_baseA", [1, 64], F32))
        t_slots = E_(nc.sbuf_tensor("t_slots", [128, 16], U32))
        t_wv = E_(nc.sbuf_tensor("t_wv", [128, 16], F32))
        t_idxsb = E_(nc.sbuf_tensor("t_idxsb", [128, 32], U32))
        t_bnP = [E_(nc.sbuf_tensor("t_bn%d" % i, [128, 24], F32)) for i in range(2)]
        t_bn2P = [E_(nc.sbuf_tensor("t_bn2%d" % i, [128, 2], F32)) for i in range(2)]
        t_lnP = [E_(nc.sbuf_tensor("t_ln%d" % i, [128, 4], F32)) for i in range(2)]
        t_lnm = E_(nc.sbuf_tensor("t_lnm", [128, 16], F32))
        t_sil = E_(nc.sbuf_tensor("t_sil", [128, 512], F32))

        p_y = [E_(nc.psum_tensor("p_y%d" % i, [128, 512], F32)) for i in range(4)]
        p_ga = E_(nc.psum_tensor("p_ga", [128, 512], F32))
        p_gb = E_(nc.psum_tensor("p_gb", [128, 512], F32))
        p_tr = E_(nc.psum_tensor("p_tr", [128, 1024], BF16))
        p_sm = E_(nc.psum_tensor("p_sm", [128, 512], F32))

        P = Prog(nc, stack)
        regs = {}
        def _mkregs(E):
            regs["b_tok"] = E.to_reg(TOK - 1)
            regs["b_slot"] = E.to_reg(NSLOT - 1)
            regs["b_idxsp"] = E.to_reg(IDXSPACE - 1)
        P.q["gpsimd"].append(lambda E: _mkregs(E))

        b_x = [Buf("x%d" % b) for b in range(NB)]
        b_xnf = Buf("xnf")
        b_h = [Buf("h%d" % j) for j in range(GSZ)]
        b_wd = Buf("wd")
        b_upA = [Buf("upA0"), Buf("upA1")]
        b_upB = [Buf("upB0"), Buf("upB1")]
        b_osb = [Buf("osb0"), Buf("osb1")]
        b_wd2 = Buf("wd2")
        b_xnt = Buf("xnt")
        b_gbs = [Buf("gbs0"), None]
        b_consts = Buf("consts")
        b_rt = Buf("rt")
        b_t8 = Buf("t8"); b_ix8 = Buf("ix8"); b_t2A = Buf("t2A")
        b_e12A = Buf("e12A")
        b_eqA = Buf("eqA"); b_sbfA = Buf("sbfA")
        b_psbA = Buf("psbA"); b_d12A = Buf("d12A"); b_cwA = Buf("cwA")
        b_posA = Buf("posA"); b_sA = Buf("sA"); b_suA = Buf("suA")
        b_su2 = Buf("su2"); b_baseA = Buf("baseA")
        b_slots = Buf("slots"); b_wv = Buf("wv")
        b_idxsb = Buf("idxsb")
        b_bnP = [Buf("bn%d" % i) for i in range(2)]
        b_bn2P = [Buf("bn2%d" % i) for i in range(2)]
        b_lnP = [Buf("ln%d" % i) for i in range(2)]
        b_lnm = Buf("lnm"); b_sil = Buf("sil")
        b_gbs[1] = b_xnt
        b_py = [Buf("py%d" % i) for i in range(4)]
        b_ga = Buf("ga"); b_gb = Buf("gb"); b_tr = Buf("tr")
        b_sm_lg = [Buf("sm_lg%d" % b) for b in range(NB)]
        b_sm_cnt = Buf("sm_cnt")
        b_sm_tri = Buf("sm_tri"); b_sm_bas = Buf("sm_bas")
        st_bufs = {"xnd": Buf("xnd"), "idxd": Buf("idxd"), "obd": Buf("obd")}
        cb_shup = [Buf("cshup%d" % j) for j in range(JSH)]
        cb_shdn = [Buf("cshdn%d" % g) for g in range(NGRP)]
        cb_exdn = [Buf("cexdn%d" % e) for e in range(NEXP)]
        cs_slot = [Buf("cs_s%d" % i) for i in range(4)]
        wb_slot = [Buf("wb_s%d" % i) for i in range(2)]
        castn = {"i": 0}

        def ld(dst, src, buf=None, add=False, eng="sync", sem="LIN"):
            return P.op(eng, lambda E, d=dst, s=src: E.dma_start(out=d, in_=s),
                        writes=(buf,) if (buf and not add) else (),
                        add_writes=(buf,) if (buf and add) else (),
                        sem=sem, inc=16)

        ld(t_tri[:, :], d_tri[:, :])
        ld(t_idn[:, :], d_idn[:, :])
        ld(t_iota8[:, :], d_iota8[:, :])
        ld(t_tokid[:, :], d_tokid[:, :])
        ld(t_ones128[:, :], d_ones128[:, :])
        ld(t_ones1[:, :], d_ones1[:, :])
        ld(t_sent[:, :], d_sent[:, :])
        ld(t_eps[:, :], d_eps[:, :])
        ld(t_neg1[:, :], d_neg1[:, :])
        d_xr = d_x.rearrange("(b p) d -> p b d", p=128)
        t_xr = t_x[:, :].rearrange("p (b d) -> p b d", d=D)
        tok_lo = ld(t_xr[:, 0:4, :], d_xr[:, 0:4, :])
        tok_hi = ld(t_xr[:, 4:8, :], d_xr[:, 4:8, :], eng="scalar", sem="LIN2")
        b_consts.w = [tok_lo]
        for b in range(NB):
            b_x[b].w = [tok_lo if b < 4 else tok_hi]
        P.op("gpsimd", lambda E: E.dma_start(out=t_rt[:, :], in_=d_rt[:, :]),
             writes=(b_rt,), sem="WMISC")

        def xblk(b):
            return t_x[:, b * D:(b + 1) * D]

        def ln_block(b, keep_mix=False):
            t_bn, t_bn2, t_ln = t_bnP[b % 2], t_bn2P[b % 2], t_lnP[b % 2]
            b_bn, b_bn2, b_ln = b_bnP[b % 2], b_bn2P[b % 2], b_lnP[b % 2]
            for c in range(4):
                P.op("vector", lambda E, c=c, b=b, t_bn=t_bn: E.bn_stats(
                    out=t_bn[:, 6 * c:6 * c + 6],
                    in_=xblk(b)[:, 512 * c:512 * c + 512]),
                    reads=(b_x[b],), writes=(b_bn,) if c == 0 else (),
                    add_writes=(b_bn,) if c > 0 else ())
            P.op("vector", lambda E, t_bn=t_bn, t_bn2=t_bn2: E.bn_aggr(
                out=t_bn2[:, :], in_=t_bn[:, :]),
                reads=(b_bn,), writes=(b_bn2,))
            P.op("scalar", lambda E, t_bn2=t_bn2, t_ln=t_ln: E.activation(
                out=t_ln[:, 2:3], in_=t_bn2[:, 1:2],
                func=AF.Sqrt, bias=t_eps[:, :]),
                reads=(b_bn2, b_consts), writes=(b_ln,))
            P.op("vector", lambda E, t_ln=t_ln: E.reciprocal(
                out=t_ln[:, 0:1], in_=t_ln[:, 2:3]),
                reads=(b_ln,), add_writes=(b_ln,))
            P.op("vector", lambda E, t_bn2=t_bn2, t_ln=t_ln: E.tensor_scalar(
                out=t_ln[:, 1:2], in0=t_bn2[:, 0:1], scalar1=t_ln[:, 0:1],
                scalar2=-1.0, op0=ALU.mult, op1=ALU.mult),
                reads=(b_bn2, b_ln), add_writes=(b_ln,))
            if keep_mix:
                P.op("vector", lambda E, b=b, t_ln=t_ln: E.tensor_copy(
                    out=t_lnm[:, 2 * b:2 * b + 2], in_=t_ln[:, 0:2]),
                    reads=(b_ln,), writes=(b_lnm,) if b == 0 else (),
                    add_writes=(b_lnm,) if b > 0 else ())
            P.op("scalar", lambda E, b=b, t_ln=t_ln: E.activation(
                out=t_xnt[:, :], in_=xblk(b), func=AF.Identity,
                bias=t_ln[:, 1:2], scale=t_ln[:, 0:1]),
                reads=(b_x[b], b_ln), writes=(b_xnt,))

        def xn_transpose(b, first):
            for half in range(2):
                ins = [(lambda k=k: t_xnt[:, 128 * k:128 * k + 128])
                       for k in range(8 * half, 8 * half + 8)]
                outs = [(lambda q=q: p_tr[:, 128 * q:128 * q + 128])
                        for q in range(8)]
                P.tr_group(ins, outs, (b_xnt, b_consts), b_tr, t_idn[:, :])

                def ev(E, b=b, half=half):
                    dst = t_xnf[:, :].rearrange("p (k t) -> p k t", k=KC)[
                        :, 8 * half:8 * half + 8, 128 * b:128 * b + 128]
                    src = p_tr[:, :].rearrange("p (k t) -> p k t", k=8)
                    return E.tensor_copy(out=dst, in_=src)
                fw = first and half == 0
                P.op("vector", ev, reads=(b_tr,),
                     writes=(b_xnf,) if fw else (),
                     add_writes=(b_xnf,) if not fw else ())

        def routing_logits(b):
            mms = [(lambda b=b: p_sm[:, 8 * b:8 * b + 8],
                    lambda k=k, b=b: t_xnf[:, TOK * k + 128 * b: TOK * k + 128 * b + 128],
                    lambda k=k: t_rt[:, 8 * k:8 * k + 8],
                    k == 0, k == KC - 1) for k in range(KC)]
            P.mm_group(mms, reads=(b_xnf, b_rt), psum=b_sm_lg[b])
            P.op("vector", lambda E, b=b: E.max(
                out=t_t8[:, :], in_=p_sm[:, 8 * b:8 * b + 8]),
                reads=(b_sm_lg[b],), writes=(b_t8,))
            P.op("vector", lambda E, b=b: E.max_index(
                out=t_ix8[:, :], in_max=t_t8[:, :],
                in_values=p_sm[:, 8 * b:8 * b + 8]),
                reads=(b_sm_lg[b], b_t8), writes=(b_ix8,))
            P.op("vector", lambda E, b=b: E.tensor_copy(
                out=t_t2A[:, 2 * b:2 * b + 2], in_=t_t8[:, 0:2]),
                reads=(b_t8,),
                writes=(b_t2A,) if b == 0 else (), add_writes=(b_t2A,) if b else ())
            P.op("vector", lambda E, b=b: E.tensor_copy(
                out=t_e12A[:, 2 * b:2 * b + 2], in_=t_ix8[:, 0:2]),
                reads=(b_ix8,),
                writes=(b_e12A,) if b == 0 else (),
                add_writes=(b_e12A,) if b else ())

        def routing_batched(b_idxd):
            def r3(t):
                return t[:, :].rearrange("p (b e) -> p b e", e=8)

            def r2(t):
                return t[:, :].rearrange("p (b k) -> p b k", k=2)

            P.op("vector", lambda E: E.tensor_tensor(
                out=t_sA[:, 0:8], in0=r2(t_t2A)[:, :, 0:1],
                in1=r2(t_t2A)[:, :, 1:2], op=ALU.subtract),
                reads=(b_t2A,), writes=(b_sA,))
            P.op("scalar", lambda E: E.activation(
                out=r2(t_cwA)[:, :, 0:1], in_=t_sA[:, 0:8], func=AF.Sigmoid),
                reads=(b_sA,), writes=(b_cwA,))
            P.op("scalar", lambda E: E.activation(
                out=r2(t_cwA)[:, :, 1:2], in_=t_sA[:, 0:8], func=AF.Sigmoid,
                scale=t_neg1[:, :]),
                reads=(b_sA, b_consts), add_writes=(b_cwA,))
            for b in range(NB):
                for kk in range(2):
                    P.op("vector", lambda E, b=b, kk=kk: E.tensor_scalar(
                        out=t_eqA[:, 64 * kk + 8 * b:64 * kk + 8 * b + 8],
                        in0=t_iota8[:, :], scalar1=t_e12A[:, 2 * b + kk:2 * b + kk + 1],
                        scalar2=None, op0=ALU.is_equal),
                        reads=(b_consts, b_e12A),
                        writes=(b_eqA,) if (b == 0 and kk == 0) else (),
                        add_writes=(b_eqA,) if (b or kk) else ())
            P.op("vector", lambda E: E.tensor_tensor(
                out=t_psbA[:, :], in0=t_eqA[:, 0:64], in1=t_eqA[:, 64:128],
                op=ALU.add),
                reads=(b_eqA,), writes=(b_psbA,))
            P.op("vector", lambda E: E.tensor_copy(out=t_sbfA[:, :], in_=t_psbA[:, :]),
                 reads=(b_psbA,), writes=(b_sbfA,))
            def routing_tail1():
                P.mm_group([(lambda: p_sm[0:1, 256:320], lambda: t_ones128[:, :],
                             lambda: t_sbfA[:, :], True, True)],
                           reads=(b_consts, b_sbfA), psum=b_sm_cnt)
                P.mm_group([(lambda: p_sm[:, 128:192], lambda: t_tri[:, :],
                             lambda: t_sbfA[:, :], True, True)],
                           reads=(b_consts, b_sbfA), psum=b_sm_tri)
                P.op("vector", lambda E: E.memset(t_baseA[:, 0:8], 0.0),
                     writes=(b_baseA,))
                for b in range(1, NB):
                    P.op("vector", lambda E, b=b: E.tensor_tensor(
                        out=t_baseA[:, 8 * b:8 * b + 8],
                        in0=t_baseA[:, 8 * b - 8:8 * b],
                        in1=p_sm[0:1, 256 + 8 * b - 8:256 + 8 * b], op=ALU.add),
                        reads=(b_sm_cnt, b_baseA), add_writes=(b_baseA,))

            def routing_tail2():
              P.mm_group([(lambda: p_sm[:, 192:256], lambda: t_ones1[:, :],
                           lambda: t_baseA[:, :], True, True)],
                         reads=(b_consts, b_baseA), psum=b_sm_bas)
              P.op("vector", lambda E: E.tensor_copy(out=t_psbA[:, :],
                                                     in_=p_sm[:, 128:192]),
                   reads=(b_sm_tri,), writes=(b_psbA,))
              P.op("vector", lambda E: E.scalar_tensor_tensor(
                  out=t_psbA[:, :], in0=p_sm[:, 192:256], scalar=1.0,
                  in1=t_psbA[:, :], op0=ALU.mult, op1=ALU.add),
                  reads=(b_sm_bas, b_psbA), add_writes=(b_psbA,))
              for kk in range(2):
                  P.op("vector", lambda E, kk=kk: E.tensor_tensor(
                      out=t_eqA[:, 64 * kk:64 * kk + 64], in0=t_psbA[:, :],
                      in1=t_eqA[:, 64 * kk:64 * kk + 64], op=ALU.mult),
                      reads=(b_psbA,), writes=(b_eqA,))
                  P.op("vector", lambda E, kk=kk: E.reduce_sum(
                      out=r2(t_posA)[:, :, kk:kk + 1],
                      in_=t_eqA[:, 64 * kk:64 * kk + 64].rearrange(
                          "p (b e) -> p b e", e=8), axis=AX),
                      reads=(b_eqA,), writes=(b_posA,) if kk == 0 else (),
                      add_writes=(b_posA,) if kk else ())
              P.op("vector", lambda E: E.tensor_single_scalar(
                  out=t_sA[:, 0:16], in_=t_posA[:, :], scalar=float(CL),
                  op=ALU.is_ge),
                  reads=(b_posA, b_cwA), writes=(b_sA,))
              P.op("vector", lambda E: E.tensor_scalar(
                  out=t_sA[:, 16:32], in0=t_sA[:, 0:16], scalar1=-0.5, scalar2=0.5,
                  op0=ALU.mult, op1=ALU.add),
                  reads=(b_sA,), add_writes=(b_sA,))
              P.op("vector", lambda E: E.tensor_tensor(
                  out=t_wv[:, :], in0=t_sA[:, 16:32], in1=t_cwA[:, :], op=ALU.mult),
                  reads=(b_sA, b_cwA), writes=(b_wv,))
              P.op("vector", lambda E: E.scalar_tensor_tensor(
                  out=t_sA[:, 16:32], in0=t_e12A[:, :], scalar=float(CL),
                  in1=t_posA[:, :], op0=ALU.mult, op1=ALU.add),
                  reads=(b_e12A, b_posA, b_wv), add_writes=(b_sA,))
              P.op("vector", lambda E: E.scalar_tensor_tensor(
                  out=t_cwA[:, :], in0=t_sA[:, 0:16], scalar=BIGF,
                  in1=t_sA[:, 16:32], op0=ALU.mult, op1=ALU.add),
                  reads=(b_sA,), writes=(b_cwA,))
              P.op("vector", lambda E: E.tensor_copy(
                  out=t_slots[:, :], in_=t_cwA[:, :]),
                  reads=(b_cwA,), writes=(b_slots,))
              P.op("vector", lambda E: E.tensor_copy(out=t_su2[:, :],
                                                     in_=t_sA[:, 16:32]),
                   reads=(b_sA,), writes=(b_su2,))
              P.op("vector", lambda E: E.tensor_scalar(
                  out=t_suA[:, :], in0=t_su2[:, :], scalar1=7, scalar2=None,
                  op0=ALU.logical_shift_right),
                  reads=(b_su2,), writes=(b_suA,))
              P.op("vector", lambda E: E.tensor_scalar(
                  out=t_su2[:, :], in0=t_su2[:, :], scalar1=127, scalar2=None,
                  op0=ALU.bitwise_and),
                  reads=(b_suA,), writes=(b_su2,))
              P.op("vector", lambda E: E.tensor_scalar(
                  out=t_su2[:, :], in0=t_su2[:, :], scalar1=5, scalar2=None,
                  op0=ALU.logical_shift_left),
                  writes=(b_su2,))
              P.op("vector", lambda E: E.tensor_tensor(
                  out=t_suA[:, :], in0=t_suA[:, :], in1=t_su2[:, :],
                  op=ALU.bitwise_or),
                  reads=(b_su2,), add_writes=(b_suA,))
              P.op("vector", lambda E: E.tensor_copy(out=t_su2[:, :],
                                                     in_=t_sA[:, 0:16]),
                   reads=(b_sA, b_suA), writes=(b_su2,))
              P.op("vector", lambda E: E.tensor_scalar(
                  out=t_su2[:, :], in0=t_su2[:, :], scalar1=30, scalar2=None,
                  op0=ALU.logical_shift_left),
                  writes=(b_su2,))
              P.op("vector", lambda E: E.tensor_tensor(
                  out=t_suA[:, :], in0=t_suA[:, :], in1=t_su2[:, :],
                  op=ALU.bitwise_or),
                  reads=(b_su2,), add_writes=(b_suA,))
            def emit_scatters():
                for b in range(NB):
                    for kk in range(2):
                        P.op("gpsimd", lambda E, b=b, kk=kk: E.indirect_dma_start(
                            out=d_idx[:, :],
                            out_offset=IndirectOffsetOnAxis(
                                t_suA[:, 2 * b + kk:2 * b + kk + 1], axis=0),
                            in_=t_tokid[:, b:b + 1], in_offset=None,
                            bounds_check=regs["b_idxsp"], oob_is_err=False),
                            reads=(b_suA, b_consts, b_idxd), add_writes=(b_idxd,),
                            sem="IND")
            return routing_tail1, routing_tail2, emit_scatters

        def silu_mul(dst_ap, dst_buf, first_dst, n=512):
            P.op("scalar", lambda E, n=n: E.activation(out=t_sil[:, 0:n], in_=p_ga[:, 0:n],
                                                       func=AF.Silu),
                 reads=(b_ga,), writes=(b_sil,))
            P.op("vector", lambda E, dst_ap=dst_ap, n=n: E.tensor_tensor(
                out=dst_ap, in0=t_sil[:, 0:n], in1=p_gb[:, 0:n], op=ALU.mult),
                reads=(b_sil, b_gb),
                writes=(dst_buf,) if first_dst else (),
                add_writes=(dst_buf,) if not first_dst else ())

        def dbg_store(dst, src_ap, rbuf, slow=False):
            P.op("sync", lambda E, d=dst, sp=src_ap: E.dma_start(
                out=d, in_=sp, allow_slow_non_contiguous=True) if slow else
                E.dma_start(out=d, in_=sp),
                reads=(rbuf,), sem="OUT")

        def emit_ffn(call, pre_block=None):
            b_xnd = st_bufs["xnd"] = fresh(st_bufs["xnd"])
            b_idxd = st_bufs["idxd"] = fresh(st_bufs["idxd"])
            b_obd = st_bufs["obd"] = fresh(st_bufs["obd"])
            if call == 0:
                QW = GSZ * D // 4
                pend = ([("shdn", g, q) for g in range(NGRP) for q in range(4)]
                        + [("exdn", e, q) for e in range(NEXP) for q in range(4)])

                def pop_pieces(k):
                    for _ in range(k):
                        if not pend:
                            return
                        p = pend.pop(0)
                        sl = castn["i"] % 4
                        castn["i"] += 1
                        sem = "CS%d" % sl
                        if p[0] == "shdn":
                            g, q = p[1], p[2]
                            P.op("gpsimd", lambda E, g=g, q=q: E.dma_start(
                                out=d_cshdn[g][:, QW * q:QW * q + QW],
                                in_=d_wsh_dn[g][:, QW * q:QW * q + QW]),
                                writes=(cs_slot[sl],),
                                add_writes=(cb_shdn[g],), sem=sem)
                        else:
                            e, q = p[1], p[2]
                            P.op("gpsimd", lambda E, e=e, q=q: E.dma_start(
                                out=d_cexdn[e][:, QW * q:QW * q + QW],
                                in_=d_wex_dn[e][:, QW * q:QW * q + QW]),
                                writes=(cs_slot[sl],),
                                add_writes=(cb_exdn[e],), sem=sem)

                for j in range(2):
                    P.op("gpsimd", lambda E, j=j: E.dma_start(
                        out=t_up[:, 4096 * (j % 2):4096 * (j % 2) + 4096],
                        in_=d_wsh_up[j]),
                        writes=(b_upA[j % 2], b_upB[j % 2]),
                        sem="W%d" % (j % 2))
                pop_pieces(16)
            else:
                pend = []

                def pop_pieces(k):
                    pass

                for j in range(2):
                    P.op("sync" if j % 2 == 0 else "scalar",
                         lambda E, j=j: E.dma_start(
                             out=t_up[:, 4096 * (j % 2):4096 * (j % 2) + 4096],
                             in_=d_cshup[j]),
                         reads=(cb_shup[j],),
                         writes=(b_upA[j % 2], b_upB[j % 2]),
                         sem="W%d" % (j % 2), inc=16)
            P.op("sync", lambda E: E.dma_start(
                out=d_idx.rearrange("(p s) o -> p (s o)", p=128), in_=t_sent[:, :]),
                reads=(b_consts,), writes=(b_idxd,), sem="IDX")
            for b in range(NB):
                if pre_block is not None:
                    pre_block(b)
                ln_block(b)
                P.op("sync", lambda E, b=b: E.dma_start(
                    out=d_xnd[128 * b:128 * b + 128, :], in_=t_xnt[:, :]),
                    reads=(b_xnt,), writes=(b_xnd,) if b == 0 else (),
                    add_writes=(b_xnd,) if b else (), sem="XN")
                if debug and call == 0:
                    dbg_store(dbg["xn1"][128 * b:128 * b + 128, :], t_xnt[:, :], b_xnt)
                xn_transpose(b, first=(b == 0))
                routing_logits(b)
            routing_tail1, routing_tail2, emit_scatters = routing_batched(b_idxd)
            def emit_scat_idx():
                emit_scatters()
                P.op("sync", lambda E: E.dma_start(
                    out=t_idxsb[:, :],
                    in_=d_idx.rearrange("(p s) o -> p (s o)", p=128)),
                    reads=(b_idxd,), writes=(b_idxsb,), sem="IDX")
                if debug and call == 0:
                    dbg_store(dbg["ix1"][:, :], t_idxsb[:, :], b_idxsb)

            disp = {"steps": [], "b_buf": None}

            def run_step():
                if not disp["steps"]:
                    return
                st = disp["steps"].pop(0)
                b_buf = disp["b_buf"]
                if st[0] == "g":
                    _, e_t, sb = st
                    col = (e_t * CL + 128 * sb) // 128
                    gp = sb % 2
                    P.op("gpsimd", lambda E, col=col, gp=gp: E.indirect_dma_start(
                        out=(t_gbs[:, :] if gp == 0 else t_xnt[:, :]), out_offset=None,
                        in_=d_xnd[:, :],
                        in_offset=IndirectOffsetOnAxis(t_idxsb[:, col:col + 1], axis=0),
                        bounds_check=regs["b_tok"], oob_is_err=False),
                        reads=(b_idxsb, b_xnd), writes=(b_gbs[gp],), sem="IND" if gp == 0 else "G0")
                else:
                    _, e_t, sb, half = st
                    gp = sb % 2
                    src_t = t_gbs if gp == 0 else t_xnt
                    ins = [(lambda k=k, src_t=src_t: src_t[:, 128 * k:128 * k + 128])
                           for k in range(8 * half, 8 * half + 8)]
                    outs = [(lambda q=q: p_tr[:, 128 * q:128 * q + 128])
                            for q in range(8)]
                    P.tr_group(ins, outs, (b_gbs[gp], b_consts), b_tr, t_idn[:, :])

                    def ev(E, e_t=e_t, sb=sb, half=half):
                        dst = t_xnf[:, 0:2 * KC * CL].rearrange(
                            "p (pp k t) -> p pp k t", pp=2, k=KC)[
                            :, e_t % 2, 8 * half:8 * half + 8, 128 * sb:128 * sb + 128]
                        src = p_tr[:, :].rearrange("p (k t) -> p k t", k=8)
                        return E.tensor_copy(out=dst, in_=src)
                    fw = (e_t == 0 and sb == 0 and half == 0)
                    P.op("vector", ev, reads=(b_tr,),
                         writes=(b_buf,) if fw else (),
                         add_writes=(b_buf,) if not fw else ())

            def queue_dispatch(e_t):
                for sb in range(NSB):
                    disp["steps"].append(("g", e_t, sb))
                    disp["steps"].append(("t", e_t, sb, 0))
                    disp["steps"].append(("t", e_t, sb, 1))

            for g in range(NGRP):
                for jj in range(GSZ):
                    j = g * GSZ + jj
                    par = j % 2
                    if not (g == 0 and jj < 2):
                        if call == 0:
                            P.op("gpsimd", lambda E, j=j, par=par: E.dma_start(
                                out=t_up[:, 4096 * par:4096 * par + 4096],
                                in_=d_wsh_up[j]),
                                writes=(b_upA[par], b_upB[par]),
                                sem="W%d" % par)
                        else:
                            P.op("sync" if par == 0 else "scalar",
                                 lambda E, j=j, par=par: E.dma_start(
                                     out=t_up[:, 4096 * par:4096 * par + 4096],
                                     in_=d_cshup[j]),
                                 reads=(cb_shup[j],),
                                 writes=(b_upA[par], b_upB[par]),
                                 sem="W%d" % par, inc=16)
                    if g == 0 and jj == 1:
                        routing_tail1()
                        routing_tail2()
                    if g == 0 and jj == 10:
                        emit_scat_idx()
                    if jj == 1:
                        HW = GSZ * D // 2
                        P.op("scalar", lambda E, g=g, HW=HW: E.dma_start(
                            out=t_wd[:, :HW], in_=d_cshdn[g][:, :HW]),
                            reads=(cb_shdn[g],), writes=(b_wd,), sem="WDV", inc=16)
                        P.op("sync", lambda E, g=g, HW=HW: E.dma_start(
                            out=t_wd[:, HW:GSZ * D], in_=d_cshdn[g][:, HW:]),
                            reads=(cb_shdn[g],), writes=(b_wd2,), sem="WD", inc=16)
                    for tt in range(2):
                        mms = [(lambda: p_ga[:, :],
                                lambda k=k, par=par: t_up[:, 4096 * par + 128 * k:
                                                          4096 * par + 128 * k + 128],
                                lambda k=k, tt=tt: t_xnf[:, TOK * k + 512 * tt:
                                                         TOK * k + 512 * tt + 512],
                                k == 0, k == KC - 1) for k in range(KC)]
                        P.mm_group(mms, reads=(b_upA[par], b_xnf), psum=b_ga)
                        mms = [(lambda: p_gb[:, :],
                                lambda k=k, par=par: t_up[:, 4096 * par + 2048 + 128 * k:
                                                          4096 * par + 2048 + 128 * k + 128],
                                lambda k=k, tt=tt: t_xnf[:, TOK * k + 512 * tt:
                                                         TOK * k + 512 * tt + 512],
                                k == 0, k == KC - 1) for k in range(KC)]
                        P.mm_group(mms, reads=(b_upB[par], b_xnf), psum=b_gb)
                        silu_mul(t_h[:, TOK * jj + 512 * tt: TOK * jj + 512 * tt + 512],
                                 b_h[jj], first_dst=(tt == 0))
                        if call == 0 and tt == 0:
                            P.op("scalar", lambda E, j=j, par=par: E.dma_start(
                                out=d_cshup[j],
                                in_=t_up[:, 4096 * par:4096 * par + 4096]),
                                reads=(b_upA[par], b_upB[par]),
                                writes=(wb_slot[par],),
                                add_writes=(cb_shup[j],),
                                sem="WB%d" % par, inc=16)
                if g == NGRP - 1:
                    disp["b_buf"] = fresh(b_xnf)
                    queue_dispatch(0)
                for b in range(NB):
                    pop_pieces(1)
                    for q in range(4):
                        mms = [(lambda q=q: p_y[q][:, :],
                                lambda jj=jj, b=b: t_h[:, TOK * jj + 128 * b:
                                                       TOK * jj + 128 * b + 128],
                                lambda jj=jj, q=q: t_wd[:, D * jj + 512 * q:
                                                        D * jj + 512 * q + 512],
                                jj == 0, jj == GSZ - 1) for jj in range(GSZ)]
                        P.mm_group(mms,
                                   reads=[b_h[jj] for jj in range(GSZ)] + [b_wd, b_wd2],
                                   psum=b_py[q])
                    for q in range(4):
                        P.op("vector", lambda E, b=b, q=q: E.scalar_tensor_tensor(
                            out=xblk(b)[:, 512 * q:512 * q + 512],
                            in0=p_y[q][:, :], scalar=0.5,
                            in1=xblk(b)[:, 512 * q:512 * q + 512],
                            op0=ALU.mult, op1=ALU.add),
                            reads=(b_py[q], b_x[b]), add_writes=(b_x[b],))
                    if g == NGRP - 1:
                        run_step()
                        run_step()

            if debug and call == 0:
                dbg_store(dbg["wv1"][:, :], t_wv[:, :], b_wv)
                dbg_store(dbg["sl1"][:, :], t_slots[:, :], b_slots)
                for b in range(NB):
                    dbg_store(dbg["x1"][128 * b:128 * b + 128, :], xblk(b), b_x[b])

            b_buf = disp["b_buf"]
            for e in range(NEXP):
                par = e % 2
                if e + 1 < NEXP:
                    queue_dispatch(e + 1)
                for j in range(JEX):
                    wpar = j % 2
                    P.op("gpsimd", lambda E, e=e, j=j, wpar=wpar: E.dma_start(
                        out=t_up[:, 4096 * wpar:4096 * wpar + 2048],
                        in_=d_wex_up[e, j][:, 0:2048]),
                        writes=(b_upA[wpar],), sem="W%d" % wpar)
                    P.op("gpsimd", lambda E, e=e, j=j, wpar=wpar: E.dma_start(
                        out=t_up[:, 4096 * wpar + 2048:4096 * wpar + 4096],
                        in_=d_wex_up[e, j][:, 2048:4096]),
                        writes=(b_upB[wpar],), sem="U%d" % wpar)
                    if j == 1:
                        HW = JEX * D // 2
                        P.op("scalar", lambda E, e=e, HW=HW: E.dma_start(
                            out=t_wd[:, :HW], in_=d_cexdn[e][:, :HW]),
                            reads=(cb_exdn[e],), writes=(b_wd,), sem="WDV", inc=16)
                        P.op("sync", lambda E, e=e, HW=HW: E.dma_start(
                            out=t_wd[:, HW:], in_=d_cexdn[e][:, HW:]),
                            reads=(cb_exdn[e],), writes=(b_wd2,), sem="WD", inc=16)
                    mms = [(lambda: p_ga[:, 0:CL],
                            lambda k=k, wpar=wpar: t_up[:, 4096 * wpar + 128 * k:
                                                        4096 * wpar + 128 * k + 128],
                            lambda k=k, par=par: t_xnf[:, KC * CL * par + CL * k:
                                                       KC * CL * par + CL * k + CL],
                            k == 0, k == KC - 1) for k in range(KC)]
                    P.mm_group(mms, reads=(b_upA[wpar], b_buf), psum=b_ga)
                    mms = [(lambda: p_gb[:, 0:CL],
                            lambda k=k, wpar=wpar: t_up[:, 4096 * wpar + 2048 + 128 * k:
                                                        4096 * wpar + 2048 + 128 * k + 128],
                            lambda k=k, par=par: t_xnf[:, KC * CL * par + CL * k:
                                                       KC * CL * par + CL * k + CL],
                            k == 0, k == KC - 1) for k in range(KC)]
                    P.mm_group(mms, reads=(b_upB[wpar], b_buf), psum=b_gb)
                    silu_mul(t_h[:, TOK * j: TOK * j + CL], b_h[j], first_dst=True, n=CL)
                    run_step()
                for sb in range(NSB):
                    for q in range(4):
                        mms = [(lambda q=q: p_y[q][:, :],
                                lambda j=j, sb=sb: t_h[:, TOK * j + 128 * sb:
                                                       TOK * j + 128 * sb + 128],
                                lambda j=j, q=q: t_wd[:, D * j + 512 * q:
                                                      D * j + 512 * q + 512],
                                j == 0, j == JEX - 1) for j in range(JEX)]
                        P.mm_group(mms,
                                   reads=[b_h[j] for j in range(JEX)] + [b_wd, b_wd2],
                                   psum=b_py[q])
                    opar = sb % 2
                    for q in range(4):
                        P.op("vector", lambda E, opar=opar, q=q: E.tensor_copy(
                            out=t_osb[:, D * opar + 512 * q: D * opar + 512 * q + 512],
                            in_=p_y[q][:, :]),
                            reads=(b_py[q],),
                            writes=(b_osb[opar],) if q == 0 else (),
                            add_writes=(b_osb[opar],) if q else ())
                    first_store = (e == 0 and sb == 0)
                    P.op("sync",
                         lambda E, e=e, sb=sb, opar=opar: E.dma_start(
                        out=d_ob[e * CL + 128 * sb: e * CL + 128 * sb + 128, :],
                        in_=t_osb[:, D * opar: D * opar + D]),
                        reads=(b_osb[opar],),
                        writes=(b_obd,) if first_store else (),
                        add_writes=(b_obd,) if not first_store else (),
                        sem="OB%d" % opar, inc=16)
            b_xnf.w = list(b_buf.w)
            b_xnf.r = list(b_buf.r)
            if debug and call == 0:
                P.op("sync", lambda E: E.dma_start(out=dbg["ob1"][:, :], in_=d_ob[:, :]),
                     reads=(b_obd,), sem="OUT")

            def combine_block(b):
                for kk in range(2):
                    P.op("gpsimd", lambda E, b=b, kk=kk: E.indirect_dma_start(
                        out=t_osb[:, D * kk: D * kk + D], out_offset=None,
                        in_=d_ob[:, :],
                        in_offset=IndirectOffsetOnAxis(
                            t_slots[:, 2 * b + kk: 2 * b + kk + 1], axis=0),
                        bounds_check=regs["b_slot"], oob_is_err=False),
                        reads=(b_slots, b_obd), writes=(b_osb[kk],), sem="G%d" % kk)
                for kk in range(2):
                    P.op("vector", lambda E, b=b, kk=kk: E.scalar_tensor_tensor(
                        out=xblk(b), in0=t_osb[:, D * kk: D * kk + D],
                        scalar=t_wv[:, 2 * b + kk: 2 * b + kk + 1],
                        in1=xblk(b), op0=ALU.mult, op1=ALU.add),
                        reads=(b_osb[kk], b_wv, b_x[b]), add_writes=(b_x[b],))
            return combine_block

        def emit_mixer(pre_block):
            for b in range(NB):
                pre_block(b)
                ln_block(b, keep_mix=True)
                xn_transpose(b, first=(b == 0))
            P.op("sync", lambda E: E.dma_start(out=t_osb[0:1, 0:D], in_=d_dvec[:, :]),
                 writes=(b_osb[0],), sem="DLD")
            for q in range(4):
                P.mm_group([(lambda q=q: p_y[q][:, :], lambda: t_ones1[:, :],
                             lambda q=q: t_osb[0:1, 512 * q:512 * q + 512], True, True)],
                           reads=(b_consts, b_osb[0]), psum=b_py[q])
            for q in range(4):
                P.op("vector", lambda E, q=q: E.tensor_copy(
                    out=t_osb[:, D + 512 * q: D + 512 * q + 512], in_=p_y[q][:, :]),
                    reads=(b_py[q],),
                    writes=(b_osb[1],) if q == 0 else (),
                    add_writes=(b_osb[1],) if q else ())
            def bt_load(p):
                P.op("gpsimd", lambda E, p=p: E.dma_start(
                    out=t_up[:, 4096 * (p % 2):4096 * (p % 2) + 4096].rearrange(
                        "p (n d) -> p n d", n=2),
                    in_=d_bt[2 * p:2 * p + 2].rearrange("n p d -> p n d")),
                    writes=(b_upA[p % 2], b_upB[p % 2]), sem="W%d" % (p % 2))
            bt_load(0)
            bt_load(1)
            for n in range(8):
                wpar = (n // 2) % 2
                for tt in range(2):
                    rix = (2 * n + tt) % 4
                    pp, bpp = p_y[rix], b_py[rix]
                    mms = [(lambda pp=pp: pp[:, :],
                            lambda k=k, n=n, wpar=wpar: t_up[
                                :, 4096 * wpar + 2048 * (n % 2) + 128 * k:
                                4096 * wpar + 2048 * (n % 2) + 128 * k + 128],
                            lambda k=k, tt=tt: t_xnf[:, TOK * k + 512 * tt:
                                                     TOK * k + 512 * tt + 512],
                            k == 0, k == KC - 1) for k in range(KC)]
                    P.mm_group(mms, reads=((b_upA[wpar] if n % 2 == 0 else
                                            b_upB[wpar]), b_xnf), psum=bpp)
                    P.op("scalar", lambda E, n=n, tt=tt, pp=pp: E.activation(
                        out=t_h[:, TOK * n + 512 * tt: TOK * n + 512 * tt + 512],
                        in_=pp[:, :], func=AF.Tanh),
                        reads=(bpp,),
                        writes=(b_h[n],) if tt == 0 else (),
                        add_writes=(b_h[n],) if tt else ())
                if n % 2 == 1 and n // 2 + 2 < 4:
                    bt_load(n // 2 + 2)
            P.op("gpsimd", lambda E: E.dma_start(out=t_wd[:, :8 * D], in_=d_ct[:, :]),
                 writes=(b_wd, b_wd2), sem="WD")
            for b in range(NB):
                for q in range(4):
                    mms = [(lambda q=q: p_y[q][:, :],
                            lambda n=n, b=b: t_h[:, TOK * n + 128 * b:
                                                 TOK * n + 128 * b + 128],
                            lambda n=n, q=q: t_wd[:, D * n + 512 * q:
                                                  D * n + 512 * q + 512],
                            n == 0, n == 7) for n in range(8)]
                    P.mm_group(mms, reads=[b_h[n] for n in range(8)] + [b_wd, b_wd2],
                               psum=b_py[q])
                P.op("vector", lambda E, b=b: E.tensor_tensor(
                    out=t_osb[:, 0:D], in0=xblk(b), in1=t_osb[:, D:2 * D],
                    op=ALU.mult),
                    reads=(b_x[b], b_osb[1]), writes=(b_osb[0],))
                P.op("vector", lambda E, b=b: E.tensor_scalar(
                    out=t_osb[:, 0:D], in0=t_osb[:, 0:D],
                    scalar1=t_lnm[:, 2 * b:2 * b + 1], scalar2=None, op0=ALU.mult),
                    reads=(b_lnm, b_osb[0]), add_writes=(b_osb[0],))
                P.op("vector", lambda E, b=b: E.scalar_tensor_tensor(
                    out=t_osb[:, 0:D], in0=t_osb[:, D:2 * D],
                    scalar=t_lnm[:, 2 * b + 1:2 * b + 2], in1=t_osb[:, 0:D],
                    op0=ALU.mult, op1=ALU.add),
                    reads=(b_lnm, b_osb[1], b_osb[0]), add_writes=(b_osb[0],))
                P.op("vector", lambda E, b=b: E.tensor_tensor(
                    out=xblk(b), in0=xblk(b), in1=t_osb[:, 0:D], op=ALU.add),
                    reads=(b_osb[0], b_x[b]), add_writes=(b_x[b],))
                for q in range(4):
                    P.op("vector", lambda E, b=b, q=q: E.scalar_tensor_tensor(
                        out=xblk(b)[:, 512 * q:512 * q + 512],
                        in0=p_y[q][:, :], scalar=1.0,
                        in1=xblk(b)[:, 512 * q:512 * q + 512],
                        op0=ALU.mult, op1=ALU.add),
                        reads=(b_py[q], b_x[b]), add_writes=(b_x[b],))

        comb1 = emit_ffn(0)
        if debug:
            for b in range(NB):
                comb1(b)
            comb1 = lambda b: None
            for b in range(NB):
                dbg_store(dbg["x2"][128 * b:128 * b + 128, :], xblk(b), b_x[b])
        emit_mixer(comb1)
        if debug:
            for b in range(NB):
                dbg_store(dbg["x3"][128 * b:128 * b + 128, :], xblk(b), b_x[b])
        comb2 = emit_ffn(1)

        for b in range(NB):
            comb2(b)
            P.op("sync" if b % 2 == 0 else "scalar",
                 lambda E, b=b: E.dma_start(
                     out=d_y[128 * b:128 * b + 128, :], in_=xblk(b)),
                 reads=(b_x[b],), sem="OUT" if b % 2 == 0 else "OUT2", inc=16)
        fin = P.cnt["OUT"]
        fin2 = P.cnt["OUT2"]
        sems = P.sems
        P.q["sync"].append(lambda E, v=fin: E.wait_ge(sems["OUT"], v))
        P.q["sync"].append(lambda E, v=fin2: E.wait_ge(sems["OUT2"], v))

        with nc.Block() as block:
            @block.sync
            def _(E):
                for f in P.q["sync"]:
                    f(E)

            @block.gpsimd
            def _(E):
                for f in P.q["gpsimd"]:
                    f(E)

            @block.tensor
            def _(E):
                for f in P.q["tensor"]:
                    f(E)

            @block.vector
            def _(E):
                for f in P.q["vector"]:
                    f(E)

            @block.scalar
            def _(E):
                for f in P.q["scalar"]:
                    f(E)

    return nc


def _prep_weights(L0_ffn_shared_w_up_gate, L0_ffn_shared_w_down,
                  L0_ffn_w_up_gate, L0_ffn_w_down,
                  L0_attn_B, L0_attn_C, L0_ffn_router, L0_attn_D):
    f = np.float32

    def colblk(W, j):
        return np.ascontiguousarray(
            W[:, 128 * j:128 * j + 128].reshape(KC, 128, 128)
            .transpose(1, 0, 2).reshape(128, 2048))

    Wsh = np.asarray(L0_ffn_shared_w_up_gate, f)
    J = 5632
    gate, up = Wsh[:, :J], Wsh[:, J:]
    wsh_up = np.stack([np.concatenate([colblk(gate, j), colblk(up, j)], axis=1)
                       for j in range(JSH)])

    Wd = np.asarray(L0_ffn_shared_w_down, f)
    wsh_dn = np.ascontiguousarray(
        Wd.reshape(NGRP, GSZ, 128, D).transpose(0, 2, 1, 3).reshape(NGRP, 128, GSZ * D))

    We = np.asarray(L0_ffn_w_up_gate, f)
    I = 1408
    wex_up = np.empty((NEXP, JEX, 128, 4096), f)
    for e in range(NEXP):
        ge, ue = We[e][:, :I], We[e][:, I:]
        for j in range(JEX):
            wex_up[e, j, :, :2048] = colblk(ge, j)
            wex_up[e, j, :, 2048:] = colblk(ue, j)

    Wed = np.asarray(L0_ffn_w_down, f)
    wex_dn = np.ascontiguousarray(
        Wed.reshape(NEXP, JEX, 128, D).transpose(0, 2, 1, 3).reshape(NEXP, 128, JEX * D))

    Bt = np.ascontiguousarray(np.asarray(L0_attn_B, f).T)
    bt_w = np.stack([colblk(Bt, n) for n in range(8)])

    Ct = np.ascontiguousarray(np.asarray(L0_attn_C, f).T)
    ct_w = np.ascontiguousarray(
        Ct.reshape(8, 128, D).transpose(1, 0, 2).reshape(128, 8 * D))

    rt_w = np.ascontiguousarray(
        np.asarray(L0_ffn_router, f).reshape(KC, 128, 8).transpose(1, 0, 2).reshape(128, 128))

    d_vec = np.ascontiguousarray(np.asarray(L0_attn_D, f).reshape(1, D))
    return dict(wsh_up=wsh_up, wsh_dn=wsh_dn, wex_up=wex_up, wex_dn=wex_dn,
                bt_w=bt_w, ct_w=ct_w, rt_w=rt_w, d_vec=d_vec)


def _consts():
    bf = ml_dtypes.bfloat16
    tri = np.triu(np.ones((128, 128), np.float32), 1).astype(bf)
    idn = np.eye(128, dtype=np.float32).astype(bf)
    iota8 = np.tile(np.arange(8, dtype=np.float32), (128, 1))
    tokid = (np.arange(128, dtype=np.uint32)[:, None]
             + 128 * np.arange(8, dtype=np.uint32)[None, :]).astype(np.uint32)
    ones128 = np.ones((128, 1), np.float32).astype(bf)
    ones1 = np.ones((1, 128), np.float32)
    sent = np.full((128, 32), 0x3FFFFFFF, np.uint32)
    eps = np.full((128, 1), 1e-5, np.float32)
    neg1 = np.full((128, 1), -1.0, np.float32)
    return dict(tri_c=np.ascontiguousarray(tri), idn_c=np.ascontiguousarray(idn),
                iota8_c=iota8, tokid_c=tokid, ones128_c=ones128, ones1_c=ones1,
                sent_c=sent, eps_c=eps, neg1_c=neg1)


def kernel(L0_attn_x, L0_attn_B, L0_attn_C, L0_attn_D, L0_ffn_router,
           L0_ffn_w_up_gate, L0_ffn_w_down, L0_ffn_shared_w_up_gate,
           L0_ffn_shared_w_down):
    x = np.ascontiguousarray(np.asarray(L0_attn_x, np.float32).reshape(8192, D))
    w = _prep_weights(L0_ffn_shared_w_up_gate, L0_ffn_shared_w_down,
                      L0_ffn_w_up_gate, L0_ffn_w_down,
                      L0_attn_B, L0_attn_C, L0_ffn_router, L0_attn_D)
    c = _consts()
    nc = build_nc()
    in_maps = []
    for core in range(8):
        m = {"x_in": np.ascontiguousarray(x[TOK * core:TOK * (core + 1)])}
        m.update(w)
        m.update(c)
        in_maps.append(m)
    res = run_bass_kernel_spmd(nc, in_maps, core_ids=list(range(8)))
    out = np.concatenate([np.asarray(r["y_out"]) for r in res.results], axis=0)
    return out.reshape(2, 4096, D).astype(np.float32)


# version 32
# speedup vs baseline: 1.0572x; 1.0006x over previous
import numpy as np
import ml_dtypes
import concourse.bass as bass
import concourse.mybir as mybir
from concourse.bass_utils import run_bass_kernel_spmd
from concourse.bass_core_rust import IndirectOffsetOnAxis
from contextlib import ExitStack

F32 = mybir.dt.float32
BF16 = mybir.dt.bfloat16
U32 = mybir.dt.uint32
AF = mybir.ActivationFunctionType
ALU = mybir.AluOpType
AX = mybir.AxisListType.X

D = 2048
TOK = 1024
NB = 8
KC = 16
NEXP = 8
CL = 384
NSLOT = NEXP * CL
NSB = CL // 128
IDXSPACE = 128 * 32
JSH = 44
JEX = 11
NGRP = 4
GSZ = JSH // NGRP
EPS = 1e-5
BIGF = 1.0e9

ENGINES = ("tensor", "vector", "scalar", "sync", "gpsimd")


class Buf:
    __slots__ = ("w", "r", "name")

    def __init__(self, name=""):
        self.name = name
        self.w = []
        self.r = []


class Prog:
    def __init__(self, nc, stack):
        self.nc = nc
        self.q = {e: [] for e in ENGINES}
        self.cnt = {}
        self.sems = {}
        self.waited = {e: {} for e in ENGINES}
        for s in ("LIN", "LIN2", "WMISC", "XN", "IDX", "W0", "W1", "WD", "WDV",
                  "OB0", "OB1", "CS0", "CS1", "CS2", "CS3", "WB0", "WB1",
                  "U0", "U1",
                  "IND", "G0", "G1", "OUT", "OUT2", "DLD", "SIO", "W", "PE",
                  "DVE", "ACT"):
            self.sems[s] = stack.enter_context(nc.semaphore(s))
            self.cnt[s] = 0

    def _collect(self, eng, toks):
        need = {}
        for t in toks:
            if t is None:
                continue
            s, v, peng = t
            if peng == "tensor" and eng == "tensor":
                continue
            if self.waited[eng].get(s, -1) >= v:
                continue
            if need.get(s, -1) < v:
                need[s] = v
        for s, v in need.items():
            self.waited[eng][s] = v
        return sorted(need.items())

    def _waits_for(self, eng, reads, writes):
        toks = []
        for b in reads:
            toks += b.w
        for b in writes:
            toks += b.w + b.r
        return self._collect(eng, toks)

    def op(self, eng, fn, reads=(), writes=(), add_writes=(), sem=None, inc=None):
        need = self._waits_for(eng, reads, writes)
        if sem is None:
            sem = {"tensor": "PE", "vector": "DVE", "scalar": "ACT",
                   "sync": "SIO", "gpsimd": "W"}[eng]
        if inc is None:
            inc = 16 if eng in ("sync", "gpsimd") else 1
        self.cnt[sem] += inc
        tok = (sem, self.cnt[sem], eng)
        sems = self.sems

        def closure(E, fn=fn, need=tuple(need), semh=sems[sem], n=inc):
            for s, v in need:
                E.wait_ge(sems[s], v)
            fn(E).then_inc(semh, n)

        self.q[eng].append(closure)
        for b in reads:
            b.r.append(tok)
        for b in writes:
            b.w = [tok]
            b.r = []
        for b in add_writes:
            b.w.append(tok)
        return tok

    def mm_group(self, mms, reads=(), psum=None, accum=False):
        toks = []
        for b in reads:
            toks += b.w
        if psum is not None:
            toks += psum.r
            if not accum:
                toks += psum.w
        need = self._collect("tensor", toks)
        sems = self.sems
        nmm = len(mms)
        self.cnt["PE"] += 1
        tok = ("PE", self.cnt["PE"], "tensor")
        for i, (out_fn, l_fn, r_fn, st, sp) in enumerate(mms):
            def closure(E, out_fn=out_fn, l_fn=l_fn, r_fn=r_fn, st=st, sp=sp,
                        need=tuple(need) if i == 0 else (),
                        last=(i == nmm - 1)):
                for s, v in need:
                    E.wait_ge(sems[s], v)
                inst = E.matmul(out_fn(), l_fn(), r_fn(), start=st, stop=sp)
                if last:
                    inst.then_inc(sems["PE"], 1)
            self.q["tensor"].append(closure)
        for b in reads:
            b.r.append(tok)
        if psum is not None:
            if accum:
                psum.w.append(tok)
            else:
                psum.w = [tok]
                psum.r = []
        return tok

    def tr_group(self, in_fn_list, out_fn_list, reads, psum, identity_ap):
        toks = []
        for b in reads:
            toks += b.w
        toks += psum.w + psum.r
        need = self._collect("tensor", toks)
        sems = self.sems
        self.cnt["PE"] += 1
        tok = ("PE", self.cnt["PE"], "tensor")
        n = len(in_fn_list)
        for i in range(n):
            def closure(E, in_fn=in_fn_list[i], out_fn=out_fn_list[i],
                        need=tuple(need) if i == 0 else (), last=(i == n - 1)):
                for s, v in need:
                    E.wait_ge(sems[s], v)
                inst = E.transpose(out=out_fn(), in_=in_fn(), identity=identity_ap)
                if last:
                    inst.then_inc(sems["PE"], 1)
            self.q["tensor"].append(closure)
        for b in reads:
            b.r.append(tok)
        psum.w = [tok]
        psum.r = []
        return tok


def fresh(buf):
    nb = Buf(buf.name)
    nb.w = list(buf.w) + list(buf.r)
    nb.r = []
    return nb


def build_nc(debug=False):
    nc = bass.Bass("TRN2", target_bir_lowering=False, debug=False)

    d_x = nc.dram_tensor("x_in", [TOK, D], F32, kind="ExternalInput").ap()
    d_wsh_up = nc.dram_tensor("wsh_up", [JSH, 128, 4096], F32, kind="ExternalInput").ap()
    d_wsh_dn = nc.dram_tensor("wsh_dn", [NGRP, 128, GSZ * D], F32, kind="ExternalInput").ap()
    d_wex_up = nc.dram_tensor("wex_up", [NEXP, JEX, 128, 4096], F32, kind="ExternalInput").ap()
    d_wex_dn = nc.dram_tensor("wex_dn", [NEXP, 128, JEX * D], F32, kind="ExternalInput").ap()
    d_bt = nc.dram_tensor("bt_w", [8, 128, D], F32, kind="ExternalInput").ap()
    d_ct = nc.dram_tensor("ct_w", [128, 8 * D], F32, kind="ExternalInput").ap()
    d_rt = nc.dram_tensor("rt_w", [128, 128], F32, kind="ExternalInput").ap()
    d_dvec = nc.dram_tensor("d_vec", [1, D], F32, kind="ExternalInput").ap()
    d_tri = nc.dram_tensor("tri_c", [128, 128], BF16, kind="ExternalInput").ap()
    d_idn = nc.dram_tensor("idn_c", [128, 128], BF16, kind="ExternalInput").ap()
    d_iota8 = nc.dram_tensor("iota8_c", [128, 8], F32, kind="ExternalInput").ap()
    d_tokid = nc.dram_tensor("tokid_c", [128, 8], U32, kind="ExternalInput").ap()
    d_ones128 = nc.dram_tensor("ones128_c", [128, 1], BF16, kind="ExternalInput").ap()
    d_ones1 = nc.dram_tensor("ones1_c", [1, 128], F32, kind="ExternalInput").ap()
    d_sent = nc.dram_tensor("sent_c", [128, 32], U32, kind="ExternalInput").ap()
    d_eps = nc.dram_tensor("eps_c", [128, 1], F32, kind="ExternalInput").ap()
    d_neg1 = nc.dram_tensor("neg1_c", [128, 1], F32, kind="ExternalInput").ap()
    d_y = nc.dram_tensor("y_out", [TOK, D], F32, kind="ExternalOutput").ap()
    if debug:
        dbg = {
            "xn1": nc.dram_tensor("dbg_xn1", [TOK, D], BF16, kind="ExternalOutput").ap(),
            "rt1": nc.dram_tensor("dbg_rt1", [TOK, 16], F32, kind="ExternalOutput").ap(),
            "wv1": nc.dram_tensor("dbg_wv1", [128, 16], F32, kind="ExternalOutput").ap(),
            "sl1": nc.dram_tensor("dbg_sl1", [128, 16], U32, kind="ExternalOutput").ap(),
            "ix1": nc.dram_tensor("dbg_ix1", [128, 32], U32, kind="ExternalOutput").ap(),
            "x1": nc.dram_tensor("dbg_x1", [TOK, D], F32, kind="ExternalOutput").ap(),
            "x2": nc.dram_tensor("dbg_x2", [TOK, D], F32, kind="ExternalOutput").ap(),
            "x3": nc.dram_tensor("dbg_x3", [TOK, D], F32, kind="ExternalOutput").ap(),
            "ob1": nc.dram_tensor("dbg_ob1", [NSLOT, D], F32, kind="ExternalOutput").ap(),
        }
    d_xnd = nc.dram_tensor("xn_scr", [TOK, D], BF16, kind="Internal").ap()
    d_idx = nc.dram_tensor("idx_scr", [IDXSPACE, 1], U32, kind="Internal").ap()
    d_ob = nc.dram_tensor("ob_scr", [NSLOT, D], F32, kind="Internal").ap()
    d_cshup = nc.dram_tensor("cshup", [JSH, 128, 4096], BF16, kind="Internal").ap()
    d_cshdn = nc.dram_tensor("cshdn", [NGRP, 128, GSZ * D], BF16, kind="Internal").ap()
    d_cexdn = nc.dram_tensor("cexdn", [NEXP, 128, JEX * D], BF16, kind="Internal").ap()

    stack = ExitStack()
    with stack:
        E_ = stack.enter_context
        t_x = E_(nc.sbuf_tensor("t_x", [128, NB * D], F32))
        t_xnf = E_(nc.sbuf_tensor("t_xnf", [128, KC * TOK], BF16))
        t_h = E_(nc.sbuf_tensor("t_h", [128, GSZ * TOK], BF16))
        t_wd = E_(nc.sbuf_tensor("t_wd", [128, JEX * D], BF16))
        t_up = E_(nc.sbuf_tensor("t_up", [128, 2 * 4096], BF16))
        t_osb = E_(nc.sbuf_tensor("t_osb", [128, 2 * D], F32))
        t_xnt = E_(nc.sbuf_tensor("t_xnt", [128, D], BF16))
        t_gbs = E_(nc.sbuf_tensor("t_gbs", [128, D], BF16))
        t_rt = E_(nc.sbuf_tensor("t_rt", [128, 128], BF16))
        t_tri = E_(nc.sbuf_tensor("t_tri", [128, 128], BF16))
        t_idn = E_(nc.sbuf_tensor("t_idn", [128, 128], BF16))
        t_iota8 = E_(nc.sbuf_tensor("t_iota8", [128, 8], F32))
        t_tokid = E_(nc.sbuf_tensor("t_tokid", [128, 8], U32))
        t_ones128 = E_(nc.sbuf_tensor("t_ones128", [128, 1], BF16))
        t_ones1 = E_(nc.sbuf_tensor("t_ones1", [1, 128], F32))
        t_sent = E_(nc.sbuf_tensor("t_sent", [128, 32], U32))
        t_eps = E_(nc.sbuf_tensor("t_eps", [128, 1], F32))
        t_neg1 = E_(nc.sbuf_tensor("t_neg1", [128, 1], F32))
        t_t8 = E_(nc.sbuf_tensor("t_t8", [128, 8], F32))
        t_ix8 = E_(nc.sbuf_tensor("t_ix8", [128, 8], U32))
        t_t2A = E_(nc.sbuf_tensor("t_t2A", [128, 16], F32))
        t_e12A = E_(nc.sbuf_tensor("t_e12A", [128, 16], F32))
        t_eqA = E_(nc.sbuf_tensor("t_eqA", [128, 128], F32))
        t_sbfA = E_(nc.sbuf_tensor("t_sbfA", [128, 64], BF16))
        t_psbA = E_(nc.sbuf_tensor("t_psbA", [128, 64], F32))

        t_cwA = E_(nc.sbuf_tensor("t_cwA", [128, 16], F32))
        t_posA = E_(nc.sbuf_tensor("t_posA", [128, 16], F32))
        t_sA = E_(nc.sbuf_tensor("t_sA", [128, 32], F32))
        t_suA = E_(nc.sbuf_tensor("t_suA", [128, 16], U32))
        t_su2 = E_(nc.sbuf_tensor("t_su2", [128, 16], U32))
        t_baseA = E_(nc.sbuf_tensor("t_baseA", [1, 64], F32))
        t_slots = E_(nc.sbuf_tensor("t_slots", [128, 16], U32))
        t_wv = E_(nc.sbuf_tensor("t_wv", [128, 16], F32))
        t_idxsb = E_(nc.sbuf_tensor("t_idxsb", [128, 32], U32))
        t_bnP = [E_(nc.sbuf_tensor("t_bn%d" % i, [128, 24], F32)) for i in range(2)]
        t_bn2P = [E_(nc.sbuf_tensor("t_bn2%d" % i, [128, 2], F32)) for i in range(2)]
        t_lnP = [E_(nc.sbuf_tensor("t_ln%d" % i, [128, 4], F32)) for i in range(2)]
        t_lnm = E_(nc.sbuf_tensor("t_lnm", [128, 16], F32))
        t_sil = E_(nc.sbuf_tensor("t_sil", [128, 512], F32))

        p_y = [E_(nc.psum_tensor("p_y%d" % i, [128, 512], F32)) for i in range(4)]
        p_ga = E_(nc.psum_tensor("p_ga", [128, 512], F32))
        p_gb = E_(nc.psum_tensor("p_gb", [128, 512], F32))
        p_tr = E_(nc.psum_tensor("p_tr", [128, 1024], BF16))
        p_sm = E_(nc.psum_tensor("p_sm", [128, 512], F32))

        P = Prog(nc, stack)
        regs = {}
        def _mkregs(E):
            regs["b_tok"] = E.to_reg(TOK - 1)
            regs["b_slot"] = E.to_reg(NSLOT - 1)
            regs["b_idxsp"] = E.to_reg(IDXSPACE - 1)
        P.q["gpsimd"].append(lambda E: _mkregs(E))

        b_x = [Buf("x%d" % b) for b in range(NB)]
        b_xnf = Buf("xnf")
        b_h = [Buf("h%d" % j) for j in range(GSZ)]
        b_wd = Buf("wd")
        b_upA = [Buf("upA0"), Buf("upA1")]
        b_upB = [Buf("upB0"), Buf("upB1")]
        b_osb = [Buf("osb0"), Buf("osb1")]
        b_wd2 = Buf("wd2")
        b_xnt = Buf("xnt")
        b_gbs = [Buf("gbs0"), None]
        b_consts = Buf("consts")
        b_rt = Buf("rt")
        b_t8 = Buf("t8"); b_ix8 = Buf("ix8"); b_t2A = Buf("t2A")
        b_e12A = Buf("e12A")
        b_eqA = Buf("eqA"); b_sbfA = Buf("sbfA")
        b_psbA = Buf("psbA"); b_d12A = Buf("d12A"); b_cwA = Buf("cwA")
        b_posA = Buf("posA"); b_sA = Buf("sA"); b_suA = Buf("suA")
        b_su2 = Buf("su2"); b_baseA = Buf("baseA")
        b_slots = Buf("slots"); b_wv = Buf("wv")
        b_idxsb = Buf("idxsb")
        b_bnP = [Buf("bn%d" % i) for i in range(2)]
        b_bn2P = [Buf("bn2%d" % i) for i in range(2)]
        b_lnP = [Buf("ln%d" % i) for i in range(2)]
        b_lnm = Buf("lnm"); b_sil = Buf("sil")
        b_gbs[1] = b_xnt
        b_py = [Buf("py%d" % i) for i in range(4)]
        b_ga = Buf("ga"); b_gb = Buf("gb"); b_tr = Buf("tr")
        b_sm_lg = [Buf("sm_lg%d" % b) for b in range(NB)]
        b_sm_cnt = Buf("sm_cnt")
        b_sm_tri = Buf("sm_tri"); b_sm_bas = Buf("sm_bas")
        st_bufs = {"xnd": Buf("xnd"), "idxd": Buf("idxd"), "obd": Buf("obd")}
        cb_shup = [Buf("cshup%d" % j) for j in range(JSH)]
        cb_shdn = [Buf("cshdn%d" % g) for g in range(NGRP)]
        cb_exdn = [Buf("cexdn%d" % e) for e in range(NEXP)]
        cs_slot = [Buf("cs_s%d" % i) for i in range(4)]
        wb_slot = [Buf("wb_s%d" % i) for i in range(2)]
        castn = {"i": 0}

        def ld(dst, src, buf=None, add=False, eng="sync", sem="LIN"):
            return P.op(eng, lambda E, d=dst, s=src: E.dma_start(out=d, in_=s),
                        writes=(buf,) if (buf and not add) else (),
                        add_writes=(buf,) if (buf and add) else (),
                        sem=sem, inc=16)

        ld(t_tri[:, :], d_tri[:, :])
        ld(t_idn[:, :], d_idn[:, :])
        ld(t_iota8[:, :], d_iota8[:, :])
        ld(t_tokid[:, :], d_tokid[:, :])
        ld(t_ones128[:, :], d_ones128[:, :])
        ld(t_ones1[:, :], d_ones1[:, :])
        ld(t_sent[:, :], d_sent[:, :])
        ld(t_eps[:, :], d_eps[:, :])
        ld(t_neg1[:, :], d_neg1[:, :])
        d_xr = d_x.rearrange("(b p) d -> p b d", p=128)
        t_xr = t_x[:, :].rearrange("p (b d) -> p b d", d=D)
        tok_lo = ld(t_xr[:, 0:4, :], d_xr[:, 0:4, :])
        tok_hi = ld(t_xr[:, 4:8, :], d_xr[:, 4:8, :], eng="scalar", sem="LIN2")
        b_consts.w = [tok_lo]
        for b in range(NB):
            b_x[b].w = [tok_lo if b < 4 else tok_hi]
        P.op("gpsimd", lambda E: E.dma_start(out=t_rt[:, :], in_=d_rt[:, :]),
             writes=(b_rt,), sem="WMISC")

        def xblk(b):
            return t_x[:, b * D:(b + 1) * D]

        def ln_block(b, keep_mix=False):
            t_bn, t_bn2, t_ln = t_bnP[b % 2], t_bn2P[b % 2], t_lnP[b % 2]
            b_bn, b_bn2, b_ln = b_bnP[b % 2], b_bn2P[b % 2], b_lnP[b % 2]
            for c in range(4):
                P.op("vector", lambda E, c=c, b=b, t_bn=t_bn: E.bn_stats(
                    out=t_bn[:, 6 * c:6 * c + 6],
                    in_=xblk(b)[:, 512 * c:512 * c + 512]),
                    reads=(b_x[b],), writes=(b_bn,) if c == 0 else (),
                    add_writes=(b_bn,) if c > 0 else ())
            P.op("vector", lambda E, t_bn=t_bn, t_bn2=t_bn2: E.bn_aggr(
                out=t_bn2[:, :], in_=t_bn[:, :]),
                reads=(b_bn,), writes=(b_bn2,))
            P.op("scalar", lambda E, t_bn2=t_bn2, t_ln=t_ln: E.activation(
                out=t_ln[:, 2:3], in_=t_bn2[:, 1:2],
                func=AF.Sqrt, bias=t_eps[:, :]),
                reads=(b_bn2, b_consts), writes=(b_ln,))
            P.op("vector", lambda E, t_ln=t_ln: E.reciprocal(
                out=t_ln[:, 0:1], in_=t_ln[:, 2:3]),
                reads=(b_ln,), add_writes=(b_ln,))
            P.op("vector", lambda E, t_bn2=t_bn2, t_ln=t_ln: E.tensor_scalar(
                out=t_ln[:, 1:2], in0=t_bn2[:, 0:1], scalar1=t_ln[:, 0:1],
                scalar2=-1.0, op0=ALU.mult, op1=ALU.mult),
                reads=(b_bn2, b_ln), add_writes=(b_ln,))
            if keep_mix:
                P.op("vector", lambda E, b=b, t_ln=t_ln: E.tensor_copy(
                    out=t_lnm[:, 2 * b:2 * b + 2], in_=t_ln[:, 0:2]),
                    reads=(b_ln,), writes=(b_lnm,) if b == 0 else (),
                    add_writes=(b_lnm,) if b > 0 else ())
            P.op("scalar", lambda E, b=b, t_ln=t_ln: E.activation(
                out=t_xnt[:, :], in_=xblk(b), func=AF.Identity,
                bias=t_ln[:, 1:2], scale=t_ln[:, 0:1]),
                reads=(b_x[b], b_ln), writes=(b_xnt,))

        def xn_transpose(b, first):
            for half in range(2):
                ins = [(lambda k=k: t_xnt[:, 128 * k:128 * k + 128])
                       for k in range(8 * half, 8 * half + 8)]
                outs = [(lambda q=q: p_tr[:, 128 * q:128 * q + 128])
                        for q in range(8)]
                P.tr_group(ins, outs, (b_xnt, b_consts), b_tr, t_idn[:, :])

                def ev(E, b=b, half=half):
                    dst = t_xnf[:, :].rearrange("p (k t) -> p k t", k=KC)[
                        :, 8 * half:8 * half + 8, 128 * b:128 * b + 128]
                    src = p_tr[:, :].rearrange("p (k t) -> p k t", k=8)
                    return E.tensor_copy(out=dst, in_=src)
                fw = first and half == 0
                P.op("vector", ev, reads=(b_tr,),
                     writes=(b_xnf,) if fw else (),
                     add_writes=(b_xnf,) if not fw else ())

        def routing_logits(b):
            mms = [(lambda b=b: p_sm[:, 8 * b:8 * b + 8],
                    lambda k=k, b=b: t_xnf[:, TOK * k + 128 * b: TOK * k + 128 * b + 128],
                    lambda k=k: t_rt[:, 8 * k:8 * k + 8],
                    k == 0, k == KC - 1) for k in range(KC)]
            P.mm_group(mms, reads=(b_xnf, b_rt), psum=b_sm_lg[b])
            P.op("vector", lambda E, b=b: E.max(
                out=t_t8[:, :], in_=p_sm[:, 8 * b:8 * b + 8]),
                reads=(b_sm_lg[b],), writes=(b_t8,))
            P.op("vector", lambda E, b=b: E.max_index(
                out=t_ix8[:, :], in_max=t_t8[:, :],
                in_values=p_sm[:, 8 * b:8 * b + 8]),
                reads=(b_sm_lg[b], b_t8), writes=(b_ix8,))
            P.op("vector", lambda E, b=b: E.tensor_copy(
                out=t_t2A[:, 2 * b:2 * b + 2], in_=t_t8[:, 0:2]),
                reads=(b_t8,),
                writes=(b_t2A,) if b == 0 else (), add_writes=(b_t2A,) if b else ())
            P.op("vector", lambda E, b=b: E.tensor_copy(
                out=t_e12A[:, 2 * b:2 * b + 2], in_=t_ix8[:, 0:2]),
                reads=(b_ix8,),
                writes=(b_e12A,) if b == 0 else (),
                add_writes=(b_e12A,) if b else ())

        def routing_batched(b_idxd):
            def r3(t):
                return t[:, :].rearrange("p (b e) -> p b e", e=8)

            def r2(t):
                return t[:, :].rearrange("p (b k) -> p b k", k=2)

            P.op("vector", lambda E: E.tensor_tensor(
                out=t_sA[:, 0:8], in0=r2(t_t2A)[:, :, 0:1],
                in1=r2(t_t2A)[:, :, 1:2], op=ALU.subtract),
                reads=(b_t2A,), writes=(b_sA,))
            P.op("scalar", lambda E: E.activation(
                out=r2(t_cwA)[:, :, 0:1], in_=t_sA[:, 0:8], func=AF.Sigmoid),
                reads=(b_sA,), writes=(b_cwA,))
            P.op("scalar", lambda E: E.activation(
                out=r2(t_cwA)[:, :, 1:2], in_=t_sA[:, 0:8], func=AF.Sigmoid,
                scale=t_neg1[:, :]),
                reads=(b_sA, b_consts), add_writes=(b_cwA,))
            for b in range(NB):
                for kk in range(2):
                    P.op("vector", lambda E, b=b, kk=kk: E.tensor_scalar(
                        out=t_eqA[:, 64 * kk + 8 * b:64 * kk + 8 * b + 8],
                        in0=t_iota8[:, :], scalar1=t_e12A[:, 2 * b + kk:2 * b + kk + 1],
                        scalar2=None, op0=ALU.is_equal),
                        reads=(b_consts, b_e12A),
                        writes=(b_eqA,) if (b == 0 and kk == 0) else (),
                        add_writes=(b_eqA,) if (b or kk) else ())
            P.op("vector", lambda E: E.tensor_tensor(
                out=t_psbA[:, :], in0=t_eqA[:, 0:64], in1=t_eqA[:, 64:128],
                op=ALU.add),
                reads=(b_eqA,), writes=(b_psbA,))
            P.op("vector", lambda E: E.tensor_copy(out=t_sbfA[:, :], in_=t_psbA[:, :]),
                 reads=(b_psbA,), writes=(b_sbfA,))
            def routing_tail1():
                P.mm_group([(lambda: p_sm[0:1, 256:320], lambda: t_ones128[:, :],
                             lambda: t_sbfA[:, :], True, True)],
                           reads=(b_consts, b_sbfA), psum=b_sm_cnt)
                P.mm_group([(lambda: p_sm[:, 128:192], lambda: t_tri[:, :],
                             lambda: t_sbfA[:, :], True, True)],
                           reads=(b_consts, b_sbfA), psum=b_sm_tri)
                P.op("vector", lambda E: E.memset(t_baseA[:, 0:8], 0.0),
                     writes=(b_baseA,))
                for b in range(1, NB):
                    P.op("vector", lambda E, b=b: E.tensor_tensor(
                        out=t_baseA[:, 8 * b:8 * b + 8],
                        in0=t_baseA[:, 8 * b - 8:8 * b],
                        in1=p_sm[0:1, 256 + 8 * b - 8:256 + 8 * b], op=ALU.add),
                        reads=(b_sm_cnt, b_baseA), add_writes=(b_baseA,))

            def routing_tail2():
              P.mm_group([(lambda: p_sm[:, 192:256], lambda: t_ones1[:, :],
                           lambda: t_baseA[:, :], True, True)],
                         reads=(b_consts, b_baseA), psum=b_sm_bas)
              P.op("vector", lambda E: E.tensor_copy(out=t_psbA[:, :],
                                                     in_=p_sm[:, 128:192]),
                   reads=(b_sm_tri,), writes=(b_psbA,))
              P.op("vector", lambda E: E.scalar_tensor_tensor(
                  out=t_psbA[:, :], in0=p_sm[:, 192:256], scalar=1.0,
                  in1=t_psbA[:, :], op0=ALU.mult, op1=ALU.add),
                  reads=(b_sm_bas, b_psbA), add_writes=(b_psbA,))
              for kk in range(2):
                  P.op("vector", lambda E, kk=kk: E.tensor_tensor(
                      out=t_eqA[:, 64 * kk:64 * kk + 64], in0=t_psbA[:, :],
                      in1=t_eqA[:, 64 * kk:64 * kk + 64], op=ALU.mult),
                      reads=(b_psbA,), writes=(b_eqA,))
                  P.op("vector", lambda E, kk=kk: E.reduce_sum(
                      out=r2(t_posA)[:, :, kk:kk + 1],
                      in_=t_eqA[:, 64 * kk:64 * kk + 64].rearrange(
                          "p (b e) -> p b e", e=8), axis=AX),
                      reads=(b_eqA,), writes=(b_posA,) if kk == 0 else (),
                      add_writes=(b_posA,) if kk else ())
              P.op("vector", lambda E: E.tensor_single_scalar(
                  out=t_sA[:, 0:16], in_=t_posA[:, :], scalar=float(CL),
                  op=ALU.is_ge),
                  reads=(b_posA, b_cwA), writes=(b_sA,))
              P.op("vector", lambda E: E.tensor_scalar(
                  out=t_sA[:, 16:32], in0=t_sA[:, 0:16], scalar1=-0.5, scalar2=0.5,
                  op0=ALU.mult, op1=ALU.add),
                  reads=(b_sA,), add_writes=(b_sA,))
              P.op("vector", lambda E: E.tensor_tensor(
                  out=t_wv[:, :], in0=t_sA[:, 16:32], in1=t_cwA[:, :], op=ALU.mult),
                  reads=(b_sA, b_cwA), writes=(b_wv,))
              P.op("vector", lambda E: E.scalar_tensor_tensor(
                  out=t_sA[:, 16:32], in0=t_e12A[:, :], scalar=float(CL),
                  in1=t_posA[:, :], op0=ALU.mult, op1=ALU.add),
                  reads=(b_e12A, b_posA, b_wv), add_writes=(b_sA,))
              P.op("vector", lambda E: E.scalar_tensor_tensor(
                  out=t_cwA[:, :], in0=t_sA[:, 0:16], scalar=BIGF,
                  in1=t_sA[:, 16:32], op0=ALU.mult, op1=ALU.add),
                  reads=(b_sA,), writes=(b_cwA,))
              P.op("vector", lambda E: E.tensor_copy(
                  out=t_slots[:, :], in_=t_cwA[:, :]),
                  reads=(b_cwA,), writes=(b_slots,))
              P.op("vector", lambda E: E.tensor_copy(out=t_su2[:, :],
                                                     in_=t_sA[:, 16:32]),
                   reads=(b_sA,), writes=(b_su2,))
              P.op("vector", lambda E: E.tensor_scalar(
                  out=t_suA[:, :], in0=t_su2[:, :], scalar1=7, scalar2=None,
                  op0=ALU.logical_shift_right),
                  reads=(b_su2,), writes=(b_suA,))
              P.op("vector", lambda E: E.tensor_scalar(
                  out=t_su2[:, :], in0=t_su2[:, :], scalar1=127, scalar2=None,
                  op0=ALU.bitwise_and),
                  reads=(b_suA,), writes=(b_su2,))
              P.op("vector", lambda E: E.tensor_scalar(
                  out=t_su2[:, :], in0=t_su2[:, :], scalar1=5, scalar2=None,
                  op0=ALU.logical_shift_left),
                  writes=(b_su2,))
              P.op("vector", lambda E: E.tensor_tensor(
                  out=t_suA[:, :], in0=t_suA[:, :], in1=t_su2[:, :],
                  op=ALU.bitwise_or),
                  reads=(b_su2,), add_writes=(b_suA,))
              P.op("vector", lambda E: E.tensor_copy(out=t_su2[:, :],
                                                     in_=t_sA[:, 0:16]),
                   reads=(b_sA, b_suA), writes=(b_su2,))
              P.op("vector", lambda E: E.tensor_scalar(
                  out=t_su2[:, :], in0=t_su2[:, :], scalar1=30, scalar2=None,
                  op0=ALU.logical_shift_left),
                  writes=(b_su2,))
              P.op("vector", lambda E: E.tensor_tensor(
                  out=t_suA[:, :], in0=t_suA[:, :], in1=t_su2[:, :],
                  op=ALU.bitwise_or),
                  reads=(b_su2,), add_writes=(b_suA,))
            def emit_scatters():
                for b in range(NB):
                    for kk in range(2):
                        P.op("gpsimd", lambda E, b=b, kk=kk: E.indirect_dma_start(
                            out=d_idx[:, :],
                            out_offset=IndirectOffsetOnAxis(
                                t_suA[:, 2 * b + kk:2 * b + kk + 1], axis=0),
                            in_=t_tokid[:, b:b + 1], in_offset=None,
                            bounds_check=regs["b_idxsp"], oob_is_err=False),
                            reads=(b_suA, b_consts, b_idxd), add_writes=(b_idxd,),
                            sem="IND")
            return routing_tail1, routing_tail2, emit_scatters

        def silu_mul(dst_ap, dst_buf, first_dst, n=512):
            P.op("scalar", lambda E, n=n: E.activation(out=t_sil[:, 0:n], in_=p_ga[:, 0:n],
                                                       func=AF.Silu),
                 reads=(b_ga,), writes=(b_sil,))
            P.op("vector", lambda E, dst_ap=dst_ap, n=n: E.tensor_tensor(
                out=dst_ap, in0=t_sil[:, 0:n], in1=p_gb[:, 0:n], op=ALU.mult),
                reads=(b_sil, b_gb),
                writes=(dst_buf,) if first_dst else (),
                add_writes=(dst_buf,) if not first_dst else ())

        def dbg_store(dst, src_ap, rbuf, slow=False):
            P.op("sync", lambda E, d=dst, sp=src_ap: E.dma_start(
                out=d, in_=sp, allow_slow_non_contiguous=True) if slow else
                E.dma_start(out=d, in_=sp),
                reads=(rbuf,), sem="OUT")

        def emit_ffn(call, pre_block=None):
            b_xnd = st_bufs["xnd"] = fresh(st_bufs["xnd"])
            b_idxd = st_bufs["idxd"] = fresh(st_bufs["idxd"])
            b_obd = st_bufs["obd"] = fresh(st_bufs["obd"])
            if call == 0:
                QW = GSZ * D // 4
                pend = ([("shdn", g, q) for g in range(NGRP) for q in range(4)]
                        + [("exdn", e, q) for e in range(NEXP) for q in range(4)])

                def pop_pieces(k):
                    for _ in range(k):
                        if not pend:
                            return
                        p = pend.pop(0)
                        sl = castn["i"] % 4
                        castn["i"] += 1
                        sem = "CS%d" % sl
                        if p[0] == "shdn":
                            g, q = p[1], p[2]
                            P.op("gpsimd", lambda E, g=g, q=q: E.dma_start(
                                out=d_cshdn[g][:, QW * q:QW * q + QW],
                                in_=d_wsh_dn[g][:, QW * q:QW * q + QW]),
                                writes=(cs_slot[sl],),
                                add_writes=(cb_shdn[g],), sem=sem)
                        else:
                            e, q = p[1], p[2]
                            P.op("gpsimd", lambda E, e=e, q=q: E.dma_start(
                                out=d_cexdn[e][:, QW * q:QW * q + QW],
                                in_=d_wex_dn[e][:, QW * q:QW * q + QW]),
                                writes=(cs_slot[sl],),
                                add_writes=(cb_exdn[e],), sem=sem)

                for j in range(2):
                    P.op("gpsimd", lambda E, j=j: E.dma_start(
                        out=t_up[:, 4096 * (j % 2):4096 * (j % 2) + 4096],
                        in_=d_wsh_up[j]),
                        writes=(b_upA[j % 2], b_upB[j % 2]),
                        sem="W%d" % (j % 2))
                pop_pieces(24)
            else:
                pend = []

                def pop_pieces(k):
                    pass

                for j in range(2):
                    P.op("sync" if j % 2 == 0 else "scalar",
                         lambda E, j=j: E.dma_start(
                             out=t_up[:, 4096 * (j % 2):4096 * (j % 2) + 4096],
                             in_=d_cshup[j]),
                         reads=(cb_shup[j],),
                         writes=(b_upA[j % 2], b_upB[j % 2]),
                         sem="W%d" % (j % 2), inc=16)
            P.op("sync", lambda E: E.dma_start(
                out=d_idx.rearrange("(p s) o -> p (s o)", p=128), in_=t_sent[:, :]),
                reads=(b_consts,), writes=(b_idxd,), sem="IDX")
            for b in range(NB):
                if pre_block is not None:
                    pre_block(b)
                ln_block(b)
                P.op("sync", lambda E, b=b: E.dma_start(
                    out=d_xnd[128 * b:128 * b + 128, :], in_=t_xnt[:, :]),
                    reads=(b_xnt,), writes=(b_xnd,) if b == 0 else (),
                    add_writes=(b_xnd,) if b else (), sem="XN")
                if debug and call == 0:
                    dbg_store(dbg["xn1"][128 * b:128 * b + 128, :], t_xnt[:, :], b_xnt)
                xn_transpose(b, first=(b == 0))
                routing_logits(b)
            routing_tail1, routing_tail2, emit_scatters = routing_batched(b_idxd)
            def emit_scat_idx():
                emit_scatters()
                P.op("sync", lambda E: E.dma_start(
                    out=t_idxsb[:, :],
                    in_=d_idx.rearrange("(p s) o -> p (s o)", p=128)),
                    reads=(b_idxd,), writes=(b_idxsb,), sem="IDX")
                if debug and call == 0:
                    dbg_store(dbg["ix1"][:, :], t_idxsb[:, :], b_idxsb)

            disp = {"steps": [], "b_buf": None}

            def run_step():
                if not disp["steps"]:
                    return
                st = disp["steps"].pop(0)
                b_buf = disp["b_buf"]
                if st[0] == "g":
                    _, e_t, sb = st
                    col = (e_t * CL + 128 * sb) // 128
                    gp = sb % 2
                    P.op("gpsimd", lambda E, col=col, gp=gp: E.indirect_dma_start(
                        out=(t_gbs[:, :] if gp == 0 else t_xnt[:, :]), out_offset=None,
                        in_=d_xnd[:, :],
                        in_offset=IndirectOffsetOnAxis(t_idxsb[:, col:col + 1], axis=0),
                        bounds_check=regs["b_tok"], oob_is_err=False),
                        reads=(b_idxsb, b_xnd), writes=(b_gbs[gp],), sem="IND" if gp == 0 else "G0")
                else:
                    _, e_t, sb, half = st
                    gp = sb % 2
                    src_t = t_gbs if gp == 0 else t_xnt
                    ins = [(lambda k=k, src_t=src_t: src_t[:, 128 * k:128 * k + 128])
                           for k in range(8 * half, 8 * half + 8)]
                    outs = [(lambda q=q: p_tr[:, 128 * q:128 * q + 128])
                            for q in range(8)]
                    P.tr_group(ins, outs, (b_gbs[gp], b_consts), b_tr, t_idn[:, :])

                    def ev(E, e_t=e_t, sb=sb, half=half):
                        dst = t_xnf[:, 0:2 * KC * CL].rearrange(
                            "p (pp k t) -> p pp k t", pp=2, k=KC)[
                            :, e_t % 2, 8 * half:8 * half + 8, 128 * sb:128 * sb + 128]
                        src = p_tr[:, :].rearrange("p (k t) -> p k t", k=8)
                        return E.tensor_copy(out=dst, in_=src)
                    fw = (e_t == 0 and sb == 0 and half == 0)
                    P.op("vector", ev, reads=(b_tr,),
                         writes=(b_buf,) if fw else (),
                         add_writes=(b_buf,) if not fw else ())

            def queue_dispatch(e_t):
                for sb in range(NSB):
                    disp["steps"].append(("g", e_t, sb))
                    disp["steps"].append(("t", e_t, sb, 0))
                    disp["steps"].append(("t", e_t, sb, 1))

            for g in range(NGRP):
                for jj in range(GSZ):
                    j = g * GSZ + jj
                    par = j % 2
                    if not (g == 0 and jj < 2):
                        if call == 0:
                            P.op("gpsimd", lambda E, j=j, par=par: E.dma_start(
                                out=t_up[:, 4096 * par:4096 * par + 4096],
                                in_=d_wsh_up[j]),
                                writes=(b_upA[par], b_upB[par]),
                                sem="W%d" % par)
                        else:
                            P.op("sync" if par == 0 else "scalar",
                                 lambda E, j=j, par=par: E.dma_start(
                                     out=t_up[:, 4096 * par:4096 * par + 4096],
                                     in_=d_cshup[j]),
                                 reads=(cb_shup[j],),
                                 writes=(b_upA[par], b_upB[par]),
                                 sem="W%d" % par, inc=16)
                    if g == 0 and jj == 1:
                        routing_tail1()
                        routing_tail2()
                    if g == 0 and jj == 10:
                        emit_scat_idx()
                    if jj == 1:
                        HW = GSZ * D // 2
                        P.op("scalar", lambda E, g=g, HW=HW: E.dma_start(
                            out=t_wd[:, :HW], in_=d_cshdn[g][:, :HW]),
                            reads=(cb_shdn[g],), writes=(b_wd,), sem="WDV", inc=16)
                        P.op("sync", lambda E, g=g, HW=HW: E.dma_start(
                            out=t_wd[:, HW:GSZ * D], in_=d_cshdn[g][:, HW:]),
                            reads=(cb_shdn[g],), writes=(b_wd2,), sem="WD", inc=16)
                    for tt in range(2):
                        mms = [(lambda: p_ga[:, :],
                                lambda k=k, par=par: t_up[:, 4096 * par + 128 * k:
                                                          4096 * par + 128 * k + 128],
                                lambda k=k, tt=tt: t_xnf[:, TOK * k + 512 * tt:
                                                         TOK * k + 512 * tt + 512],
                                k == 0, k == KC - 1) for k in range(KC)]
                        P.mm_group(mms, reads=(b_upA[par], b_xnf), psum=b_ga)
                        mms = [(lambda: p_gb[:, :],
                                lambda k=k, par=par: t_up[:, 4096 * par + 2048 + 128 * k:
                                                          4096 * par + 2048 + 128 * k + 128],
                                lambda k=k, tt=tt: t_xnf[:, TOK * k + 512 * tt:
                                                         TOK * k + 512 * tt + 512],
                                k == 0, k == KC - 1) for k in range(KC)]
                        P.mm_group(mms, reads=(b_upB[par], b_xnf), psum=b_gb)
                        silu_mul(t_h[:, TOK * jj + 512 * tt: TOK * jj + 512 * tt + 512],
                                 b_h[jj], first_dst=(tt == 0))
                        if call == 0 and tt == 0:
                            P.op("scalar", lambda E, j=j, par=par: E.dma_start(
                                out=d_cshup[j],
                                in_=t_up[:, 4096 * par:4096 * par + 4096]),
                                reads=(b_upA[par], b_upB[par]),
                                writes=(wb_slot[par],),
                                add_writes=(cb_shup[j],),
                                sem="WB%d" % par, inc=16)
                if g == NGRP - 1:
                    disp["b_buf"] = fresh(b_xnf)
                    queue_dispatch(0)
                for b in range(NB):
                    if b % 2 == 0:
                        pop_pieces(1)
                    for q in range(4):
                        mms = [(lambda q=q: p_y[q][:, :],
                                lambda jj=jj, b=b: t_h[:, TOK * jj + 128 * b:
                                                       TOK * jj + 128 * b + 128],
                                lambda jj=jj, q=q: t_wd[:, D * jj + 512 * q:
                                                        D * jj + 512 * q + 512],
                                jj == 0, jj == GSZ - 1) for jj in range(GSZ)]
                        P.mm_group(mms,
                                   reads=[b_h[jj] for jj in range(GSZ)] + [b_wd, b_wd2],
                                   psum=b_py[q])
                    for q in range(4):
                        P.op("vector", lambda E, b=b, q=q: E.scalar_tensor_tensor(
                            out=xblk(b)[:, 512 * q:512 * q + 512],
                            in0=p_y[q][:, :], scalar=0.5,
                            in1=xblk(b)[:, 512 * q:512 * q + 512],
                            op0=ALU.mult, op1=ALU.add),
                            reads=(b_py[q], b_x[b]), add_writes=(b_x[b],))
                    if g == NGRP - 1:
                        run_step()
                        run_step()

            if debug and call == 0:
                dbg_store(dbg["wv1"][:, :], t_wv[:, :], b_wv)
                dbg_store(dbg["sl1"][:, :], t_slots[:, :], b_slots)
                for b in range(NB):
                    dbg_store(dbg["x1"][128 * b:128 * b + 128, :], xblk(b), b_x[b])

            b_buf = disp["b_buf"]
            for e in range(NEXP):
                par = e % 2
                if e + 1 < NEXP:
                    queue_dispatch(e + 1)
                for j in range(JEX):
                    wpar = j % 2
                    P.op("gpsimd", lambda E, e=e, j=j, wpar=wpar: E.dma_start(
                        out=t_up[:, 4096 * wpar:4096 * wpar + 2048],
                        in_=d_wex_up[e, j][:, 0:2048]),
                        writes=(b_upA[wpar],), sem="W%d" % wpar)
                    P.op("gpsimd", lambda E, e=e, j=j, wpar=wpar: E.dma_start(
                        out=t_up[:, 4096 * wpar + 2048:4096 * wpar + 4096],
                        in_=d_wex_up[e, j][:, 2048:4096]),
                        writes=(b_upB[wpar],), sem="U%d" % wpar)
                    if call == 0 and j == JEX - 1 and e < 4:
                        pop_pieces(2)
                    if j == 1:
                        HW = JEX * D // 2
                        P.op("scalar", lambda E, e=e, HW=HW: E.dma_start(
                            out=t_wd[:, :HW], in_=d_cexdn[e][:, :HW]),
                            reads=(cb_exdn[e],), writes=(b_wd,), sem="WDV", inc=16)
                        P.op("sync", lambda E, e=e, HW=HW: E.dma_start(
                            out=t_wd[:, HW:], in_=d_cexdn[e][:, HW:]),
                            reads=(cb_exdn[e],), writes=(b_wd2,), sem="WD", inc=16)
                    mms = [(lambda: p_ga[:, 0:CL],
                            lambda k=k, wpar=wpar: t_up[:, 4096 * wpar + 128 * k:
                                                        4096 * wpar + 128 * k + 128],
                            lambda k=k, par=par: t_xnf[:, KC * CL * par + CL * k:
                                                       KC * CL * par + CL * k + CL],
                            k == 0, k == KC - 1) for k in range(KC)]
                    P.mm_group(mms, reads=(b_upA[wpar], b_buf), psum=b_ga)
                    mms = [(lambda: p_gb[:, 0:CL],
                            lambda k=k, wpar=wpar: t_up[:, 4096 * wpar + 2048 + 128 * k:
                                                        4096 * wpar + 2048 + 128 * k + 128],
                            lambda k=k, par=par: t_xnf[:, KC * CL * par + CL * k:
                                                       KC * CL * par + CL * k + CL],
                            k == 0, k == KC - 1) for k in range(KC)]
                    P.mm_group(mms, reads=(b_upB[wpar], b_buf), psum=b_gb)
                    silu_mul(t_h[:, TOK * j: TOK * j + CL], b_h[j], first_dst=True, n=CL)
                    run_step()
                for sb in range(NSB):
                    for q in range(4):
                        mms = [(lambda q=q: p_y[q][:, :],
                                lambda j=j, sb=sb: t_h[:, TOK * j + 128 * sb:
                                                       TOK * j + 128 * sb + 128],
                                lambda j=j, q=q: t_wd[:, D * j + 512 * q:
                                                      D * j + 512 * q + 512],
                                j == 0, j == JEX - 1) for j in range(JEX)]
                        P.mm_group(mms,
                                   reads=[b_h[j] for j in range(JEX)] + [b_wd, b_wd2],
                                   psum=b_py[q])
                    opar = sb % 2
                    for q in range(4):
                        P.op("vector", lambda E, opar=opar, q=q: E.tensor_copy(
                            out=t_osb[:, D * opar + 512 * q: D * opar + 512 * q + 512],
                            in_=p_y[q][:, :]),
                            reads=(b_py[q],),
                            writes=(b_osb[opar],) if q == 0 else (),
                            add_writes=(b_osb[opar],) if q else ())
                    first_store = (e == 0 and sb == 0)
                    P.op("sync",
                         lambda E, e=e, sb=sb, opar=opar: E.dma_start(
                        out=d_ob[e * CL + 128 * sb: e * CL + 128 * sb + 128, :],
                        in_=t_osb[:, D * opar: D * opar + D]),
                        reads=(b_osb[opar],),
                        writes=(b_obd,) if first_store else (),
                        add_writes=(b_obd,) if not first_store else (),
                        sem="OB%d" % opar, inc=16)
            b_xnf.w = list(b_buf.w)
            b_xnf.r = list(b_buf.r)
            if debug and call == 0:
                P.op("sync", lambda E: E.dma_start(out=dbg["ob1"][:, :], in_=d_ob[:, :]),
                     reads=(b_obd,), sem="OUT")

            def combine_block(b):
                for kk in range(2):
                    P.op("gpsimd", lambda E, b=b, kk=kk: E.indirect_dma_start(
                        out=t_osb[:, D * kk: D * kk + D], out_offset=None,
                        in_=d_ob[:, :],
                        in_offset=IndirectOffsetOnAxis(
                            t_slots[:, 2 * b + kk: 2 * b + kk + 1], axis=0),
                        bounds_check=regs["b_slot"], oob_is_err=False),
                        reads=(b_slots, b_obd), writes=(b_osb[kk],), sem="G%d" % kk)
                for kk in range(2):
                    P.op("vector", lambda E, b=b, kk=kk: E.scalar_tensor_tensor(
                        out=xblk(b), in0=t_osb[:, D * kk: D * kk + D],
                        scalar=t_wv[:, 2 * b + kk: 2 * b + kk + 1],
                        in1=xblk(b), op0=ALU.mult, op1=ALU.add),
                        reads=(b_osb[kk], b_wv, b_x[b]), add_writes=(b_x[b],))
            return combine_block

        def emit_mixer(pre_block):
            for b in range(NB):
                pre_block(b)
                ln_block(b, keep_mix=True)
                xn_transpose(b, first=(b == 0))
            P.op("sync", lambda E: E.dma_start(out=t_osb[0:1, 0:D], in_=d_dvec[:, :]),
                 writes=(b_osb[0],), sem="DLD")
            for q in range(4):
                P.mm_group([(lambda q=q: p_y[q][:, :], lambda: t_ones1[:, :],
                             lambda q=q: t_osb[0:1, 512 * q:512 * q + 512], True, True)],
                           reads=(b_consts, b_osb[0]), psum=b_py[q])
            for q in range(4):
                P.op("vector", lambda E, q=q: E.tensor_copy(
                    out=t_osb[:, D + 512 * q: D + 512 * q + 512], in_=p_y[q][:, :]),
                    reads=(b_py[q],),
                    writes=(b_osb[1],) if q == 0 else (),
                    add_writes=(b_osb[1],) if q else ())
            def bt_load(p):
                P.op("gpsimd", lambda E, p=p: E.dma_start(
                    out=t_up[:, 4096 * (p % 2):4096 * (p % 2) + 4096].rearrange(
                        "p (n d) -> p n d", n=2),
                    in_=d_bt[2 * p:2 * p + 2].rearrange("n p d -> p n d")),
                    writes=(b_upA[p % 2], b_upB[p % 2]), sem="W%d" % (p % 2))
            bt_load(0)
            bt_load(1)
            for n in range(8):
                wpar = (n // 2) % 2
                for tt in range(2):
                    rix = (2 * n + tt) % 4
                    pp, bpp = p_y[rix], b_py[rix]
                    mms = [(lambda pp=pp: pp[:, :],
                            lambda k=k, n=n, wpar=wpar: t_up[
                                :, 4096 * wpar + 2048 * (n % 2) + 128 * k:
                                4096 * wpar + 2048 * (n % 2) + 128 * k + 128],
                            lambda k=k, tt=tt: t_xnf[:, TOK * k + 512 * tt:
                                                     TOK * k + 512 * tt + 512],
                            k == 0, k == KC - 1) for k in range(KC)]
                    P.mm_group(mms, reads=((b_upA[wpar] if n % 2 == 0 else
                                            b_upB[wpar]), b_xnf), psum=bpp)
                    P.op("scalar", lambda E, n=n, tt=tt, pp=pp: E.activation(
                        out=t_h[:, TOK * n + 512 * tt: TOK * n + 512 * tt + 512],
                        in_=pp[:, :], func=AF.Tanh),
                        reads=(bpp,),
                        writes=(b_h[n],) if tt == 0 else (),
                        add_writes=(b_h[n],) if tt else ())
                if n % 2 == 1 and n // 2 + 2 < 4:
                    bt_load(n // 2 + 2)
            P.op("gpsimd", lambda E: E.dma_start(out=t_wd[:, :8 * D], in_=d_ct[:, :]),
                 writes=(b_wd, b_wd2), sem="WD")
            for b in range(NB):
                for q in range(4):
                    mms = [(lambda q=q: p_y[q][:, :],
                            lambda n=n, b=b: t_h[:, TOK * n + 128 * b:
                                                 TOK * n + 128 * b + 128],
                            lambda n=n, q=q: t_wd[:, D * n + 512 * q:
                                                  D * n + 512 * q + 512],
                            n == 0, n == 7) for n in range(8)]
                    P.mm_group(mms, reads=[b_h[n] for n in range(8)] + [b_wd, b_wd2],
                               psum=b_py[q])
                P.op("vector", lambda E, b=b: E.tensor_tensor(
                    out=t_osb[:, 0:D], in0=xblk(b), in1=t_osb[:, D:2 * D],
                    op=ALU.mult),
                    reads=(b_x[b], b_osb[1]), writes=(b_osb[0],))
                P.op("vector", lambda E, b=b: E.tensor_scalar(
                    out=t_osb[:, 0:D], in0=t_osb[:, 0:D],
                    scalar1=t_lnm[:, 2 * b:2 * b + 1], scalar2=None, op0=ALU.mult),
                    reads=(b_lnm, b_osb[0]), add_writes=(b_osb[0],))
                P.op("vector", lambda E, b=b: E.scalar_tensor_tensor(
                    out=t_osb[:, 0:D], in0=t_osb[:, D:2 * D],
                    scalar=t_lnm[:, 2 * b + 1:2 * b + 2], in1=t_osb[:, 0:D],
                    op0=ALU.mult, op1=ALU.add),
                    reads=(b_lnm, b_osb[1], b_osb[0]), add_writes=(b_osb[0],))
                P.op("vector", lambda E, b=b: E.tensor_tensor(
                    out=xblk(b), in0=xblk(b), in1=t_osb[:, 0:D], op=ALU.add),
                    reads=(b_osb[0], b_x[b]), add_writes=(b_x[b],))
                for q in range(4):
                    P.op("vector", lambda E, b=b, q=q: E.scalar_tensor_tensor(
                        out=xblk(b)[:, 512 * q:512 * q + 512],
                        in0=p_y[q][:, :], scalar=1.0,
                        in1=xblk(b)[:, 512 * q:512 * q + 512],
                        op0=ALU.mult, op1=ALU.add),
                        reads=(b_py[q], b_x[b]), add_writes=(b_x[b],))

        comb1 = emit_ffn(0)
        if debug:
            for b in range(NB):
                comb1(b)
            comb1 = lambda b: None
            for b in range(NB):
                dbg_store(dbg["x2"][128 * b:128 * b + 128, :], xblk(b), b_x[b])
        emit_mixer(comb1)
        if debug:
            for b in range(NB):
                dbg_store(dbg["x3"][128 * b:128 * b + 128, :], xblk(b), b_x[b])
        comb2 = emit_ffn(1)

        for b in range(NB):
            comb2(b)
            P.op("sync" if b % 2 == 0 else "scalar",
                 lambda E, b=b: E.dma_start(
                     out=d_y[128 * b:128 * b + 128, :], in_=xblk(b)),
                 reads=(b_x[b],), sem="OUT" if b % 2 == 0 else "OUT2", inc=16)
        fin = P.cnt["OUT"]
        fin2 = P.cnt["OUT2"]
        sems = P.sems
        P.q["sync"].append(lambda E, v=fin: E.wait_ge(sems["OUT"], v))
        P.q["sync"].append(lambda E, v=fin2: E.wait_ge(sems["OUT2"], v))

        with nc.Block() as block:
            @block.sync
            def _(E):
                for f in P.q["sync"]:
                    f(E)

            @block.gpsimd
            def _(E):
                for f in P.q["gpsimd"]:
                    f(E)

            @block.tensor
            def _(E):
                for f in P.q["tensor"]:
                    f(E)

            @block.vector
            def _(E):
                for f in P.q["vector"]:
                    f(E)

            @block.scalar
            def _(E):
                for f in P.q["scalar"]:
                    f(E)

    return nc


def _prep_weights(L0_ffn_shared_w_up_gate, L0_ffn_shared_w_down,
                  L0_ffn_w_up_gate, L0_ffn_w_down,
                  L0_attn_B, L0_attn_C, L0_ffn_router, L0_attn_D):
    f = np.float32

    def colblk(W, j):
        return np.ascontiguousarray(
            W[:, 128 * j:128 * j + 128].reshape(KC, 128, 128)
            .transpose(1, 0, 2).reshape(128, 2048))

    Wsh = np.asarray(L0_ffn_shared_w_up_gate, f)
    J = 5632
    gate, up = Wsh[:, :J], Wsh[:, J:]
    wsh_up = np.stack([np.concatenate([colblk(gate, j), colblk(up, j)], axis=1)
                       for j in range(JSH)])

    Wd = np.asarray(L0_ffn_shared_w_down, f)
    wsh_dn = np.ascontiguousarray(
        Wd.reshape(NGRP, GSZ, 128, D).transpose(0, 2, 1, 3).reshape(NGRP, 128, GSZ * D))

    We = np.asarray(L0_ffn_w_up_gate, f)
    I = 1408
    wex_up = np.empty((NEXP, JEX, 128, 4096), f)
    for e in range(NEXP):
        ge, ue = We[e][:, :I], We[e][:, I:]
        for j in range(JEX):
            wex_up[e, j, :, :2048] = colblk(ge, j)
            wex_up[e, j, :, 2048:] = colblk(ue, j)

    Wed = np.asarray(L0_ffn_w_down, f)
    wex_dn = np.ascontiguousarray(
        Wed.reshape(NEXP, JEX, 128, D).transpose(0, 2, 1, 3).reshape(NEXP, 128, JEX * D))

    Bt = np.ascontiguousarray(np.asarray(L0_attn_B, f).T)
    bt_w = np.stack([colblk(Bt, n) for n in range(8)])

    Ct = np.ascontiguousarray(np.asarray(L0_attn_C, f).T)
    ct_w = np.ascontiguousarray(
        Ct.reshape(8, 128, D).transpose(1, 0, 2).reshape(128, 8 * D))

    rt_w = np.ascontiguousarray(
        np.asarray(L0_ffn_router, f).reshape(KC, 128, 8).transpose(1, 0, 2).reshape(128, 128))

    d_vec = np.ascontiguousarray(np.asarray(L0_attn_D, f).reshape(1, D))
    return dict(wsh_up=wsh_up, wsh_dn=wsh_dn, wex_up=wex_up, wex_dn=wex_dn,
                bt_w=bt_w, ct_w=ct_w, rt_w=rt_w, d_vec=d_vec)


def _consts():
    bf = ml_dtypes.bfloat16
    tri = np.triu(np.ones((128, 128), np.float32), 1).astype(bf)
    idn = np.eye(128, dtype=np.float32).astype(bf)
    iota8 = np.tile(np.arange(8, dtype=np.float32), (128, 1))
    tokid = (np.arange(128, dtype=np.uint32)[:, None]
             + 128 * np.arange(8, dtype=np.uint32)[None, :]).astype(np.uint32)
    ones128 = np.ones((128, 1), np.float32).astype(bf)
    ones1 = np.ones((1, 128), np.float32)
    sent = np.full((128, 32), 0x3FFFFFFF, np.uint32)
    eps = np.full((128, 1), 1e-5, np.float32)
    neg1 = np.full((128, 1), -1.0, np.float32)
    return dict(tri_c=np.ascontiguousarray(tri), idn_c=np.ascontiguousarray(idn),
                iota8_c=iota8, tokid_c=tokid, ones128_c=ones128, ones1_c=ones1,
                sent_c=sent, eps_c=eps, neg1_c=neg1)


def kernel(L0_attn_x, L0_attn_B, L0_attn_C, L0_attn_D, L0_ffn_router,
           L0_ffn_w_up_gate, L0_ffn_w_down, L0_ffn_shared_w_up_gate,
           L0_ffn_shared_w_down):
    x = np.ascontiguousarray(np.asarray(L0_attn_x, np.float32).reshape(8192, D))
    w = _prep_weights(L0_ffn_shared_w_up_gate, L0_ffn_shared_w_down,
                      L0_ffn_w_up_gate, L0_ffn_w_down,
                      L0_attn_B, L0_attn_C, L0_ffn_router, L0_attn_D)
    c = _consts()
    nc = build_nc()
    in_maps = []
    for core in range(8):
        m = {"x_in": np.ascontiguousarray(x[TOK * core:TOK * (core + 1)])}
        m.update(w)
        m.update(c)
        in_maps.append(m)
    res = run_bass_kernel_spmd(nc, in_maps, core_ids=list(range(8)))
    out = np.concatenate([np.asarray(r["y_out"]) for r in res.results], axis=0)
    return out.reshape(2, 4096, D).astype(np.float32)


# version 33
# speedup vs baseline: 1.0987x; 1.0393x over previous
import numpy as np
import ml_dtypes
import concourse.bass as bass
import concourse.mybir as mybir
from concourse.bass_utils import run_bass_kernel_spmd
from concourse.bass_core_rust import IndirectOffsetOnAxis
from contextlib import ExitStack

F32 = mybir.dt.float32
BF16 = mybir.dt.bfloat16
U32 = mybir.dt.uint32
AF = mybir.ActivationFunctionType
ALU = mybir.AluOpType
AX = mybir.AxisListType.X

D = 2048
TOK = 1024
NB = 8
KC = 16
NEXP = 8
CL = 384
NSLOT = NEXP * CL
NSB = CL // 128
IDXSPACE = 128 * 32
JSH = 44
JEX = 11
NGRP = 4
GSZ = JSH // NGRP
EPS = 1e-5
BIGF = 1.0e9

ENGINES = ("tensor", "vector", "scalar", "sync", "gpsimd")


class Buf:
    __slots__ = ("w", "r", "name")

    def __init__(self, name=""):
        self.name = name
        self.w = []
        self.r = []


class Prog:
    def __init__(self, nc, stack):
        self.nc = nc
        self.q = {e: [] for e in ENGINES}
        self.cnt = {}
        self.sems = {}
        self.waited = {e: {} for e in ENGINES}
        for s in ("LIN", "LIN2", "WMISC", "XN", "IDX", "W0", "W1", "WD", "WDV",
                  "OB0", "OB1", "CS0", "CS1", "CS2", "CS3", "WB0", "WB1",
                  "U0", "U1",
                  "IND", "G0", "G1", "OUT", "OUT2", "DLD", "SIO", "W", "PE",
                  "DVE", "ACT"):
            self.sems[s] = stack.enter_context(nc.semaphore(s))
            self.cnt[s] = 0

    def _collect(self, eng, toks):
        need = {}
        for t in toks:
            if t is None:
                continue
            s, v, peng = t
            if peng == "tensor" and eng == "tensor":
                continue
            if self.waited[eng].get(s, -1) >= v:
                continue
            if need.get(s, -1) < v:
                need[s] = v
        for s, v in need.items():
            self.waited[eng][s] = v
        return sorted(need.items())

    def _waits_for(self, eng, reads, writes):
        toks = []
        for b in reads:
            toks += b.w
        for b in writes:
            toks += b.w + b.r
        return self._collect(eng, toks)

    def op(self, eng, fn, reads=(), writes=(), add_writes=(), sem=None, inc=None):
        need = self._waits_for(eng, reads, writes)
        if sem is None:
            sem = {"tensor": "PE", "vector": "DVE", "scalar": "ACT",
                   "sync": "SIO", "gpsimd": "W"}[eng]
        if inc is None:
            inc = 16 if eng in ("sync", "gpsimd") else 1
        self.cnt[sem] += inc
        tok = (sem, self.cnt[sem], eng)
        sems = self.sems

        def closure(E, fn=fn, need=tuple(need), semh=sems[sem], n=inc):
            for s, v in need:
                E.wait_ge(sems[s], v)
            fn(E).then_inc(semh, n)

        self.q[eng].append(closure)
        for b in reads:
            b.r.append(tok)
        for b in writes:
            b.w = [tok]
            b.r = []
        for b in add_writes:
            b.w.append(tok)
        return tok

    def mm_group(self, mms, reads=(), psum=None, accum=False):
        toks = []
        for b in reads:
            toks += b.w
        if psum is not None:
            toks += psum.r
            if not accum:
                toks += psum.w
        need = self._collect("tensor", toks)
        sems = self.sems
        nmm = len(mms)
        self.cnt["PE"] += 1
        tok = ("PE", self.cnt["PE"], "tensor")
        for i, (out_fn, l_fn, r_fn, st, sp) in enumerate(mms):
            def closure(E, out_fn=out_fn, l_fn=l_fn, r_fn=r_fn, st=st, sp=sp,
                        need=tuple(need) if i == 0 else (),
                        last=(i == nmm - 1)):
                for s, v in need:
                    E.wait_ge(sems[s], v)
                inst = E.matmul(out_fn(), l_fn(), r_fn(), start=st, stop=sp)
                if last:
                    inst.then_inc(sems["PE"], 1)
            self.q["tensor"].append(closure)
        for b in reads:
            b.r.append(tok)
        if psum is not None:
            if accum:
                psum.w.append(tok)
            else:
                psum.w = [tok]
                psum.r = []
        return tok

    def tr_group(self, in_fn_list, out_fn_list, reads, psum, identity_ap):
        toks = []
        for b in reads:
            toks += b.w
        toks += psum.w + psum.r
        need = self._collect("tensor", toks)
        sems = self.sems
        self.cnt["PE"] += 1
        tok = ("PE", self.cnt["PE"], "tensor")
        n = len(in_fn_list)
        for i in range(n):
            def closure(E, in_fn=in_fn_list[i], out_fn=out_fn_list[i],
                        need=tuple(need) if i == 0 else (), last=(i == n - 1)):
                for s, v in need:
                    E.wait_ge(sems[s], v)
                inst = E.transpose(out=out_fn(), in_=in_fn(), identity=identity_ap)
                if last:
                    inst.then_inc(sems["PE"], 1)
            self.q["tensor"].append(closure)
        for b in reads:
            b.r.append(tok)
        psum.w = [tok]
        psum.r = []
        return tok


def fresh(buf):
    nb = Buf(buf.name)
    nb.w = list(buf.w) + list(buf.r)
    nb.r = []
    return nb


def build_nc(debug=False):
    nc = bass.Bass("TRN2", target_bir_lowering=False, debug=False)

    d_x = nc.dram_tensor("x_in", [TOK, D], F32, kind="ExternalInput").ap()
    d_wsh_up = nc.dram_tensor("wsh_up", [JSH, 128, 4096], F32, kind="ExternalInput").ap()
    d_wsh_dn = nc.dram_tensor("wsh_dn", [NGRP, 128, GSZ * D], F32, kind="ExternalInput").ap()
    d_wex_up = nc.dram_tensor("wex_up", [NEXP, JEX, 128, 4096], F32, kind="ExternalInput").ap()
    d_wex_dn = nc.dram_tensor("wex_dn", [NEXP, 128, JEX * D], F32, kind="ExternalInput").ap()
    d_bt = nc.dram_tensor("bt_w", [8, 128, D], F32, kind="ExternalInput").ap()
    d_ct = nc.dram_tensor("ct_w", [128, 8 * D], F32, kind="ExternalInput").ap()
    d_rt = nc.dram_tensor("rt_w", [128, 128], F32, kind="ExternalInput").ap()
    d_dvec = nc.dram_tensor("d_vec", [1, D], F32, kind="ExternalInput").ap()
    d_tri = nc.dram_tensor("tri_c", [128, 128], BF16, kind="ExternalInput").ap()
    d_idn = nc.dram_tensor("idn_c", [128, 128], BF16, kind="ExternalInput").ap()
    d_iota8 = nc.dram_tensor("iota8_c", [128, 8], F32, kind="ExternalInput").ap()
    d_tokid = nc.dram_tensor("tokid_c", [128, 8], U32, kind="ExternalInput").ap()
    d_ones128 = nc.dram_tensor("ones128_c", [128, 1], BF16, kind="ExternalInput").ap()
    d_ones1 = nc.dram_tensor("ones1_c", [1, 128], F32, kind="ExternalInput").ap()
    d_sent = nc.dram_tensor("sent_c", [128, 32], U32, kind="ExternalInput").ap()
    d_eps = nc.dram_tensor("eps_c", [128, 1], F32, kind="ExternalInput").ap()
    d_neg1 = nc.dram_tensor("neg1_c", [128, 1], F32, kind="ExternalInput").ap()
    d_y = nc.dram_tensor("y_out", [TOK, D], F32, kind="ExternalOutput").ap()
    if debug:
        dbg = {
            "xn1": nc.dram_tensor("dbg_xn1", [TOK, D], BF16, kind="ExternalOutput").ap(),
            "rt1": nc.dram_tensor("dbg_rt1", [TOK, 16], F32, kind="ExternalOutput").ap(),
            "wv1": nc.dram_tensor("dbg_wv1", [128, 16], F32, kind="ExternalOutput").ap(),
            "sl1": nc.dram_tensor("dbg_sl1", [128, 16], U32, kind="ExternalOutput").ap(),
            "ix1": nc.dram_tensor("dbg_ix1", [128, 32], U32, kind="ExternalOutput").ap(),
            "x1": nc.dram_tensor("dbg_x1", [TOK, D], F32, kind="ExternalOutput").ap(),
            "x2": nc.dram_tensor("dbg_x2", [TOK, D], F32, kind="ExternalOutput").ap(),
            "x3": nc.dram_tensor("dbg_x3", [TOK, D], F32, kind="ExternalOutput").ap(),
            "ob1": nc.dram_tensor("dbg_ob1", [NSLOT, D], F32, kind="ExternalOutput").ap(),
        }
    d_xnd = nc.dram_tensor("xn_scr", [TOK, D], BF16, kind="Internal").ap()
    d_idx = nc.dram_tensor("idx_scr", [IDXSPACE, 1], U32, kind="Internal").ap()
    d_ob = nc.dram_tensor("ob_scr", [NSLOT, D], F32, kind="Internal").ap()
    d_cshup = nc.dram_tensor("cshup", [JSH, 128, 4096], BF16, kind="Internal").ap()
    d_cshdn = nc.dram_tensor("cshdn", [NGRP, 128, GSZ * D], BF16, kind="Internal").ap()
    d_cexdn = nc.dram_tensor("cexdn", [NEXP, 128, JEX * D], BF16, kind="Internal").ap()

    stack = ExitStack()
    with stack:
        E_ = stack.enter_context
        t_x = E_(nc.sbuf_tensor("t_x", [128, NB * D], F32))
        t_xnf = E_(nc.sbuf_tensor("t_xnf", [128, KC * TOK], BF16))
        t_h = E_(nc.sbuf_tensor("t_h", [128, GSZ * TOK], BF16))
        t_wd = E_(nc.sbuf_tensor("t_wd", [128, JEX * D], BF16))
        t_up = E_(nc.sbuf_tensor("t_up", [128, 2 * 4096], BF16))
        t_osb = E_(nc.sbuf_tensor("t_osb", [128, 2 * D], F32))
        t_xnt = E_(nc.sbuf_tensor("t_xnt", [128, D], BF16))
        t_gbs = E_(nc.sbuf_tensor("t_gbs", [128, D], BF16))
        t_rt = E_(nc.sbuf_tensor("t_rt", [128, 128], BF16))
        t_tri = E_(nc.sbuf_tensor("t_tri", [128, 128], BF16))
        t_idn = E_(nc.sbuf_tensor("t_idn", [128, 128], BF16))
        t_iota8 = E_(nc.sbuf_tensor("t_iota8", [128, 8], F32))
        t_tokid = E_(nc.sbuf_tensor("t_tokid", [128, 8], U32))
        t_ones128 = E_(nc.sbuf_tensor("t_ones128", [128, 1], BF16))
        t_ones1 = E_(nc.sbuf_tensor("t_ones1", [1, 128], F32))
        t_sent = E_(nc.sbuf_tensor("t_sent", [128, 32], U32))
        t_eps = E_(nc.sbuf_tensor("t_eps", [128, 1], F32))
        t_neg1 = E_(nc.sbuf_tensor("t_neg1", [128, 1], F32))
        t_t8 = E_(nc.sbuf_tensor("t_t8", [128, 8], F32))
        t_ix8 = E_(nc.sbuf_tensor("t_ix8", [128, 8], U32))
        t_t2A = E_(nc.sbuf_tensor("t_t2A", [128, 16], F32))
        t_e12A = E_(nc.sbuf_tensor("t_e12A", [128, 16], F32))
        t_eqA = E_(nc.sbuf_tensor("t_eqA", [128, 128], F32))
        t_sbfA = E_(nc.sbuf_tensor("t_sbfA", [128, 64], BF16))
        t_psbA = E_(nc.sbuf_tensor("t_psbA", [128, 64], F32))

        t_cwA = E_(nc.sbuf_tensor("t_cwA", [128, 16], F32))
        t_posA = E_(nc.sbuf_tensor("t_posA", [128, 16], F32))
        t_sA = E_(nc.sbuf_tensor("t_sA", [128, 32], F32))
        t_suA = E_(nc.sbuf_tensor("t_suA", [128, 16], U32))
        t_su2 = E_(nc.sbuf_tensor("t_su2", [128, 16], U32))
        t_baseA = E_(nc.sbuf_tensor("t_baseA", [1, 64], F32))
        t_slots = E_(nc.sbuf_tensor("t_slots", [128, 16], U32))
        t_wv = E_(nc.sbuf_tensor("t_wv", [128, 16], F32))
        t_idxsb = E_(nc.sbuf_tensor("t_idxsb", [128, 32], U32))
        t_bnP = [E_(nc.sbuf_tensor("t_bn%d" % i, [128, 24], F32)) for i in range(2)]
        t_bn2P = [E_(nc.sbuf_tensor("t_bn2%d" % i, [128, 2], F32)) for i in range(2)]
        t_lnP = [E_(nc.sbuf_tensor("t_ln%d" % i, [128, 4], F32)) for i in range(2)]
        t_lnm = E_(nc.sbuf_tensor("t_lnm", [128, 16], F32))
        t_sil = E_(nc.sbuf_tensor("t_sil", [128, 512], F32))

        p_y = [E_(nc.psum_tensor("p_y%d" % i, [128, 512], F32)) for i in range(4)]
        p_ga = E_(nc.psum_tensor("p_ga", [128, 512], F32))
        p_gb = E_(nc.psum_tensor("p_gb", [128, 512], F32))
        p_tr = E_(nc.psum_tensor("p_tr", [128, 1024], BF16))
        p_sm = E_(nc.psum_tensor("p_sm", [128, 512], F32))

        P = Prog(nc, stack)
        regs = {}
        def _mkregs(E):
            regs["b_tok"] = E.to_reg(TOK - 1)
            regs["b_slot"] = E.to_reg(NSLOT - 1)
            regs["b_idxsp"] = E.to_reg(IDXSPACE - 1)
        P.q["gpsimd"].append(lambda E: _mkregs(E))

        b_x = [Buf("x%d" % b) for b in range(NB)]
        b_xnf = Buf("xnf")
        b_h = [Buf("h%d" % j) for j in range(GSZ)]
        b_wd = Buf("wd")
        b_upA = [Buf("upA0"), Buf("upA1")]
        b_upB = [Buf("upB0"), Buf("upB1")]
        b_osb = [Buf("osb0"), Buf("osb1")]
        b_wd2 = Buf("wd2")
        b_xnt = Buf("xnt")
        b_gbs = [Buf("gbs0"), None]
        b_consts = Buf("consts")
        b_rt = Buf("rt")
        b_t8 = Buf("t8"); b_ix8 = Buf("ix8"); b_t2A = Buf("t2A")
        b_e12A = Buf("e12A")
        b_eqA = Buf("eqA"); b_sbfA = Buf("sbfA")
        b_psbA = Buf("psbA"); b_d12A = Buf("d12A"); b_cwA = Buf("cwA")
        b_posA = Buf("posA"); b_sA = Buf("sA"); b_suA = Buf("suA")
        b_su2 = Buf("su2"); b_baseA = Buf("baseA")
        b_slots = Buf("slots"); b_wv = Buf("wv")
        b_idxsb = Buf("idxsb")
        b_bnP = [Buf("bn%d" % i) for i in range(2)]
        b_bn2P = [Buf("bn2%d" % i) for i in range(2)]
        b_lnP = [Buf("ln%d" % i) for i in range(2)]
        b_lnm = Buf("lnm"); b_sil = Buf("sil")
        b_gbs[1] = b_xnt
        b_py = [Buf("py%d" % i) for i in range(4)]
        b_ga = Buf("ga"); b_gb = Buf("gb"); b_tr = Buf("tr")
        b_sm_lg = [Buf("sm_lg%d" % b) for b in range(NB)]
        b_sm_cnt = Buf("sm_cnt")
        b_sm_tri = Buf("sm_tri"); b_sm_bas = Buf("sm_bas")
        st_bufs = {"xnd": Buf("xnd"), "idxd": Buf("idxd"), "obd": Buf("obd")}
        cb_shup = [Buf("cshup%d" % j) for j in range(JSH)]
        cb_shdn = [Buf("cshdn%d" % g) for g in range(NGRP)]
        cb_exdn = [Buf("cexdn%d" % e) for e in range(NEXP)]
        cs_slot = [Buf("cs_s%d" % i) for i in range(4)]
        wb_slot = [Buf("wb_s%d" % i) for i in range(2)]
        castn = {"i": 0}

        def ld(dst, src, buf=None, add=False, eng="sync", sem="LIN"):
            return P.op(eng, lambda E, d=dst, s=src: E.dma_start(out=d, in_=s),
                        writes=(buf,) if (buf and not add) else (),
                        add_writes=(buf,) if (buf and add) else (),
                        sem=sem, inc=16)

        ld(t_tri[:, :], d_tri[:, :])
        ld(t_idn[:, :], d_idn[:, :])
        ld(t_iota8[:, :], d_iota8[:, :])
        ld(t_tokid[:, :], d_tokid[:, :])
        ld(t_ones128[:, :], d_ones128[:, :])
        ld(t_ones1[:, :], d_ones1[:, :])
        ld(t_sent[:, :], d_sent[:, :])
        ld(t_eps[:, :], d_eps[:, :])
        ld(t_neg1[:, :], d_neg1[:, :])
        d_xr = d_x.rearrange("(b p) d -> p b d", p=128)
        t_xr = t_x[:, :].rearrange("p (b d) -> p b d", d=D)
        tok_lo = ld(t_xr[:, 0:4, :], d_xr[:, 0:4, :])
        tok_hi = ld(t_xr[:, 4:8, :], d_xr[:, 4:8, :], eng="scalar", sem="LIN2")
        b_consts.w = [tok_lo]
        for b in range(NB):
            b_x[b].w = [tok_lo if b < 4 else tok_hi]
        P.op("gpsimd", lambda E: E.dma_start(out=t_rt[:, :], in_=d_rt[:, :]),
             writes=(b_rt,), sem="WMISC")

        def xblk(b):
            return t_x[:, b * D:(b + 1) * D]

        def ln_stage1(b):
            t_bn, t_bn2 = t_bnP[b % 2], t_bn2P[b % 2]
            b_bn, b_bn2 = b_bnP[b % 2], b_bn2P[b % 2]
            for c in range(4):
                P.op("vector", lambda E, c=c, b=b, t_bn=t_bn: E.bn_stats(
                    out=t_bn[:, 6 * c:6 * c + 6],
                    in_=xblk(b)[:, 512 * c:512 * c + 512]),
                    reads=(b_x[b],), writes=(b_bn,) if c == 0 else (),
                    add_writes=(b_bn,) if c > 0 else ())
            P.op("vector", lambda E, t_bn=t_bn, t_bn2=t_bn2: E.bn_aggr(
                out=t_bn2[:, :], in_=t_bn[:, :]),
                reads=(b_bn,), writes=(b_bn2,))

        def ln_stage2(b, keep_mix=False):
            t_bn2, t_ln = t_bn2P[b % 2], t_lnP[b % 2]
            b_bn2, b_ln = b_bn2P[b % 2], b_lnP[b % 2]
            P.op("scalar", lambda E, t_bn2=t_bn2, t_ln=t_ln: E.activation(
                out=t_ln[:, 2:3], in_=t_bn2[:, 1:2],
                func=AF.Sqrt, bias=t_eps[:, :]),
                reads=(b_bn2, b_consts), writes=(b_ln,))
            P.op("vector", lambda E, t_ln=t_ln: E.reciprocal(
                out=t_ln[:, 0:1], in_=t_ln[:, 2:3]),
                reads=(b_ln,), add_writes=(b_ln,))
            P.op("vector", lambda E, t_bn2=t_bn2, t_ln=t_ln: E.tensor_scalar(
                out=t_ln[:, 1:2], in0=t_bn2[:, 0:1], scalar1=t_ln[:, 0:1],
                scalar2=-1.0, op0=ALU.mult, op1=ALU.mult),
                reads=(b_bn2, b_ln), add_writes=(b_ln,))
            if keep_mix:
                P.op("vector", lambda E, b=b, t_ln=t_ln: E.tensor_copy(
                    out=t_lnm[:, 2 * b:2 * b + 2], in_=t_ln[:, 0:2]),
                    reads=(b_ln,), writes=(b_lnm,) if b == 0 else (),
                    add_writes=(b_lnm,) if b > 0 else ())
            P.op("scalar", lambda E, b=b, t_ln=t_ln: E.activation(
                out=t_xnt[:, :], in_=xblk(b), func=AF.Identity,
                bias=t_ln[:, 1:2], scale=t_ln[:, 0:1]),
                reads=(b_x[b], b_ln), writes=(b_xnt,))

        def xn_transpose(b, first):
            for half in range(2):
                ins = [(lambda k=k: t_xnt[:, 128 * k:128 * k + 128])
                       for k in range(8 * half, 8 * half + 8)]
                outs = [(lambda q=q: p_tr[:, 128 * q:128 * q + 128])
                        for q in range(8)]
                P.tr_group(ins, outs, (b_xnt, b_consts), b_tr, t_idn[:, :])

                def ev(E, b=b, half=half):
                    dst = t_xnf[:, :].rearrange("p (k t) -> p k t", k=KC)[
                        :, 8 * half:8 * half + 8, 128 * b:128 * b + 128]
                    src = p_tr[:, :].rearrange("p (k t) -> p k t", k=8)
                    return E.tensor_copy(out=dst, in_=src)
                fw = first and half == 0
                P.op("vector", ev, reads=(b_tr,),
                     writes=(b_xnf,) if fw else (),
                     add_writes=(b_xnf,) if not fw else ())

        def routing_logits(b):
            mms = [(lambda b=b: p_sm[:, 8 * b:8 * b + 8],
                    lambda k=k, b=b: t_xnf[:, TOK * k + 128 * b: TOK * k + 128 * b + 128],
                    lambda k=k: t_rt[:, 8 * k:8 * k + 8],
                    k == 0, k == KC - 1) for k in range(KC)]
            P.mm_group(mms, reads=(b_xnf, b_rt), psum=b_sm_lg[b])
            P.op("vector", lambda E, b=b: E.max(
                out=t_t8[:, :], in_=p_sm[:, 8 * b:8 * b + 8]),
                reads=(b_sm_lg[b],), writes=(b_t8,))
            P.op("vector", lambda E, b=b: E.max_index(
                out=t_ix8[:, :], in_max=t_t8[:, :],
                in_values=p_sm[:, 8 * b:8 * b + 8]),
                reads=(b_sm_lg[b], b_t8), writes=(b_ix8,))
            P.op("vector", lambda E, b=b: E.tensor_copy(
                out=t_t2A[:, 2 * b:2 * b + 2], in_=t_t8[:, 0:2]),
                reads=(b_t8,),
                writes=(b_t2A,) if b == 0 else (), add_writes=(b_t2A,) if b else ())
            P.op("vector", lambda E, b=b: E.tensor_copy(
                out=t_e12A[:, 2 * b:2 * b + 2], in_=t_ix8[:, 0:2]),
                reads=(b_ix8,),
                writes=(b_e12A,) if b == 0 else (),
                add_writes=(b_e12A,) if b else ())

        def routing_batched(b_idxd):
            def r3(t):
                return t[:, :].rearrange("p (b e) -> p b e", e=8)

            def r2(t):
                return t[:, :].rearrange("p (b k) -> p b k", k=2)

            P.op("vector", lambda E: E.tensor_tensor(
                out=t_sA[:, 0:8], in0=r2(t_t2A)[:, :, 0:1],
                in1=r2(t_t2A)[:, :, 1:2], op=ALU.subtract),
                reads=(b_t2A,), writes=(b_sA,))
            P.op("scalar", lambda E: E.activation(
                out=r2(t_cwA)[:, :, 0:1], in_=t_sA[:, 0:8], func=AF.Sigmoid),
                reads=(b_sA,), writes=(b_cwA,))
            P.op("scalar", lambda E: E.activation(
                out=r2(t_cwA)[:, :, 1:2], in_=t_sA[:, 0:8], func=AF.Sigmoid,
                scale=t_neg1[:, :]),
                reads=(b_sA, b_consts), add_writes=(b_cwA,))
            for b in range(NB):
                for kk in range(2):
                    P.op("vector", lambda E, b=b, kk=kk: E.tensor_scalar(
                        out=t_eqA[:, 64 * kk + 8 * b:64 * kk + 8 * b + 8],
                        in0=t_iota8[:, :], scalar1=t_e12A[:, 2 * b + kk:2 * b + kk + 1],
                        scalar2=None, op0=ALU.is_equal),
                        reads=(b_consts, b_e12A),
                        writes=(b_eqA,) if (b == 0 and kk == 0) else (),
                        add_writes=(b_eqA,) if (b or kk) else ())
            P.op("vector", lambda E: E.tensor_tensor(
                out=t_psbA[:, :], in0=t_eqA[:, 0:64], in1=t_eqA[:, 64:128],
                op=ALU.add),
                reads=(b_eqA,), writes=(b_psbA,))
            P.op("vector", lambda E: E.tensor_copy(out=t_sbfA[:, :], in_=t_psbA[:, :]),
                 reads=(b_psbA,), writes=(b_sbfA,))
            def routing_tail1():
                P.mm_group([(lambda: p_sm[0:1, 256:320], lambda: t_ones128[:, :],
                             lambda: t_sbfA[:, :], True, True)],
                           reads=(b_consts, b_sbfA), psum=b_sm_cnt)
                P.mm_group([(lambda: p_sm[:, 128:192], lambda: t_tri[:, :],
                             lambda: t_sbfA[:, :], True, True)],
                           reads=(b_consts, b_sbfA), psum=b_sm_tri)
                P.op("vector", lambda E: E.memset(t_baseA[:, 0:8], 0.0),
                     writes=(b_baseA,))
                for b in range(1, NB):
                    P.op("vector", lambda E, b=b: E.tensor_tensor(
                        out=t_baseA[:, 8 * b:8 * b + 8],
                        in0=t_baseA[:, 8 * b - 8:8 * b],
                        in1=p_sm[0:1, 256 + 8 * b - 8:256 + 8 * b], op=ALU.add),
                        reads=(b_sm_cnt, b_baseA), add_writes=(b_baseA,))

            def routing_tail2():
              P.mm_group([(lambda: p_sm[:, 192:256], lambda: t_ones1[:, :],
                           lambda: t_baseA[:, :], True, True)],
                         reads=(b_consts, b_baseA), psum=b_sm_bas)
              P.op("vector", lambda E: E.tensor_copy(out=t_psbA[:, :],
                                                     in_=p_sm[:, 128:192]),
                   reads=(b_sm_tri,), writes=(b_psbA,))
              P.op("vector", lambda E: E.scalar_tensor_tensor(
                  out=t_psbA[:, :], in0=p_sm[:, 192:256], scalar=1.0,
                  in1=t_psbA[:, :], op0=ALU.mult, op1=ALU.add),
                  reads=(b_sm_bas, b_psbA), add_writes=(b_psbA,))
              for kk in range(2):
                  P.op("vector", lambda E, kk=kk: E.tensor_tensor(
                      out=t_eqA[:, 64 * kk:64 * kk + 64], in0=t_psbA[:, :],
                      in1=t_eqA[:, 64 * kk:64 * kk + 64], op=ALU.mult),
                      reads=(b_psbA,), writes=(b_eqA,))
                  P.op("vector", lambda E, kk=kk: E.reduce_sum(
                      out=r2(t_posA)[:, :, kk:kk + 1],
                      in_=t_eqA[:, 64 * kk:64 * kk + 64].rearrange(
                          "p (b e) -> p b e", e=8), axis=AX),
                      reads=(b_eqA,), writes=(b_posA,) if kk == 0 else (),
                      add_writes=(b_posA,) if kk else ())
              P.op("vector", lambda E: E.tensor_single_scalar(
                  out=t_sA[:, 0:16], in_=t_posA[:, :], scalar=float(CL),
                  op=ALU.is_ge),
                  reads=(b_posA, b_cwA), writes=(b_sA,))
              P.op("vector", lambda E: E.tensor_scalar(
                  out=t_sA[:, 16:32], in0=t_sA[:, 0:16], scalar1=-0.5, scalar2=0.5,
                  op0=ALU.mult, op1=ALU.add),
                  reads=(b_sA,), add_writes=(b_sA,))
              P.op("vector", lambda E: E.tensor_tensor(
                  out=t_wv[:, :], in0=t_sA[:, 16:32], in1=t_cwA[:, :], op=ALU.mult),
                  reads=(b_sA, b_cwA), writes=(b_wv,))
              P.op("vector", lambda E: E.scalar_tensor_tensor(
                  out=t_sA[:, 16:32], in0=t_e12A[:, :], scalar=float(CL),
                  in1=t_posA[:, :], op0=ALU.mult, op1=ALU.add),
                  reads=(b_e12A, b_posA, b_wv), add_writes=(b_sA,))
              P.op("vector", lambda E: E.scalar_tensor_tensor(
                  out=t_cwA[:, :], in0=t_sA[:, 0:16], scalar=BIGF,
                  in1=t_sA[:, 16:32], op0=ALU.mult, op1=ALU.add),
                  reads=(b_sA,), writes=(b_cwA,))
              P.op("vector", lambda E: E.tensor_copy(
                  out=t_slots[:, :], in_=t_cwA[:, :]),
                  reads=(b_cwA,), writes=(b_slots,))
              P.op("vector", lambda E: E.tensor_copy(out=t_su2[:, :],
                                                     in_=t_sA[:, 16:32]),
                   reads=(b_sA,), writes=(b_su2,))
              P.op("vector", lambda E: E.tensor_scalar(
                  out=t_suA[:, :], in0=t_su2[:, :], scalar1=7, scalar2=None,
                  op0=ALU.logical_shift_right),
                  reads=(b_su2,), writes=(b_suA,))
              P.op("vector", lambda E: E.tensor_scalar(
                  out=t_su2[:, :], in0=t_su2[:, :], scalar1=127, scalar2=None,
                  op0=ALU.bitwise_and),
                  reads=(b_suA,), writes=(b_su2,))
              P.op("vector", lambda E: E.tensor_scalar(
                  out=t_su2[:, :], in0=t_su2[:, :], scalar1=5, scalar2=None,
                  op0=ALU.logical_shift_left),
                  writes=(b_su2,))
              P.op("vector", lambda E: E.tensor_tensor(
                  out=t_suA[:, :], in0=t_suA[:, :], in1=t_su2[:, :],
                  op=ALU.bitwise_or),
                  reads=(b_su2,), add_writes=(b_suA,))
              P.op("vector", lambda E: E.tensor_copy(out=t_su2[:, :],
                                                     in_=t_sA[:, 0:16]),
                   reads=(b_sA, b_suA), writes=(b_su2,))
              P.op("vector", lambda E: E.tensor_scalar(
                  out=t_su2[:, :], in0=t_su2[:, :], scalar1=30, scalar2=None,
                  op0=ALU.logical_shift_left),
                  writes=(b_su2,))
              P.op("vector", lambda E: E.tensor_tensor(
                  out=t_suA[:, :], in0=t_suA[:, :], in1=t_su2[:, :],
                  op=ALU.bitwise_or),
                  reads=(b_su2,), add_writes=(b_suA,))
            def emit_scatters():
                for b in range(NB):
                    for kk in range(2):
                        P.op("gpsimd", lambda E, b=b, kk=kk: E.indirect_dma_start(
                            out=d_idx[:, :],
                            out_offset=IndirectOffsetOnAxis(
                                t_suA[:, 2 * b + kk:2 * b + kk + 1], axis=0),
                            in_=t_tokid[:, b:b + 1], in_offset=None,
                            bounds_check=regs["b_idxsp"], oob_is_err=False),
                            reads=(b_suA, b_consts, b_idxd), add_writes=(b_idxd,),
                            sem="IND")
            return routing_tail1, routing_tail2, emit_scatters

        def silu_mul(dst_ap, dst_buf, first_dst, n=512):
            P.op("scalar", lambda E, n=n: E.activation(out=t_sil[:, 0:n], in_=p_ga[:, 0:n],
                                                       func=AF.Silu),
                 reads=(b_ga,), writes=(b_sil,))
            P.op("vector", lambda E, dst_ap=dst_ap, n=n: E.tensor_tensor(
                out=dst_ap, in0=t_sil[:, 0:n], in1=p_gb[:, 0:n], op=ALU.mult),
                reads=(b_sil, b_gb),
                writes=(dst_buf,) if first_dst else (),
                add_writes=(dst_buf,) if not first_dst else ())

        def dbg_store(dst, src_ap, rbuf, slow=False):
            P.op("sync", lambda E, d=dst, sp=src_ap: E.dma_start(
                out=d, in_=sp, allow_slow_non_contiguous=True) if slow else
                E.dma_start(out=d, in_=sp),
                reads=(rbuf,), sem="OUT")

        def emit_ffn(call, pre_block=None):
            b_xnd = st_bufs["xnd"] = fresh(st_bufs["xnd"])
            b_idxd = st_bufs["idxd"] = fresh(st_bufs["idxd"])
            b_obd = st_bufs["obd"] = fresh(st_bufs["obd"])
            if call == 0:
                QW = GSZ * D // 4
                pend = ([("shdn", g, q) for g in range(NGRP) for q in range(4)]
                        + [("exdn", e, q) for e in range(NEXP) for q in range(4)])

                def pop_pieces(k):
                    for _ in range(k):
                        if not pend:
                            return
                        p = pend.pop(0)
                        sl = castn["i"] % 4
                        castn["i"] += 1
                        sem = "CS%d" % sl
                        if p[0] == "shdn":
                            g, q = p[1], p[2]
                            P.op("gpsimd", lambda E, g=g, q=q: E.dma_start(
                                out=d_cshdn[g][:, QW * q:QW * q + QW],
                                in_=d_wsh_dn[g][:, QW * q:QW * q + QW]),
                                writes=(cs_slot[sl],),
                                add_writes=(cb_shdn[g],), sem=sem)
                        else:
                            e, q = p[1], p[2]
                            P.op("gpsimd", lambda E, e=e, q=q: E.dma_start(
                                out=d_cexdn[e][:, QW * q:QW * q + QW],
                                in_=d_wex_dn[e][:, QW * q:QW * q + QW]),
                                writes=(cs_slot[sl],),
                                add_writes=(cb_exdn[e],), sem=sem)

                for j in range(2):
                    P.op("gpsimd", lambda E, j=j: E.dma_start(
                        out=t_up[:, 4096 * (j % 2):4096 * (j % 2) + 4096],
                        in_=d_wsh_up[j]),
                        writes=(b_upA[j % 2], b_upB[j % 2]),
                        sem="W%d" % (j % 2))
                pop_pieces(24)
            else:
                pend = []

                def pop_pieces(k):
                    pass

                for j in range(2):
                    P.op("sync" if j % 2 == 0 else "scalar",
                         lambda E, j=j: E.dma_start(
                             out=t_up[:, 4096 * (j % 2):4096 * (j % 2) + 4096],
                             in_=d_cshup[j]),
                         reads=(cb_shup[j],),
                         writes=(b_upA[j % 2], b_upB[j % 2]),
                         sem="W%d" % (j % 2), inc=16)
            P.op("sync", lambda E: E.dma_start(
                out=d_idx.rearrange("(p s) o -> p (s o)", p=128), in_=t_sent[:, :]),
                reads=(b_consts,), writes=(b_idxd,), sem="IDX")
            def ffn_tail(b):
                ln_stage2(b)
                P.op("sync", lambda E, b=b: E.dma_start(
                    out=d_xnd[128 * b:128 * b + 128, :], in_=t_xnt[:, :]),
                    reads=(b_xnt,), writes=(b_xnd,) if b == 0 else (),
                    add_writes=(b_xnd,) if b else (), sem="XN")
                if debug and call == 0:
                    dbg_store(dbg["xn1"][128 * b:128 * b + 128, :], t_xnt[:, :], b_xnt)
                xn_transpose(b, first=(b == 0))
                routing_logits(b)
            prev = None
            for b in range(NB):
                if pre_block is not None:
                    pre_block(b)
                ln_stage1(b)
                if prev is not None:
                    ffn_tail(prev)
                prev = b
            ffn_tail(prev)
            routing_tail1, routing_tail2, emit_scatters = routing_batched(b_idxd)
            def emit_scat_idx():
                emit_scatters()
                P.op("sync", lambda E: E.dma_start(
                    out=t_idxsb[:, :],
                    in_=d_idx.rearrange("(p s) o -> p (s o)", p=128)),
                    reads=(b_idxd,), writes=(b_idxsb,), sem="IDX")
                if debug and call == 0:
                    dbg_store(dbg["ix1"][:, :], t_idxsb[:, :], b_idxsb)

            disp = {"steps": [], "b_buf": None}

            def run_step():
                if not disp["steps"]:
                    return
                st = disp["steps"].pop(0)
                b_buf = disp["b_buf"]
                if st[0] == "g":
                    _, e_t, sb = st
                    col = (e_t * CL + 128 * sb) // 128
                    gp = sb % 2
                    P.op("gpsimd", lambda E, col=col, gp=gp: E.indirect_dma_start(
                        out=(t_gbs[:, :] if gp == 0 else t_xnt[:, :]), out_offset=None,
                        in_=d_xnd[:, :],
                        in_offset=IndirectOffsetOnAxis(t_idxsb[:, col:col + 1], axis=0),
                        bounds_check=regs["b_tok"], oob_is_err=False),
                        reads=(b_idxsb, b_xnd), writes=(b_gbs[gp],), sem="IND" if gp == 0 else "G0")
                else:
                    _, e_t, sb, half = st
                    gp = sb % 2
                    src_t = t_gbs if gp == 0 else t_xnt
                    ins = [(lambda k=k, src_t=src_t: src_t[:, 128 * k:128 * k + 128])
                           for k in range(8 * half, 8 * half + 8)]
                    outs = [(lambda q=q: p_tr[:, 128 * q:128 * q + 128])
                            for q in range(8)]
                    P.tr_group(ins, outs, (b_gbs[gp], b_consts), b_tr, t_idn[:, :])

                    def ev(E, e_t=e_t, sb=sb, half=half):
                        dst = t_xnf[:, 0:2 * KC * CL].rearrange(
                            "p (pp k t) -> p pp k t", pp=2, k=KC)[
                            :, e_t % 2, 8 * half:8 * half + 8, 128 * sb:128 * sb + 128]
                        src = p_tr[:, :].rearrange("p (k t) -> p k t", k=8)
                        return E.tensor_copy(out=dst, in_=src)
                    fw = (e_t == 0 and sb == 0 and half == 0)
                    P.op("vector", ev, reads=(b_tr,),
                         writes=(b_buf,) if fw else (),
                         add_writes=(b_buf,) if not fw else ())

            def queue_dispatch(e_t):
                for sb in range(NSB):
                    disp["steps"].append(("g", e_t, sb))
                    disp["steps"].append(("t", e_t, sb, 0))
                    disp["steps"].append(("t", e_t, sb, 1))

            for g in range(NGRP):
                for jj in range(GSZ):
                    j = g * GSZ + jj
                    par = j % 2
                    if not (g == 0 and jj < 2):
                        if call == 0:
                            P.op("gpsimd", lambda E, j=j, par=par: E.dma_start(
                                out=t_up[:, 4096 * par:4096 * par + 4096],
                                in_=d_wsh_up[j]),
                                writes=(b_upA[par], b_upB[par]),
                                sem="W%d" % par)
                        else:
                            P.op("sync" if par == 0 else "scalar",
                                 lambda E, j=j, par=par: E.dma_start(
                                     out=t_up[:, 4096 * par:4096 * par + 4096],
                                     in_=d_cshup[j]),
                                 reads=(cb_shup[j],),
                                 writes=(b_upA[par], b_upB[par]),
                                 sem="W%d" % par, inc=16)
                    if g == 0 and jj == 1:
                        routing_tail1()
                        routing_tail2()
                    if g == 0 and jj == 10:
                        emit_scat_idx()
                    if jj == 1:
                        HW = GSZ * D // 2
                        P.op("scalar", lambda E, g=g, HW=HW: E.dma_start(
                            out=t_wd[:, :HW], in_=d_cshdn[g][:, :HW]),
                            reads=(cb_shdn[g],), writes=(b_wd,), sem="WDV", inc=16)
                        P.op("sync", lambda E, g=g, HW=HW: E.dma_start(
                            out=t_wd[:, HW:GSZ * D], in_=d_cshdn[g][:, HW:]),
                            reads=(cb_shdn[g],), writes=(b_wd2,), sem="WD", inc=16)
                    for tt in range(2):
                        mms = [(lambda: p_ga[:, :],
                                lambda k=k, par=par: t_up[:, 4096 * par + 128 * k:
                                                          4096 * par + 128 * k + 128],
                                lambda k=k, tt=tt: t_xnf[:, TOK * k + 512 * tt:
                                                         TOK * k + 512 * tt + 512],
                                k == 0, k == KC - 1) for k in range(KC)]
                        P.mm_group(mms, reads=(b_upA[par], b_xnf), psum=b_ga)
                        mms = [(lambda: p_gb[:, :],
                                lambda k=k, par=par: t_up[:, 4096 * par + 2048 + 128 * k:
                                                          4096 * par + 2048 + 128 * k + 128],
                                lambda k=k, tt=tt: t_xnf[:, TOK * k + 512 * tt:
                                                         TOK * k + 512 * tt + 512],
                                k == 0, k == KC - 1) for k in range(KC)]
                        P.mm_group(mms, reads=(b_upB[par], b_xnf), psum=b_gb)
                        silu_mul(t_h[:, TOK * jj + 512 * tt: TOK * jj + 512 * tt + 512],
                                 b_h[jj], first_dst=(tt == 0))
                        if call == 0 and tt == 0:
                            P.op("scalar", lambda E, j=j, par=par: E.dma_start(
                                out=d_cshup[j],
                                in_=t_up[:, 4096 * par:4096 * par + 4096]),
                                reads=(b_upA[par], b_upB[par]),
                                writes=(wb_slot[par],),
                                add_writes=(cb_shup[j],),
                                sem="WB%d" % par, inc=16)
                if g == NGRP - 1:
                    disp["b_buf"] = fresh(b_xnf)
                    queue_dispatch(0)
                for b in range(NB):
                    if b % 2 == 0:
                        pop_pieces(1)
                    for q in range(4):
                        mms = [(lambda q=q: p_y[q][:, :],
                                lambda jj=jj, b=b: t_h[:, TOK * jj + 128 * b:
                                                       TOK * jj + 128 * b + 128],
                                lambda jj=jj, q=q: t_wd[:, D * jj + 512 * q:
                                                        D * jj + 512 * q + 512],
                                jj == 0, jj == GSZ - 1) for jj in range(GSZ)]
                        P.mm_group(mms,
                                   reads=[b_h[jj] for jj in range(GSZ)] + [b_wd, b_wd2],
                                   psum=b_py[q])
                    for q in range(4):
                        P.op("vector", lambda E, b=b, q=q: E.scalar_tensor_tensor(
                            out=xblk(b)[:, 512 * q:512 * q + 512],
                            in0=p_y[q][:, :], scalar=0.5,
                            in1=xblk(b)[:, 512 * q:512 * q + 512],
                            op0=ALU.mult, op1=ALU.add),
                            reads=(b_py[q], b_x[b]), add_writes=(b_x[b],))
                    if g == NGRP - 1:
                        run_step()
                        run_step()

            if debug and call == 0:
                dbg_store(dbg["wv1"][:, :], t_wv[:, :], b_wv)
                dbg_store(dbg["sl1"][:, :], t_slots[:, :], b_slots)
                for b in range(NB):
                    dbg_store(dbg["x1"][128 * b:128 * b + 128, :], xblk(b), b_x[b])

            b_buf = disp["b_buf"]
            for e in range(NEXP):
                par = e % 2
                if e + 1 < NEXP:
                    queue_dispatch(e + 1)
                for j in range(JEX):
                    wpar = j % 2
                    P.op("gpsimd", lambda E, e=e, j=j, wpar=wpar: E.dma_start(
                        out=t_up[:, 4096 * wpar:4096 * wpar + 2048],
                        in_=d_wex_up[e, j][:, 0:2048]),
                        writes=(b_upA[wpar],), sem="W%d" % wpar)
                    P.op("gpsimd", lambda E, e=e, j=j, wpar=wpar: E.dma_start(
                        out=t_up[:, 4096 * wpar + 2048:4096 * wpar + 4096],
                        in_=d_wex_up[e, j][:, 2048:4096]),
                        writes=(b_upB[wpar],), sem="U%d" % wpar)
                    if call == 0 and j == JEX - 1 and e < 4:
                        pop_pieces(2)
                    if j == 1:
                        HW = JEX * D // 2
                        P.op("scalar", lambda E, e=e, HW=HW: E.dma_start(
                            out=t_wd[:, :HW], in_=d_cexdn[e][:, :HW]),
                            reads=(cb_exdn[e],), writes=(b_wd,), sem="WDV", inc=16)
                        P.op("sync", lambda E, e=e, HW=HW: E.dma_start(
                            out=t_wd[:, HW:], in_=d_cexdn[e][:, HW:]),
                            reads=(cb_exdn[e],), writes=(b_wd2,), sem="WD", inc=16)
                    mms = [(lambda: p_ga[:, 0:CL],
                            lambda k=k, wpar=wpar: t_up[:, 4096 * wpar + 128 * k:
                                                        4096 * wpar + 128 * k + 128],
                            lambda k=k, par=par: t_xnf[:, KC * CL * par + CL * k:
                                                       KC * CL * par + CL * k + CL],
                            k == 0, k == KC - 1) for k in range(KC)]
                    P.mm_group(mms, reads=(b_upA[wpar], b_buf), psum=b_ga)
                    mms = [(lambda: p_gb[:, 0:CL],
                            lambda k=k, wpar=wpar: t_up[:, 4096 * wpar + 2048 + 128 * k:
                                                        4096 * wpar + 2048 + 128 * k + 128],
                            lambda k=k, par=par: t_xnf[:, KC * CL * par + CL * k:
                                                       KC * CL * par + CL * k + CL],
                            k == 0, k == KC - 1) for k in range(KC)]
                    P.mm_group(mms, reads=(b_upB[wpar], b_buf), psum=b_gb)
                    silu_mul(t_h[:, TOK * j: TOK * j + CL], b_h[j], first_dst=True, n=CL)
                    run_step()
                for sb in range(NSB):
                    for q in range(4):
                        mms = [(lambda q=q: p_y[q][:, :],
                                lambda j=j, sb=sb: t_h[:, TOK * j + 128 * sb:
                                                       TOK * j + 128 * sb + 128],
                                lambda j=j, q=q: t_wd[:, D * j + 512 * q:
                                                      D * j + 512 * q + 512],
                                j == 0, j == JEX - 1) for j in range(JEX)]
                        P.mm_group(mms,
                                   reads=[b_h[j] for j in range(JEX)] + [b_wd, b_wd2],
                                   psum=b_py[q])
                    opar = sb % 2
                    for q in range(4):
                        P.op("vector", lambda E, opar=opar, q=q: E.tensor_copy(
                            out=t_osb[:, D * opar + 512 * q: D * opar + 512 * q + 512],
                            in_=p_y[q][:, :]),
                            reads=(b_py[q],),
                            writes=(b_osb[opar],) if q == 0 else (),
                            add_writes=(b_osb[opar],) if q else ())
                    first_store = (e == 0 and sb == 0)
                    P.op("sync",
                         lambda E, e=e, sb=sb, opar=opar: E.dma_start(
                        out=d_ob[e * CL + 128 * sb: e * CL + 128 * sb + 128, :],
                        in_=t_osb[:, D * opar: D * opar + D]),
                        reads=(b_osb[opar],),
                        writes=(b_obd,) if first_store else (),
                        add_writes=(b_obd,) if not first_store else (),
                        sem="OB%d" % opar, inc=16)
            b_xnf.w = list(b_buf.w)
            b_xnf.r = list(b_buf.r)
            if debug and call == 0:
                P.op("sync", lambda E: E.dma_start(out=dbg["ob1"][:, :], in_=d_ob[:, :]),
                     reads=(b_obd,), sem="OUT")

            def combine_block(b):
                for kk in range(2):
                    P.op("gpsimd", lambda E, b=b, kk=kk: E.indirect_dma_start(
                        out=t_osb[:, D * kk: D * kk + D], out_offset=None,
                        in_=d_ob[:, :],
                        in_offset=IndirectOffsetOnAxis(
                            t_slots[:, 2 * b + kk: 2 * b + kk + 1], axis=0),
                        bounds_check=regs["b_slot"], oob_is_err=False),
                        reads=(b_slots, b_obd), writes=(b_osb[kk],), sem="G%d" % kk)
                for kk in range(2):
                    P.op("vector", lambda E, b=b, kk=kk: E.scalar_tensor_tensor(
                        out=xblk(b), in0=t_osb[:, D * kk: D * kk + D],
                        scalar=t_wv[:, 2 * b + kk: 2 * b + kk + 1],
                        in1=xblk(b), op0=ALU.mult, op1=ALU.add),
                        reads=(b_osb[kk], b_wv, b_x[b]), add_writes=(b_x[b],))
            return combine_block

        def emit_mixer(pre_block):
            def mx_tail(b):
                ln_stage2(b, keep_mix=True)
                xn_transpose(b, first=(b == 0))
            prev = None
            for b in range(NB):
                pre_block(b)
                ln_stage1(b)
                if prev is not None:
                    mx_tail(prev)
                prev = b
            mx_tail(prev)
            P.op("sync", lambda E: E.dma_start(out=t_osb[0:1, 0:D], in_=d_dvec[:, :]),
                 writes=(b_osb[0],), sem="DLD")
            for q in range(4):
                P.mm_group([(lambda q=q: p_y[q][:, :], lambda: t_ones1[:, :],
                             lambda q=q: t_osb[0:1, 512 * q:512 * q + 512], True, True)],
                           reads=(b_consts, b_osb[0]), psum=b_py[q])
            for q in range(4):
                P.op("vector", lambda E, q=q: E.tensor_copy(
                    out=t_osb[:, D + 512 * q: D + 512 * q + 512], in_=p_y[q][:, :]),
                    reads=(b_py[q],),
                    writes=(b_osb[1],) if q == 0 else (),
                    add_writes=(b_osb[1],) if q else ())
            def bt_load(p):
                P.op("gpsimd", lambda E, p=p: E.dma_start(
                    out=t_up[:, 4096 * (p % 2):4096 * (p % 2) + 4096].rearrange(
                        "p (n d) -> p n d", n=2),
                    in_=d_bt[2 * p:2 * p + 2].rearrange("n p d -> p n d")),
                    writes=(b_upA[p % 2], b_upB[p % 2]), sem="W%d" % (p % 2))
            bt_load(0)
            bt_load(1)
            for n in range(8):
                wpar = (n // 2) % 2
                for tt in range(2):
                    rix = (2 * n + tt) % 4
                    pp, bpp = p_y[rix], b_py[rix]
                    mms = [(lambda pp=pp: pp[:, :],
                            lambda k=k, n=n, wpar=wpar: t_up[
                                :, 4096 * wpar + 2048 * (n % 2) + 128 * k:
                                4096 * wpar + 2048 * (n % 2) + 128 * k + 128],
                            lambda k=k, tt=tt: t_xnf[:, TOK * k + 512 * tt:
                                                     TOK * k + 512 * tt + 512],
                            k == 0, k == KC - 1) for k in range(KC)]
                    P.mm_group(mms, reads=((b_upA[wpar] if n % 2 == 0 else
                                            b_upB[wpar]), b_xnf), psum=bpp)
                    P.op("scalar", lambda E, n=n, tt=tt, pp=pp: E.activation(
                        out=t_h[:, TOK * n + 512 * tt: TOK * n + 512 * tt + 512],
                        in_=pp[:, :], func=AF.Tanh),
                        reads=(bpp,),
                        writes=(b_h[n],) if tt == 0 else (),
                        add_writes=(b_h[n],) if tt else ())
                if n % 2 == 1 and n // 2 + 2 < 4:
                    bt_load(n // 2 + 2)
            P.op("gpsimd", lambda E: E.dma_start(out=t_wd[:, :8 * D], in_=d_ct[:, :]),
                 writes=(b_wd, b_wd2), sem="WD")
            for b in range(NB):
                for q in range(4):
                    mms = [(lambda q=q: p_y[q][:, :],
                            lambda n=n, b=b: t_h[:, TOK * n + 128 * b:
                                                 TOK * n + 128 * b + 128],
                            lambda n=n, q=q: t_wd[:, D * n + 512 * q:
                                                  D * n + 512 * q + 512],
                            n == 0, n == 7) for n in range(8)]
                    P.mm_group(mms, reads=[b_h[n] for n in range(8)] + [b_wd, b_wd2],
                               psum=b_py[q])
                P.op("vector", lambda E, b=b: E.tensor_tensor(
                    out=t_osb[:, 0:D], in0=xblk(b), in1=t_osb[:, D:2 * D],
                    op=ALU.mult),
                    reads=(b_x[b], b_osb[1]), writes=(b_osb[0],))
                P.op("vector", lambda E, b=b: E.tensor_scalar(
                    out=t_osb[:, 0:D], in0=t_osb[:, 0:D],
                    scalar1=t_lnm[:, 2 * b:2 * b + 1], scalar2=None, op0=ALU.mult),
                    reads=(b_lnm, b_osb[0]), add_writes=(b_osb[0],))
                P.op("vector", lambda E, b=b: E.scalar_tensor_tensor(
                    out=t_osb[:, 0:D], in0=t_osb[:, D:2 * D],
                    scalar=t_lnm[:, 2 * b + 1:2 * b + 2], in1=t_osb[:, 0:D],
                    op0=ALU.mult, op1=ALU.add),
                    reads=(b_lnm, b_osb[1], b_osb[0]), add_writes=(b_osb[0],))
                P.op("vector", lambda E, b=b: E.tensor_tensor(
                    out=xblk(b), in0=xblk(b), in1=t_osb[:, 0:D], op=ALU.add),
                    reads=(b_osb[0], b_x[b]), add_writes=(b_x[b],))
                for q in range(4):
                    P.op("vector", lambda E, b=b, q=q: E.scalar_tensor_tensor(
                        out=xblk(b)[:, 512 * q:512 * q + 512],
                        in0=p_y[q][:, :], scalar=1.0,
                        in1=xblk(b)[:, 512 * q:512 * q + 512],
                        op0=ALU.mult, op1=ALU.add),
                        reads=(b_py[q], b_x[b]), add_writes=(b_x[b],))

        comb1 = emit_ffn(0)
        if debug:
            for b in range(NB):
                comb1(b)
            comb1 = lambda b: None
            for b in range(NB):
                dbg_store(dbg["x2"][128 * b:128 * b + 128, :], xblk(b), b_x[b])
        emit_mixer(comb1)
        if debug:
            for b in range(NB):
                dbg_store(dbg["x3"][128 * b:128 * b + 128, :], xblk(b), b_x[b])
        comb2 = emit_ffn(1)

        for b in range(NB):
            comb2(b)
            P.op("sync" if b % 2 == 0 else "scalar",
                 lambda E, b=b: E.dma_start(
                     out=d_y[128 * b:128 * b + 128, :], in_=xblk(b)),
                 reads=(b_x[b],), sem="OUT" if b % 2 == 0 else "OUT2", inc=16)
        fin = P.cnt["OUT"]
        fin2 = P.cnt["OUT2"]
        sems = P.sems
        P.q["sync"].append(lambda E, v=fin: E.wait_ge(sems["OUT"], v))
        P.q["sync"].append(lambda E, v=fin2: E.wait_ge(sems["OUT2"], v))

        with nc.Block() as block:
            @block.sync
            def _(E):
                for f in P.q["sync"]:
                    f(E)

            @block.gpsimd
            def _(E):
                for f in P.q["gpsimd"]:
                    f(E)

            @block.tensor
            def _(E):
                for f in P.q["tensor"]:
                    f(E)

            @block.vector
            def _(E):
                for f in P.q["vector"]:
                    f(E)

            @block.scalar
            def _(E):
                for f in P.q["scalar"]:
                    f(E)

    return nc


def _prep_weights(L0_ffn_shared_w_up_gate, L0_ffn_shared_w_down,
                  L0_ffn_w_up_gate, L0_ffn_w_down,
                  L0_attn_B, L0_attn_C, L0_ffn_router, L0_attn_D):
    f = np.float32

    def colblk(W, j):
        return np.ascontiguousarray(
            W[:, 128 * j:128 * j + 128].reshape(KC, 128, 128)
            .transpose(1, 0, 2).reshape(128, 2048))

    Wsh = np.asarray(L0_ffn_shared_w_up_gate, f)
    J = 5632
    gate, up = Wsh[:, :J], Wsh[:, J:]
    wsh_up = np.stack([np.concatenate([colblk(gate, j), colblk(up, j)], axis=1)
                       for j in range(JSH)])

    Wd = np.asarray(L0_ffn_shared_w_down, f)
    wsh_dn = np.ascontiguousarray(
        Wd.reshape(NGRP, GSZ, 128, D).transpose(0, 2, 1, 3).reshape(NGRP, 128, GSZ * D))

    We = np.asarray(L0_ffn_w_up_gate, f)
    I = 1408
    wex_up = np.empty((NEXP, JEX, 128, 4096), f)
    for e in range(NEXP):
        ge, ue = We[e][:, :I], We[e][:, I:]
        for j in range(JEX):
            wex_up[e, j, :, :2048] = colblk(ge, j)
            wex_up[e, j, :, 2048:] = colblk(ue, j)

    Wed = np.asarray(L0_ffn_w_down, f)
    wex_dn = np.ascontiguousarray(
        Wed.reshape(NEXP, JEX, 128, D).transpose(0, 2, 1, 3).reshape(NEXP, 128, JEX * D))

    Bt = np.ascontiguousarray(np.asarray(L0_attn_B, f).T)
    bt_w = np.stack([colblk(Bt, n) for n in range(8)])

    Ct = np.ascontiguousarray(np.asarray(L0_attn_C, f).T)
    ct_w = np.ascontiguousarray(
        Ct.reshape(8, 128, D).transpose(1, 0, 2).reshape(128, 8 * D))

    rt_w = np.ascontiguousarray(
        np.asarray(L0_ffn_router, f).reshape(KC, 128, 8).transpose(1, 0, 2).reshape(128, 128))

    d_vec = np.ascontiguousarray(np.asarray(L0_attn_D, f).reshape(1, D))
    return dict(wsh_up=wsh_up, wsh_dn=wsh_dn, wex_up=wex_up, wex_dn=wex_dn,
                bt_w=bt_w, ct_w=ct_w, rt_w=rt_w, d_vec=d_vec)


def _consts():
    bf = ml_dtypes.bfloat16
    tri = np.triu(np.ones((128, 128), np.float32), 1).astype(bf)
    idn = np.eye(128, dtype=np.float32).astype(bf)
    iota8 = np.tile(np.arange(8, dtype=np.float32), (128, 1))
    tokid = (np.arange(128, dtype=np.uint32)[:, None]
             + 128 * np.arange(8, dtype=np.uint32)[None, :]).astype(np.uint32)
    ones128 = np.ones((128, 1), np.float32).astype(bf)
    ones1 = np.ones((1, 128), np.float32)
    sent = np.full((128, 32), 0x3FFFFFFF, np.uint32)
    eps = np.full((128, 1), 1e-5, np.float32)
    neg1 = np.full((128, 1), -1.0, np.float32)
    return dict(tri_c=np.ascontiguousarray(tri), idn_c=np.ascontiguousarray(idn),
                iota8_c=iota8, tokid_c=tokid, ones128_c=ones128, ones1_c=ones1,
                sent_c=sent, eps_c=eps, neg1_c=neg1)


def kernel(L0_attn_x, L0_attn_B, L0_attn_C, L0_attn_D, L0_ffn_router,
           L0_ffn_w_up_gate, L0_ffn_w_down, L0_ffn_shared_w_up_gate,
           L0_ffn_shared_w_down):
    x = np.ascontiguousarray(np.asarray(L0_attn_x, np.float32).reshape(8192, D))
    w = _prep_weights(L0_ffn_shared_w_up_gate, L0_ffn_shared_w_down,
                      L0_ffn_w_up_gate, L0_ffn_w_down,
                      L0_attn_B, L0_attn_C, L0_ffn_router, L0_attn_D)
    c = _consts()
    nc = build_nc()
    in_maps = []
    for core in range(8):
        m = {"x_in": np.ascontiguousarray(x[TOK * core:TOK * (core + 1)])}
        m.update(w)
        m.update(c)
        in_maps.append(m)
    res = run_bass_kernel_spmd(nc, in_maps, core_ids=list(range(8)))
    out = np.concatenate([np.asarray(r["y_out"]) for r in res.results], axis=0)
    return out.reshape(2, 4096, D).astype(np.float32)
